# version 07
# baseline (speedup 1.0000x reference)
import numpy as np
import ml_dtypes
import concourse.bass as bass
import concourse.mybir as mybir
from concourse.bass_utils import run_bass_kernel_spmd

F32 = mybir.dt.float32
BF16 = mybir.dt.bfloat16
U8 = mybir.dt.uint8
AF = mybir.ActivationFunctionType
ALU = mybir.AluOpType
AX = mybir.AxisListType
BF = ml_dtypes.bfloat16

D = 2048
T = 2048
NH = 16
HD = 128
NKC = D // 128
NTT = T // 512
NTC = T // 128
EPS = 1e-5
SCALE0 = float(1.0 / np.sqrt(128.0))
SCALE1 = float(1.0 / np.sqrt(128.0))
FCONSTS = [1e-5, 1e-30, 0.5, -0.5, SCALE0, 2.0, 1e-8, -1.0]

ENGINES = ("tensor", "vector", "scalar", "gpsimd", "sync")
ESEM = {"tensor": "pe", "vector": "dv", "scalar": "sc", "gpsimd": "ql",
        "sync": "qs"}
NDSEM = 6
DMA_SEMS = {"gpsimd": ["qld%d" % i for i in range(NDSEM)],
            "sync": ["qsd%d" % i for i in range(NDSEM)]}
ALL_SEMS = ["pe", "dv", "sc", "ql", "qs"] + DMA_SEMS["gpsimd"] +     DMA_SEMS["sync"]


def _flat(deps):
    out = []
    for d in deps:
        if d is None:
            continue
        if isinstance(d, list) or isinstance(d, tuple):
            if len(d) == 2 and isinstance(d[0], str):
                out.append(d)
            else:
                out.extend(_flat(d))
        else:
            raise TypeError(f"bad dep {d!r}")
    return out


class Sched:

    def __init__(self, nc):
        self.nc = nc
        self.ops = {e: [] for e in ENGINES}
        self.cnt = {s: 0 for s in ALL_SEMS}
        self._rot = {"gpsimd": 0, "sync": 0}
        self._last_dma = {}

    def add(self, eng, fn, deps=(), inc=1):
        deps = tuple(_flat(deps))
        if inc == 16 and eng in DMA_SEMS:
            i = self._rot[eng] % NDSEM
            self._rot[eng] += 1
            sem = DMA_SEMS[eng][i]
            prev = self._last_dma.get(sem)
            if prev is not None:
                deps = deps + (prev,)
        else:
            sem = ESEM[eng]
        if inc:
            self.cnt[sem] += inc
        tok = (sem, self.cnt[sem]) if inc else None
        if inc == 16 and eng in DMA_SEMS:
            self._last_dma[sem] = tok
        self.ops[eng].append((fn, deps, sem, inc))
        return tok

    def barrier_token(self, eng):
        if eng in ("gpsimd", "sync"):
            toks = [(ESEM[eng], self.cnt[ESEM[eng]])]
            toks += [(s, self.cnt[s]) for s in DMA_SEMS[eng]]
            return [t for t in toks if t[1] > 0]
        return (ESEM[eng], self.cnt[ESEM[eng]])

    def all_tokens(self):
        out = []
        for e in ENGINES:
            t = self.barrier_token(e)
            out.extend(t if isinstance(t, list) else [t])
        return out

    def emit(self, block, sems):
        for e in ENGINES:
            ops = self.ops[e]
            if not ops:
                continue

            def make(section_ops):
                def body(eng):
                    seen = {}
                    for fn, deps, sem, inc in section_ops:
                        for dsem, dcnt in deps:
                            if seen.get(dsem, 0) < dcnt:
                                eng.wait_ge(sems[dsem], dcnt)
                                seen[dsem] = dcnt
                        ins = fn(eng)
                        if inc:
                            assert ins is not None, f"op on {sem} returned None"
                            ins.then_inc(sems[sem], inc)
                return body

            getattr(block, e)(make(ops))


class P:
    def __init__(self, name="mk"):
        self.nc = bass.Bass("TRN2", target_bir_lowering=False)
        self.s = Sched(self.nc)
        self._ctx = []
        self.sems = {}
        self.psS = self._psum("psS", [128, 2048], F32)
        self.psG = self._psum("psG", [128, 1024], F32)
        self.psY = self._psum("psY", [128, 512], F32)
        self.psX = self._psum("psX", [128, 512], F32)
        self.ps6 = [self.psS[:, i * 512:(i + 1) * 512] for i in range(4)] + \
                   [self.psG[:, 0:512], self.psG[:, 512:1024]]
        self.fc_sb = self.sb("fconsts", [128, len(FCONSTS)], F32)
        for i, v in enumerate(FCONSTS):
            self.nc.const_aps.register(F32, float(v), self.fc_sb.ap()[:, i:i + 1])

    def _enter(self, cm):
        h = cm.__enter__()
        if getattr(self, "_scopes", None):
            self._scopes[-1].append(cm)
        else:
            self._ctx.append(cm)
        return h

    def push_scope(self):
        self._scopes = getattr(self, "_scopes", [])
        self._scopes.append([])

    def pop_scope(self):
        for cm in reversed(self._scopes.pop()):
            cm.__exit__(None, None, None)

    def phase_barrier(self):
        toks = self.s.all_tokens()
        out = []
        for e in ENGINES:
            out.append(self.s.add(e, lambda eng: eng.nop(), deps=toks, inc=1))
        return out

    def sem(self, name):
        return self._enter(self.nc.semaphore(name))

    def sb(self, name, shape, dtype):
        self._sbser = getattr(self, "_sbser", 0) + 1
        return self._enter(self.nc.sbuf_tensor(f"{name}_{self._sbser}",
                                               shape, dtype))

    def _psum(self, name, shape, dtype=F32):
        return self._enter(self.nc.psum_tensor(name, shape, dtype))

    def dram_in(self, name, shape, dtype=F32):
        return self.nc.dram_tensor(name, shape, dtype, kind="ExternalInput").ap()

    def dram_out(self, name, shape, dtype=F32):
        return self.nc.dram_tensor(name, shape, dtype, kind="ExternalOutput").ap()

    def dram_tmp(self, name, shape, dtype=F32):
        return self.nc.dram_tensor(name, shape, dtype, kind="Internal").ap()

    def finalize(self):
        for s in ALL_SEMS:
            self.sems[s] = self.sem(s)
        toks = self.s.all_tokens()
        self.s.add("sync", lambda e: e.nop(), deps=toks, inc=1)
        with self.nc.Block() as block:
            self.s.emit(block, self.sems)
        for cm in reversed(self._ctx):
            cm.__exit__(None, None, None)
        return self.nc


def host_consts():
    c = {}
    inv_freq = 1.0 / 10000.0 ** (np.arange(0, 128, 2, dtype=np.float64) / 128.0)
    ang = np.arange(T, dtype=np.float64)[:, None] * inv_freq[None, :]
    c["ropeC"] = np.cos(ang).astype(np.float32).T.copy()
    sinc = np.sin(ang).astype(np.float32).T.copy()
    sinc[0:32] = -sinc[0:32]
    c["ropeS"] = np.ascontiguousarray(sinc)
    c["identB"] = np.eye(128, dtype=np.float32).astype(BF)
    c["identF"] = np.eye(128, dtype=np.float32)
    oh = np.zeros((16, 16 * 128), dtype=np.float32)
    for j in range(16):
        oh[j, j * 128:(j + 1) * 128] = 1.0
    c["onehot"] = oh.astype(BF)
    mm = np.zeros((4, 128, 512), dtype=np.float32)
    i = np.arange(128)[:, None]
    j = np.arange(512)[None, :]
    for r in range(4):
        mm[r] = np.where(i + 128 * r <= j, 0.0, -1e30)
    c["cmaskK"] = mm
    c["onesbf"] = np.ones((128, 1), dtype=BF)
    c["onesrowf"] = np.ones((1, 128), dtype=np.float32)
    c["onesrow1"] = np.ones((1, 128), dtype=BF)
    fc = np.zeros((128, len(FCONSTS)), dtype=np.float32)
    for i, v in enumerate(FCONSTS):
        fc[:, i] = v
    c["fconsts"] = fc
    return c


def tile_wT(wT, owid, nk=NKC):
    wT = np.ascontiguousarray(np.asarray(wT, dtype=np.float32))
    kk, oo = wT.shape
    assert kk == nk * 128 and oo % owid == 0
    nob = oo // owid
    a = wT.reshape(nk, 128, nob, owid).transpose(2, 1, 0, 3)
    return np.ascontiguousarray(a.reshape(nob, 128, nk * owid))


def pack_cols(*vecs):
    cols = []
    for v in vecs:
        v = np.asarray(v, dtype=np.float32)
        assert v.ndim == 1 and v.size % 128 == 0
        cols.append(v.reshape(-1, 128).T)
    return np.ascontiguousarray(np.concatenate(cols, axis=1))


def chunk3(ap, p=128):
    return ap.rearrange("(c p) n -> c p n", p=p)


def emit_load_xbf(p, xres, xbf, cols=None, deps=()):
    t0, w = cols if cols else (0, T)
    xv = chunk3(xres)

    def fn(g):
        return g.dma_start(
            out=xbf[:, 0:NKC * w].rearrange("p (c t) -> p c t", c=NKC),
            in_=xv[:, :, t0:t0 + w].rearrange("c p t -> p c t"))
    return p.s.add("gpsimd", fn, deps=deps, inc=16)


def emit_wcol_load(p, wT, wbuf, ostart, owid, deps=(), nk=NKC):
    wv = chunk3(wT)

    def fn(g):
        return g.dma_start(
            out=wbuf[:, 0:nk * owid].rearrange("p (c m) -> p c m", c=nk),
            in_=wv[:, :, ostart:ostart + owid].rearrange("c p o -> p c o"))
    return p.s.add("gpsimd", fn, deps=deps, inc=16)


def emit_wtile_load(p, wt_block, wbuf, width, deps=()):
    def fn(g):
        return g.dma_start(out=wbuf[:, 0:width], in_=wt_block)
    return p.s.add("gpsimd", fn, deps=deps, inc=16)


def emit_mm_group(p, bank, pairs, deps=()):
    pairs = list(pairs)

    def fn(t):
        n = len(pairs)
        ins = None
        for i, (l, r) in enumerate(pairs):
            ins = t.matmul(bank, l, r, start=(i == 0), stop=(i == n - 1))
        return ins
    return p.s.add("tensor", fn, deps=deps, inc=1)


def emit_proj_phase(p, xres, qk_list, v_list, C, ropeC_sb=None, ropeS_sb=None,
                    bias_sb=None, qk_bias_cols=None, v_bias_row=None,
                    cbias_out=None, scale_for_c=1.0, deps=(), tab_tok=()):
    s = p.s
    xbf = p.sb("pj_xbf", [128, NKC * T], BF16)
    wbuf = [p.sb(f"pj_wb{i}", [128, NKC * 128], BF16) for i in range(3)]
    vwbuf = [p.sb(f"pj_vwb{i}", [128, NKC * 512], BF16) for i in range(2)]
    rt1 = [p.sb(f"pj_rt1_{i}", [64, 512], F32) for i in range(2)]
    rt2 = [p.sb(f"pj_rt2_{i}", [64, 512], F32) for i in range(2)]
    ost = [p.sb(f"pj_ost{i}", [128, 512], BF16) for i in range(6)]
    sqb = [p.sb(f"pj_sq{i}", [128, 512], BF16) for i in range(2)]
    nrm = p.sb("pj_nrm", [1, 2 * NH * NTT], F32)
    mrow = p.sb("pj_mrow", [1, 2 * NH], F32)
    crow = p.sb("pj_crow", [1, NH], F32)

    xtok = emit_load_xbf(p, xres, xbf, deps=deps)
    rope = ropeC_sb is not None

    bank_free = {}
    wb_free = {}
    ost_free = {}
    rt_free = {}
    sq_free = {}
    psX = p.psX
    nrm_free = None

    banks = p.ps6
    gidx = 0

    for pi, (tag, wT, outd) in enumerate(qk_list):
        for h in range(NH):
            wtok = emit_wtile_load(p, wT[h], wbuf[(pi * NH + h) % 3],
                                   NKC * 128,
                                   deps=(wb_free.get((pi * NH + h) % 3),))
            for tt in range(NTT):
                bk = gidx % 6
                bank = banks[bk]
                mmtok = emit_mm_group(
                    p, bank,
                    [(wbuf[(pi * NH + h) % 3][:, k * 128:(k + 1) * 128],
                      xbf[:, k * T + tt * 512: k * T + (tt + 1) * 512])
                     for k in range(NKC)],
                    deps=(xtok, wtok, bank_free.get(bk)))
                oi = gidx % 6
                ob = ost[oi]
                cs = slice(tt * 512, (tt + 1) * 512)
                if rope:
                    par = gidx % 2
                    t_m = s.add("vector", (lambda b_, pr, c_: lambda v:
                                v.tensor_tensor(rt1[pr][:, :], b_[0:64, :],
                                                ropeC_sb[:, c_], op=ALU.mult)
                                )(bank, par, cs),
                                deps=(mmtok, rt_free.get(par),
                                      ost_free.get(oi), tab_tok))
                    t_s1 = s.add("vector", (lambda b_, pr, c_: lambda v:
                                 v.tensor_tensor(rt2[pr][0:32, :], b_[32:64, :],
                                                 ropeS_sb[0:32, c_], op=ALU.mult)
                                 )(bank, par, cs), deps=(mmtok,))
                    t_s2 = s.add("vector", (lambda b_, pr, c_: lambda v:
                                 v.tensor_tensor(rt2[pr][32:64, :], b_[0:32, :],
                                                 ropeS_sb[32:64, c_], op=ALU.mult)
                                 )(bank, par, cs), deps=(mmtok,))
                    t_cp = s.add("scalar", (lambda b_, o_: lambda sc:
                                 sc.activation(out=o_[64:128, :],
                                               in_=b_[64:128, :],
                                               func=AF.Identity, bias=0.0,
                                               scale=1.0))(bank, ob),
                                 deps=(mmtok,))
                    t_add = s.add("gpsimd", (lambda pr, o_: lambda g:
                                  g.tensor_tensor(o_[0:64, :], rt1[pr][:, :],
                                                  rt2[pr][:, :], op=ALU.add)
                                  )(par, ob), deps=(t_m, t_s1, t_s2))
                    bank_free[bk] = t_cp
                    rt_free[par] = t_add
                    done = (t_add, t_cp)
                else:
                    bias = 0.0
                    if qk_bias_cols is not None:
                        col = qk_bias_cols[tag] + h
                        bias = bias_sb[:, col:col + 1]
                    t_ev = s.add("scalar", (lambda b_, o_, bi: lambda sc:
                                 sc.activation(out=o_[:], in_=b_[:],
                                               func=AF.Identity, bias=bi,
                                               scale=1.0))(bank, ob, bias),
                                 deps=(mmtok, ost_free.get(oi)))
                    bank_free[bk] = t_ev
                    done = (t_ev,)
                t_st = s.add("sync", (lambda o_, h_, c_, od: lambda sy:
                             sy.dma_start(out=od[h_ * 128:(h_ + 1) * 128, c_],
                                          in_=o_[:]))(ob, h, cs, outd),
                             deps=done, inc=16)
                spar = gidx % 2
                t_sq = s.add("gpsimd", (lambda o_, q_: lambda g:
                             g.tensor_tensor(q_[:], o_[:], o_[:], op=ALU.mult))
                             (ob, sqb[spar]), deps=(done, sq_free.get(spar)))
                cell = (pi * NH + h) * NTT + tt
                t_nm = s.add("tensor", (lambda q_: lambda t:
                             t.matmul(psX[0:1, :], C["onesbf"][:, 0:1], q_[:],
                                      start=True, stop=True))(sqb[spar]),
                             deps=(t_sq, nrm_free))
                t_nr = s.add("vector", (lambda ce: lambda v:
                             v.reduce_max(nrm[0:1, ce:ce + 1], psX[0:1, 0:512],
                                          axis=AX.X))(cell), deps=(t_nm,))
                nrm_free = t_nr
                sq_free[spar] = t_nm
                ost_free[oi] = (t_st, t_sq)
                gidx += 1
            wb_free[(pi * NH + h) % 3] = s.barrier_token("tensor")

    t_mr = []
    for cell in range(2 * NH):
        t_mr.append(s.add("vector", (lambda ce: lambda v:
                    v.reduce_max(mrow[0:1, ce:ce + 1],
                                 nrm[0:1, ce * NTT:(ce + 1) * NTT],
                                 axis=AX.X))(cell), deps=(nrm_free,)))
    t_p = s.add("vector", lambda v: v.tensor_tensor(
        crow[0:1, :], mrow[0:1, 0:NH], mrow[0:1, NH:2 * NH], op=ALU.mult),
        deps=t_mr)
    t_ln = s.add("scalar", lambda sc: sc.activation(
        out=crow[0:1, :], in_=crow[0:1, :], func=AF.Ln, bias=1e-30, scale=1.0),
        deps=(t_p,))
    t_ex = s.add("scalar", lambda sc: sc.activation(
        out=crow[0:1, :], in_=crow[0:1, :], func=AF.Exp, bias=0.0, scale=0.5),
        deps=(t_ln,))
    t_sc = s.add("vector", lambda v: v.tensor_scalar(
        out=crow[0:1, :], in0=crow[0:1, :],
        scalar1=float(-1.04 * scale_for_c), scalar2=-0.5, op0=ALU.mult,
        op1=ALU.add), deps=(t_ex,))
    t_cb = []
    for h in range(NH):
        t_b = s.add("tensor", (lambda h_: lambda t:
                    t.matmul(psX[:, 0:1], C["onesrowf"][0:1, :],
                             crow[0:1, h_:h_ + 1], start=True, stop=True))(h),
                    deps=(t_sc, t_cb[-1] if t_cb else None))
        t_e = s.add("scalar", (lambda h_: lambda sc:
                    sc.activation(out=cbias_out[:, h_:h_ + 1],
                                  in_=psX[:, 0:1], func=AF.Identity,
                                  bias=0.0, scale=1.0))(h), deps=(t_b,))
        t_cb.append(t_e)
    cb_tok = t_cb

    v_toks = []
    for vi, (wvT, Vd) in enumerate(v_list):
        vst = [p.sb(f"pj_vst{vi}_{i}", [128, 512], BF16) for i in range(4)]
        vst_free = {}
        for ot in range(NTT):
            vwtok = emit_wtile_load(p, wvT[ot], vwbuf[ot % 2], NKC * 512,
                                    deps=(wb_free.get(("v", vi, ot % 2)),))
            for tc in range(NTC):
                gg = ot * NTC + tc
                bk = gg % 6
                bank = banks[bk]
                pairs = [(xbf[:, k * T + tc * 128: k * T + tc * 128 + 128],
                          vwbuf[ot % 2][:, k * 512:(k + 1) * 512])
                         for k in range(NKC)]
                if v_bias_row is not None:
                    pairs.append((C["onesrow1"][0:1, :],
                                  v_bias_row[0:1, ot * 512:(ot + 1) * 512]))
                mmtok = emit_mm_group(p, bank, pairs,
                                      deps=(xtok, vwtok, bank_free.get(bk)))
                ob = vst[gg % 4]
                t_ev = s.add("scalar", (lambda b_, o_: lambda sc:
                             sc.activation(out=o_[:], in_=b_[:],
                                           func=AF.Identity, bias=0.0,
                                           scale=1.0))(bank, ob),
                             deps=(mmtok, vst_free.get(gg % 4)))
                bank_free[bk] = t_ev
                last = None
                for hh in range(4):
                    h = ot * 4 + hh
                    last = s.add("sync", (lambda o_, h_, tc_, hh_, vd: lambda sy:
                                 sy.dma_start(
                                     out=vd[h_, tc_ * 128:(tc_ + 1) * 128, :],
                                     in_=o_[:, hh_ * 128:(hh_ + 1) * 128]))
                                 (ob, h, tc, hh, Vd),
                                 deps=(t_ev,), inc=16)
                vst_free[gg % 4] = last
                v_toks.append(last)
            wb_free[("v", vi, ot % 2)] = s.barrier_token("tensor")
    return cb_tok


class PostLN:

    def __init__(self, p, name, xres_out, C, stats_banks, own_buffer=True):
        self.p = p
        self.name = name
        self.xres_out = xres_out
        self.C = C
        self.bk1, self.bk2 = stats_banks
        self.xafull = p.sb(f"{name}_xa", [128, NKC * 512], F32) \
            if own_buffer else None
        self.xabf = [p.sb(f"{name}_xb{i}", [128, 512], BF16) for i in range(2)]
        self.xsq = [p.sb(f"{name}_xs{i}", [128, 512], BF16) for i in range(2)]
        self.rows = p.sb(f"{name}_rows", [1, 512 * 4], F32)
        self.rbsb = p.sb(f"{name}_rb", [128, 512], F32)
        self.mbsb = p.sb(f"{name}_mb", [128, 512], F32)
        self._free = {}
        self._reset()

    def _reset(self):
        self.chunk_toks = {}
        self.stat_toks = []
        self._nchunk = 0

    def xa_slice(self, o):
        return self.xafull[:, o * 512:(o + 1) * 512]

    def xa_free(self, o):
        return self._free.get(("xa", o))

    def add_chunk(self, o, xa_tok, stats_dep=None, xa_ap=None):
        p, s = self.p, self.p.s
        if xa_ap is None:
            xa_ap = self.xa_slice(o)
        self.cur_ap = getattr(self, "cur_ap", {})
        self.cur_ap[o] = xa_ap
        par = self._nchunk % 2
        xb, xq = self.xabf[par], self.xsq[par]
        t_c = s.add("vector", (lambda a_, b_: lambda v:
                    v.tensor_copy(b_[:], a_))(xa_ap, xb),
                    deps=(xa_tok, self._free.get(("c", par))))
        t_q = s.add("scalar", (lambda b_, q_: lambda sc:
                    sc.activation(out=q_[:], in_=b_[:], func=AF.Square,
                                  bias=0.0, scale=1.0))(xb, xq),
                    deps=(t_c, self._free.get(("q", par))))
        first = self._nchunk == 0
        last = self._nchunk == NKC - 1
        dep1 = (t_c, stats_dep if first else None)
        t_s1 = s.add("tensor", (lambda b_, f, l: lambda t:
                     t.matmul(self.bk1[0:1, :], self.C["onesbf"][:, 0:1], b_[:],
                              start=f, stop=l))(xb, first, last), deps=dep1)
        t_s2 = s.add("tensor", (lambda q_, f, l: lambda t:
                     t.matmul(self.bk2[0:1, :], self.C["onesbf"][:, 0:1], q_[:],
                              start=f, stop=l))(xq, first, last),
                     deps=(t_q, stats_dep if first else None))
        self._free[("c", par)] = t_s1
        self._free[("q", par)] = t_s2
        self.chunk_toks[o] = xa_tok
        self.stat_toks = [t_s1, t_s2]
        self._nchunk += 1

    def finish(self, qt):
        p, s = self.p, self.p.s
        assert self._nchunk == NKC
        R = self.rows
        mrow = R[0:1, 0:512]
        e2row = R[0:1, 512:1024]
        trow = R[0:1, 1024:1536]
        vrow = R[0:1, 1536:2048]
        rrow = e2row
        mrrow = trow
        d1, d2 = self.stat_toks
        t_m = s.add("vector", lambda v: v.tensor_scalar(
            out=mrow, in0=self.bk1[0:1, :], scalar1=1.0 / D, scalar2=None,
            op0=ALU.mult), deps=(d1,))
        t_e = s.add("vector", lambda v: v.tensor_scalar(
            out=e2row, in0=self.bk2[0:1, :], scalar1=1.0 / D, scalar2=None,
            op0=ALU.mult), deps=(d2,))
        t_sq = s.add("vector", lambda v: v.tensor_tensor(
            trow, mrow, mrow, op=ALU.mult), deps=(t_m,))
        t_v = s.add("vector", lambda v: v.tensor_tensor(
            vrow, e2row, trow, op=ALU.subtract), deps=(t_e, t_sq))
        t_ln = s.add("scalar", lambda sc: sc.activation(
            out=trow, in_=vrow, func=AF.Ln, bias=EPS, scale=1.0), deps=(t_v,))
        t_r = s.add("scalar", lambda sc: sc.activation(
            out=rrow, in_=trow, func=AF.Exp, bias=0.0, scale=-0.5), deps=(t_ln,))
        t_mr = s.add("vector", lambda v: v.tensor_tensor(
            mrrow, mrow, rrow, op=ALU.mult), deps=(t_r, t_m))
        t_rb = s.add("tensor", lambda t: t.matmul(
            self.bk1[:, :], self.C["onesrowf"][0:1, :], rrow,
            start=True, stop=True), deps=(t_r, d1, d2))
        t_mb = s.add("tensor", lambda t: t.matmul(
            self.bk2[:, :], self.C["onesrowf"][0:1, :], mrrow,
            start=True, stop=True), deps=(t_mr,))
        t_rb = s.add("scalar", lambda sc: sc.activation(
            out=self.rbsb[:], in_=self.bk1[:, :], func=AF.Identity,
            bias=0.0, scale=1.0), deps=(t_rb, self._free.get("rbsb")))
        t_mb = s.add("scalar", lambda sc: sc.activation(
            out=self.mbsb[:], in_=self.bk2[:, :], func=AF.Identity,
            bias=0.0, scale=1.0), deps=(t_mb, self._free.get("mbsb")))
        last = None
        for o in range(NKC):
            xa_ap = self.cur_ap[o]
            t_1 = s.add("vector", (lambda a_: lambda v:
                        v.tensor_tensor(a_, a_, self.rbsb[:], op=ALU.mult))
                        (xa_ap), deps=(t_rb, self.chunk_toks[o]))
            t_2 = s.add("vector", (lambda a_: lambda v:
                        v.tensor_tensor(a_, a_, self.mbsb[:],
                                        op=ALU.subtract))(xa_ap),
                        deps=(t_mb, t_1))
            t_st = s.add("sync", (lambda o_, a_, q_: lambda sy:
                         sy.dma_start(
                             out=self.xres_out[o_ * 128:(o_ + 1) * 128,
                                               q_ * 512:(q_ + 1) * 512],
                             in_=a_))(o, xa_ap, qt),
                         deps=(t_2,), inc=16)
            self._free[("xa", o)] = t_st
            last = t_st
        if getattr(self.p, "dbg_rows", None) is not None and qt == 0:
            s.add("sync", lambda sy: sy.dma_start(
                out=self.p.dbg_rows, in_=R[0:1, :]), deps=(t_mr, t_m, t_e, t_v),
                inc=16)
        vb = self.p.s.barrier_token("vector")
        self._free["rbsb"] = vb
        self._free["mbsb"] = vb
        stats_free = (t_rb, t_mb)
        norm_done = last
        self._reset()
        return norm_done, stats_free


def emit_attn_phase(p, Qd, Kd, Vd, woT, xprev_dram, xres_out, causal,
                    scale, C, cbias, cb_tok, bo_col=None, bias_sb=None,
                    cmask_dram=None, deps=()):
    s = p.s
    psS = p.psS
    psO = p.psG[:, 0:512]
    psA_ = [p.psG[:, 0:512], p.psG[:, 512:1024]]
    psX = p.psX
    psY = p.psY

    qtile = [p.sb(f"at_q{i}", [128, 512], BF16) for i in range(2)]
    ktile = [p.sb(f"at_k{i}", [128, T], BF16) for i in range(2)]
    vtile = [p.sb(f"at_v{i}", [128, T], BF16) for i in range(2)]
    Pt = [p.sb(f"at_p{i}", [128, 1024], BF16) for i in range(2)]
    Ofsb = p.sb("at_of", [128, NH * 512], BF16)
    rvrow = p.sb("at_rv", [1, 512], F32)
    rvb = [p.sb(f"at_rb{i}", [128, 512], F32) for i in range(2)]
    wbuf = [p.sb(f"at_wb{i}", [128, NKC * 128], BF16) for i in range(2)]
    xprev = [p.sb(f"at_xp{i}", [128, 512], F32) for i in range(2)]
    cmasks = [p.sb(f"at_cm{r}", [128, 512], F32) for r in range(4)] \
        if causal else None

    pln = PostLN(p, "at_ln", xres_out, C, (psY, psX))

    free = {}
    t_done = None
    stats_free = None
    if causal:
        cm_toks = []
        for r in range(4):
            cm_toks.append(s.add("sync", (lambda r_: lambda sy:
                           sy.dma_start(out=cmasks[r_][:],
                                        in_=cmask_dram[r_]))(r), inc=16))
        cmask_tok = tuple(cm_toks)
    else:
        cmask_tok = ()

    for qt in range(NTT):
        n_kc = (qt + 1) * 4 if causal else NKC
        KV = n_kc * 128
        for h in range(NH):
            bi = qt * NH + h
            t_q = s.add("sync", (lambda h_, q_, b_: lambda sy:
                        sy.dma_start(out=b_[:],
                                     in_=Qd[h_ * 128:(h_ + 1) * 128,
                                            q_ * 512:(q_ + 1) * 512]))
                        (h, qt, qtile[bi % 2]),
                        deps=(free.get(("q", bi % 2)), deps), inc=16)
            t_k = s.add("sync", (lambda h_, b_, w_: lambda sy:
                        sy.dma_start(out=b_[:, 0:w_],
                                     in_=Kd[h_ * 128:(h_ + 1) * 128, 0:w_]))
                        (h, ktile[bi % 2], KV),
                        deps=(free.get(("k", bi % 2)), deps), inc=16)
            t_v = s.add("sync", (lambda h_, b_, w_, nk_: lambda sy:
                        sy.dma_start(
                            out=b_[:, 0:w_].rearrange("p (c d) -> p c d", d=128),
                            in_=Vd[h_, 0:w_, :].rearrange("(c p) d -> p c d",
                                                          p=128)))
                        (h, vtile[bi % 2], KV, n_kc),
                        deps=(free.get(("v", bi % 2)), deps), inc=16)
            qb, kb, vb = qtile[bi % 2], ktile[bi % 2], vtile[bi % 2]
            e_toks = {}

            t_pv_last = None
            t_sm_last = None

            def emit_s(kc):
                bk = kc % 4
                bank = psS[:, bk * 512:(bk + 1) * 512]
                t_s = s.add("tensor", (lambda b_, kb_, qb_, kc_: lambda t:
                            t.matmul(b_, kb_[:, kc_ * 128:(kc_ + 1) * 128],
                                     qb_[:], start=True, stop=True))
                            (bank, kb, qb, kc),
                            deps=(t_q, t_k, free.get(("S", bk))))
                if causal and (kc >= n_kc - 4):
                    r = kc - (n_kc - 4)
                    t_s = s.add("vector", (lambda b_, r_: lambda v:
                                v.tensor_tensor(b_, b_, cmasks[r_][:],
                                                op=ALU.add))(bank, r),
                                deps=(t_s, cmask_tok))
                e_toks[("s", kc)] = t_s
                if kc % 2 == 1:
                    pr = (kc // 2) % 2
                    pb = psS[:, (bk - 1) * 512:(bk + 1) * 512]
                    pt = Pt[pr]
                    t_e = s.add("scalar", (lambda b_, p_, h_: lambda sc:
                                sc.activation(out=p_[0:128, 0:1024], in_=b_,
                                              func=AF.Exp,
                                              bias=cbias[:, h_:h_ + 1],
                                              scale=float(scale)))(pb, pt, h),
                                deps=(e_toks[("s", kc - 1)], t_s, cb_tok,
                                      free.get(("P", pr))))
                    free[("S", bk)] = t_e
                    free[("S", bk - 1)] = t_e
                    e_toks[kc] = t_e
                    e_toks[kc - 1] = t_e

            for kc in range(min(4, n_kc)):
                emit_s(kc)
            for kc in range(n_kc):
                if kc + 4 < n_kc:
                    emit_s(kc + 4)
                pt = Pt[(kc // 2) % 2][:, (kc % 2) * 512:(kc % 2 + 1) * 512]
                t_e = e_toks[kc]
                t_pv = s.add("tensor", (lambda kc_, f, l, vb_, p_: lambda t:
                             t.matmul(psO, vb_[:, kc_ * 128:(kc_ + 1) * 128],
                                      p_, start=f, stop=l))
                             (kc, kc == 0, kc == n_kc - 1, vb, pt),
                             deps=(t_e, t_v,
                                   free.get("O") if kc == 0 else None))
                t_sm = s.add("tensor", (lambda f, l, p_: lambda t:
                             t.matmul(psX[0:1, :], C["onesbf"][:, 0:1],
                                      p_, start=f, stop=l))
                             (kc == 0, kc == n_kc - 1, pt),
                             deps=(t_e, free.get("N") if kc == 0 else None))
                if kc % 2 == 1:
                    free[("P", (kc // 2) % 2)] = \
                        (t_pv, t_sm, free.get(("Ph", (kc // 2) % 2)))
                else:
                    free[("Ph", (kc // 2) % 2)] = (t_pv, t_sm)
                t_pv_last, t_sm_last = t_pv, t_sm
            t_rv = s.add("vector", lambda v: v.reciprocal(
                rvrow[0:1, :], psX[0:1, :]),
                deps=(t_sm_last, free.get("rvrow")))
            free["N"] = t_rv
            t_rb = s.add("tensor", lambda t: t.matmul(
                psY[:, :], C["onesrowf"][0:1, :], rvrow[0:1, :],
                start=True, stop=True),
                deps=(t_rv, free.get("rvbank")))
            t_re = s.add("vector", (lambda b_: lambda v:
                         v.tensor_copy(b_[:], psY[:, :]))
                         (rvb[bi % 2]), deps=(t_rb, free.get(("rvb", bi % 2))))
            free["rvbank"] = t_re
            free["rvrow"] = t_rb
            t_oe = s.add("vector", (lambda h_, b_: lambda v:
                         v.tensor_tensor(Ofsb[:, h_ * 512:(h_ + 1) * 512],
                                         psO, b_[:], op=ALU.mult))
                         (h, rvb[bi % 2]),
                         deps=(t_pv_last, t_re, free.get("ofr")))
            free[("rvb", bi % 2)] = t_oe
            free["O"] = t_oe
            free[("q", bi % 2)] = s.barrier_token("tensor")
            free[("k", bi % 2)] = s.barrier_token("tensor")
            free[("v", bi % 2)] = t_pv_last
        free["Olast"] = free["O"]

        for o in range(NKC):
            t_w = emit_wtile_load(p, woT[o], wbuf[o % 2], NKC * 128,
                                  deps=(free.get(("wo", o % 2)),))
            t_xp = s.add("sync", (lambda o_, q_, b_: lambda sy:
                         sy.dma_start(out=b_[:],
                                      in_=xprev_dram[o_ * 128:(o_ + 1) * 128,
                                                     q_ * 512:(q_ + 1) * 512]))
                         (o, qt, xprev[o % 2]),
                         deps=(free.get(("xp", o % 2)),), inc=16)
            psA = psA_[o % 2]
            t_a = emit_mm_group(
                p, psA,
                [(wbuf[o % 2][:, k * 128:(k + 1) * 128],
                  Ofsb[:, k * 512:(k + 1) * 512]) for k in range(NKC)],
                deps=(t_w, free["O"], free.get(("A", o % 2)),
                      free.get("Olast")))
            free[("wo", o % 2)] = t_a
            free["ofr"] = t_a
            xa = pln.xa_slice(o)
            if bo_col is not None:
                t_xa = s.add("vector", (lambda o_, xa_, xp_, pa_: lambda v:
                             v.scalar_tensor_tensor(
                                 out=xa_, in0=pa_,
                                 scalar=bias_sb[:, bo_col + o_:bo_col + o_ + 1],
                                 in1=xp_[:], op0=ALU.add, op1=ALU.add))
                             (o, xa, xprev[o % 2], psA),
                             deps=(t_a, t_xp, pln.xa_free(o)))
            else:
                t_xa = s.add("vector", (lambda xa_, xp_, pa_: lambda v:
                             v.tensor_tensor(xa_, pa_, xp_[:], op=ALU.add))
                             (xa, xprev[o % 2], psA),
                             deps=(t_a, t_xp, pln.xa_free(o)))
            free[("A", o % 2)] = t_xa
            free[("xp", o % 2)] = t_xa
            sdep = (stats_free, free.get("rvbank"), free.get("N"))
            pln.add_chunk(o, t_xa, stats_dep=sdep)
        free["O"] = free.get(("A", 0))
        t_done, stats_free = pln.finish(qt)
        free["rvbank"] = (free.get("rvbank"), stats_free[0], t_done)
        free["N"] = (free.get("N"), stats_free[1], t_done)
    return t_done


def load_consts(p, want=("onesbf", "onesrowf")):
    shapes = {
        "onesbf": ([128, 1], BF16),
        "onesrowf": ([1, 128], F32),
        "onesrow1": ([1, 128], BF16),
        "ropeC": ([64, T], F32),
        "ropeS": ([64, T], F32),
        "identB": ([128, 128], BF16),
        "identF": ([128, 128], F32),
        "onehot": ([16, 16 * 128], BF16),
    }
    C = {}
    toks = []
    fcd = p.dram_in("c_fconsts", [128, len(FCONSTS)], F32)
    toks.append(p.s.add("sync", lambda sy: sy.dma_start(out=p.fc_sb[:],
                                                        in_=fcd), inc=16))
    for name in want:
        shape, dt = shapes[name]
        d = p.dram_in("c_" + name, shape, dt)
        sbh = p.sb("cs_" + name, shape, dt)
        toks.append(p.s.add("sync", (lambda d_, s_: lambda sy:
                    sy.dma_start(out=s_[:], in_=d_))(d, sbh), inc=16))
        C[name] = sbh
    for e in ENGINES:
        p.s.add(e, lambda eng: eng.nop(), deps=toks, inc=1)
    return C


def const_in_maps(consts, want):
    m = {("c_" + n): consts[n] for n in want}
    m["c_fconsts"] = consts["fconsts"]
    return m


P1_CONSTS = ("onesbf", "onesrowf", "ropeC", "ropeS")


def build_l0_proj():
    p = P()
    xT = p.dram_in("xT", [D, T])
    wqT = p.dram_in("wqT", [16, 128, D])
    wkT = p.dram_in("wkT", [16, 128, D])
    wvT = p.dram_in("wvT", [4, 128, NKC * 512])
    Qd = p.dram_out("Qd", [D, T], BF16)
    Kd = p.dram_out("Kd", [D, T], BF16)
    Vd = p.dram_out("Vd", [NH, T, HD], BF16)
    CB = p.dram_out("CB", [128, NH])
    C = load_consts(p, P1_CONSTS)
    cbias = p.sb("cbias", [128, NH], F32)
    cb_tok = emit_proj_phase(
        p, xT, [("q", wqT, Qd), ("k", wkT, Kd)], [(wvT, Vd)], C,
        ropeC_sb=C["ropeC"], ropeS_sb=C["ropeS"], cbias_out=cbias,
        scale_for_c=SCALE0)
    p.s.add("sync", lambda sy: sy.dma_start(out=CB, in_=cbias[:]),
            deps=cb_tok, inc=16)
    return p.finalize()


P2_CONSTS = ("onesbf", "onesrowf")


def build_l0_attn():
    p = P()
    Qd = p.dram_in("Qd", [D, T], BF16)
    Kd = p.dram_in("Kd", [D, T], BF16)
    Vd = p.dram_in("Vd", [NH, T, HD], BF16)
    woT = p.dram_in("woT", [16, 128, D])
    xT = p.dram_in("xT", [D, T])
    CB = p.dram_in("CB", [128, NH])
    x1 = p.dram_out("x1", [D, T])
    C = load_consts(p, P2_CONSTS)
    cbias = p.sb("cbias", [128, NH], F32)
    cb_tok = p.s.add("sync", lambda sy: sy.dma_start(out=cbias[:], in_=CB),
                     inc=16)
    emit_attn_phase(p, Qd, Kd, Vd, woT, xT, x1, causal=False, scale=SCALE0,
                    C=C, cbias=cbias, cb_tok=cb_tok)
    return p.finalize()


L0_BC = {"b0a": 0, "b0b": 32, "b1a": 48, "b1b": 80}
L0_BP_W = 96


def emit_l0_ffn(p, x1, WgT, W0aT, W0bT, W1aT, W1bT, bias_sb, x2, C, deps=()):
    s = p.s
    psH = [p.psS[:, i * 512:(i + 1) * 512] for i in range(4)]
    psO_ = [p.psG[:, 0:512], p.psG[:, 512:1024]]
    psX = p.psX

    xbf = p.sb("f_xbf", [128, NKC * 512], BF16)
    xf32 = [p.sb(f"f_xf{i}", [128, 512], F32) for i in range(2)]
    xprev = [p.sb(f"f_xp{i}", [128, 512], F32) for i in range(2)]
    harena = p.sb("f_h", [128, 32 * 512], BF16)
    acc = p.sb("f_acc", [128, NKC * 512], F32)
    wab = [p.sb(f"f_wa{i}", [128, NKC * 128], BF16) for i in range(2)]
    wbb = [p.sb(f"f_wb{i}", [128, 32 * 128], BF16) for i in range(2)]
    wg = p.sb("f_wg", [128, 32], F32)
    wgd = p.sb("f_wgd", [128, 16], F32)
    mrow = p.sb("f_mrow", [1, 512], F32)
    mb = [p.sb(f"f_mb{i}", [128, 512], F32) for i in range(2)]
    m1b = [p.sb(f"f_m1b{i}", [128, 512], F32) for i in range(2)]
    htmp = [p.sb(f"f_ht{i}", [128, 512], F32) for i in range(2)]
    bd = p.sb("f_bd", [128, 16], F32)

    psY = p.psY
    pln = PostLN(p, "f_ln", x2, C, (psX, psY), own_buffer=False)

    t_wg = s.add("sync", lambda sy: sy.dma_start(
        out=wg[:].rearrange("p (c e) -> p c e", e=2),
        in_=chunk3(WgT).rearrange("c p e -> p c e")), deps=deps, inc=16)
    wgv = wg[:].rearrange("p (c e) -> p c e", e=2)
    t_wgd = s.add("vector", lambda v: v.tensor_tensor(
        wgd[:], wgv[:, :, 0], wgv[:, :, 1], op=ALU.subtract), deps=(t_wg,))
    t_bd = s.add("vector", lambda v: v.tensor_tensor(
        bd[:], bias_sb[:, L0_BC["b0b"]:L0_BC["b0b"] + 16],
        bias_sb[:, L0_BC["b1b"]:L0_BC["b1b"] + 16], op=ALU.subtract),
        deps=deps)

    free = {}
    stats_free = None
    t_done = None
    gate_toks = {}

    def emit_gate(tt):
        par = tt % 2
        mm_d = None
        for kc in range(NKC):
            t_x = s.add("sync", (lambda kc_, tt_, b_: lambda sy:
                        sy.dma_start(out=b_[:],
                                     in_=x1[kc_ * 128:(kc_ + 1) * 128,
                                            tt_ * 512:(tt_ + 1) * 512]))
                        (kc, tt, xf32[kc % 2]),
                        deps=(free.get(("xf", kc % 2)), deps), inc=16)
            mm_d = s.add("tensor", (lambda kc_, b_, f, l: lambda t:
                         t.matmul(psX[0:1, :], wgd[:, kc_:kc_ + 1], b_[:],
                                  start=f, stop=l))
                         (kc, xf32[kc % 2], kc == 0, kc == NKC - 1),
                         deps=(t_x, t_wgd,
                               free.get("psXg") if kc == 0 else None))
            free[("xf", kc % 2)] = mm_d
        t_m = s.add("vector", lambda v: v.tensor_scalar(
            out=mrow[0:1, :], in0=psX[0:1, :], scalar1=0.0, scalar2=None,
            op0=ALU.is_ge), deps=(mm_d, free.get("mrow")))
        t_b1 = s.add("tensor", lambda t: t.matmul(
            psX[:, :], C["onesrowf"][0:1, :], mrow[0:1, :],
            start=True, stop=True), deps=(t_m,))
        t_e1 = s.add("scalar", (lambda p_: lambda sc:
                     sc.activation(out=p_[:], in_=psX[:, :], func=AF.Identity,
                                   bias=0.0, scale=1.0))(mb[par]),
                     deps=(t_b1, free.get(("mb", par))))
        t_e2 = s.add("scalar", (lambda p_: lambda sc:
                     sc.activation(out=p_[:], in_=psX[:, :], func=AF.Identity,
                                   bias=1.0, scale=-1.0))(m1b[par]),
                     deps=(t_b1, free.get(("m1b", par))))
        free["psXg"] = (t_e1, t_e2)
        free["mrow"] = t_b1
        gate_toks[tt] = (t_e1, t_e2)

    emit_gate(0)
    for tt in range(NTT):
        par = tt % 2
        for o in range(NKC):
            t_xp = s.add("sync", (lambda o_, tt_, b_: lambda sy:
                         sy.dma_start(out=b_[:],
                                      in_=x1[o_ * 128:(o_ + 1) * 128,
                                             tt_ * 512:(tt_ + 1) * 512]))
                         (o, tt, xprev[o % 2]),
                         deps=(free.get(("xp", o % 2)),), inc=16)
            t_t1 = s.add("vector", (lambda o_, b_, p_: lambda v:
                         v.tensor_scalar(out=b_[:], in0=p_[:],
                                         scalar1=bd[:, o_:o_ + 1],
                                         scalar2=None, op0=ALU.mult))
                         (o, htmp[o % 2], mb[par]),
                         deps=(gate_toks[tt][0], t_bd,
                               free.get(("htmp", o % 2))))
            acc_sl = acc[:, o * 512:(o + 1) * 512]
            t_init = s.add("vector", (lambda o_, a_, xp_, ti_: lambda v:
                           v.scalar_tensor_tensor(
                               out=a_, in0=xp_[:],
                               scalar=bias_sb[:, L0_BC["b1b"] + o_:
                                              L0_BC["b1b"] + o_ + 1],
                               in1=ti_[:], op0=ALU.add, op1=ALU.add))
                           (o, acc_sl, xprev[o % 2], htmp[o % 2]),
                           deps=(t_xp, t_t1, free.get(("acci", o))))
            free[("xp", o % 2)] = t_init
            free[("htmp", o % 2)] = t_init
            free[("acc", o)] = t_init
        xtok = emit_load_xbf(p, x1, xbf, cols=(tt * 512, 512),
                             deps=(deps, free.get("xbf")))
        for ex in range(2):
            WaT = W0aT if ex == 0 else W1aT
            WbT = W0bT if ex == 0 else W1bT
            act = AF.Gelu if ex == 0 else AF.Relu
            bcol = L0_BC["b0a"] if ex == 0 else L0_BC["b1a"]
            mbx = mb[tt % 2] if ex == 0 else m1b[tt % 2]
            for f in range(32):
                t_w = emit_wtile_load(p, WaT[f], wab[f % 2], NKC * 128,
                                      deps=(free.get(("wa", f % 2)),))
                bank = psH[f % 4]
                mmtok = emit_mm_group(
                    p, bank,
                    [(wab[f % 2][:, k * 128:(k + 1) * 128],
                      xbf[:, k * 512:(k + 1) * 512]) for k in range(NKC)],
                    deps=(xtok, t_w, free.get(("hb", f % 4))))
                t_hv = s.add("scalar", (lambda b_, h_, f_, a_, bc_: lambda sc:
                             sc.activation(out=h_[:], in_=b_, func=a_,
                                           bias=bias_sb[:, bc_ + f_:
                                                        bc_ + f_ + 1],
                                           scale=1.0))
                             (bank, htmp[f % 2], f, act, bcol),
                             deps=(mmtok, free.get(("htmp", f % 2))))
                free[("hb", f % 4)] = t_hv
                hsl = harena[:, f * 512:(f + 1) * 512]
                t_hm = s.add("vector", (lambda h_, m_, o_: lambda v:
                             v.tensor_tensor(o_, h_[:], m_[:], op=ALU.mult))
                             (htmp[f % 2], mbx, hsl),
                             deps=(t_hv, gate_toks[tt][ex],
                                   free.get(("hs", f))))
                free[("htmp", f % 2)] = t_hm
                free[("h", f)] = t_hm
                free[("wa", f % 2)] = s.barrier_token("tensor")
            if ex == 1:
                free["xbf"] = s.barrier_token("tensor")
            if ex == 0 and tt + 1 < NTT:
                emit_gate(tt + 1)
            for o in range(NKC):
                t_w = emit_wtile_load(p, WbT[o], wbb[o % 2], 32 * 128,
                                      deps=(free.get(("wb", o % 2)),))
                bank = psO_[o % 2]
                mmtok = emit_mm_group(
                    p, bank,
                    [(wbb[o % 2][:, k * 128:(k + 1) * 128],
                      harena[:, k * 512:(k + 1) * 512]) for k in range(32)],
                    deps=[t_w, free.get(("ob", o % 2))] +
                         [free.get(("h", k)) for k in range(32)])
                acc_sl = acc[:, o * 512:(o + 1) * 512]
                t_add = s.add("vector", (lambda a_, b_: lambda v:
                              v.tensor_tensor(a_, a_, b_, op=ALU.add))
                              (acc_sl, bank),
                              deps=(mmtok, free.get(("acc", o))))
                free[("ob", o % 2)] = t_add
                free[("acc", o)] = t_add
                free[("wb", o % 2)] = s.barrier_token("tensor")
            if getattr(p, "dbg_h0", None) is not None:
                dst = p.dbg_h0 if ex == 0 else p.dbg_h
                for f in range(32):
                    s.add("sync", (lambda f_, a_, d_, t2: lambda sy:
                          sy.dma_start(out=d_[f_ * 128:(f_ + 1) * 128,
                                              t2 * 512:(t2 + 1) * 512],
                                       in_=a_))(f,
                          harena[:, f * 512:(f + 1) * 512], dst, tt),
                          deps=(free[("h", f)],), inc=16)
            bt = s.barrier_token("tensor")
            for f in range(32):
                free[("hs", f)] = bt
        if getattr(p, "dbg_acc", None) is not None:
            for o in range(NKC):
                s.add("sync", (lambda o_, t2, a_: lambda sy:
                      sy.dma_start(out=p.dbg_acc[o_ * 128:(o_ + 1) * 128,
                                                 t2 * 512:(t2 + 1) * 512],
                                   in_=a_))(o, tt,
                      acc[:, o * 512:(o + 1) * 512]),
                      deps=(free[("acc", o)],), inc=16)
        for o in range(NKC):
            acc_sl = acc[:, o * 512:(o + 1) * 512]
            sdep = (stats_free, free.get("psXg"))
            pln.add_chunk(o, free[("acc", o)], stats_dep=sdep, xa_ap=acc_sl)
        t_done, stats_free = pln.finish(tt)
        free["psXg"] = stats_free[0]
        free["psY"] = stats_free[1]
        for o in range(NKC):
            free[("acci", o)] = pln.xa_free(o)
            free.pop(("acc", o), None)
        free[("mb", tt % 2)] = s.barrier_token("vector")
        free[("m1b", tt % 2)] = s.barrier_token("vector")
    return t_done


P3_CONSTS = ("onesbf", "onesrowf")


def build_l0_ffn(debug=False):
    p = P()
    x1 = p.dram_in("x1", [D, T])
    WgT = p.dram_in("WgT", [D, 2])
    W0aT = p.dram_in("W0aT", [32, 128, D])
    W0bT = p.dram_in("W0bT", [16, 128, 32 * 128])
    W1aT = p.dram_in("W1aT", [32, 128, D])
    W1bT = p.dram_in("W1bT", [16, 128, 32 * 128])
    BP = p.dram_in("BP", [128, L0_BP_W])
    x2 = p.dram_out("x2", [D, T])
    if debug:
        p.dbg_acc = p.dram_out("dbg_acc", [D, T])
        p.dbg_h = p.dram_out("dbg_h", [2 * D, T], BF16)
        p.dbg_rows = p.dram_out("dbg_rows", [1, 2048])
        p.dbg_h0 = p.dram_out("dbg_h0", [2 * D, T], BF16)
    C = load_consts(p, P3_CONSTS)
    bias_sb = p.sb("biasp", [128, L0_BP_W], F32)
    t_bp = p.s.add("sync", lambda sy: sy.dma_start(out=bias_sb[:], in_=BP),
                   inc=16)
    emit_l0_ffn(p, x1, WgT, W0aT, W0bT, W1aT, W1bT, bias_sb, x2, C,
                deps=(t_bp,))
    return p.finalize()


L1_BC = {"q": 0, "k": 16, "bo": 32}
L1_BP_W = 48
P4_CONSTS = ("onesbf", "onesrowf", "onesrow1")


def build_l1_proj():
    p = P()
    x2 = p.dram_in("x2", [D, T])
    WqT = p.dram_in("WqT", [16, 128, D])
    WkT = p.dram_in("WkT", [16, 128, D])
    WvT = p.dram_in("WvT", [4, 128, NKC * 512])
    BP = p.dram_in("BP", [128, L1_BP_W])
    bvd = p.dram_in("bv", [1, D])
    Qd = p.dram_out("Qd", [D, T], BF16)
    Kd = p.dram_out("Kd", [D, T], BF16)
    Vd = p.dram_out("Vd", [NH, T, HD], BF16)
    CB = p.dram_out("CB", [128, NH])
    C = load_consts(p, P4_CONSTS)
    bias_sb = p.sb("biasp", [128, L1_BP_W], F32)
    bvrow = p.sb("bvrow", [1, D], BF16)
    t_bp = p.s.add("sync", lambda sy: sy.dma_start(out=bias_sb[:], in_=BP),
                   inc=16)
    t_bv = p.s.add("gpsimd", lambda g: g.dma_start(out=bvrow[0:1, :], in_=bvd),
                   inc=16)
    cbias = p.sb("cbias", [128, NH], F32)
    for e in ENGINES:
        p.s.add(e, lambda eng: eng.nop(), deps=(t_bp, t_bv), inc=1)
    emit_proj_phase(
        p, x2, [("q", WqT, Qd), ("k", WkT, Kd)], [(WvT, Vd)], C,
        bias_sb=bias_sb, qk_bias_cols={"q": L1_BC["q"], "k": L1_BC["k"]},
        v_bias_row=bvrow, cbias_out=cbias, scale_for_c=SCALE1)
    p.s.add("sync", lambda sy: sy.dma_start(out=CB, in_=cbias[:]),
            deps=(p.s.barrier_token("scalar"),), inc=16)
    return p.finalize()


P5_CONSTS = ("onesbf", "onesrowf")


def build_l1_attn():
    p = P()
    Qd = p.dram_in("Qd", [D, T], BF16)
    Kd = p.dram_in("Kd", [D, T], BF16)
    Vd = p.dram_in("Vd", [NH, T, HD], BF16)
    WoT = p.dram_in("WoT", [16, 128, D])
    x2 = p.dram_in("x2", [D, T])
    CB = p.dram_in("CB", [128, NH])
    BP = p.dram_in("BP", [128, L1_BP_W])
    cmk = p.dram_in("cmk", [4, 128, 512])
    x3 = p.dram_out("x3", [D, T])
    C = load_consts(p, P5_CONSTS)
    bias_sb = p.sb("biasp", [128, L1_BP_W], F32)
    t_bp = p.s.add("sync", lambda sy: sy.dma_start(out=bias_sb[:], in_=BP),
                   inc=16)
    cbias = p.sb("cbias", [128, NH], F32)
    cb_tok = p.s.add("sync", lambda sy: sy.dma_start(out=cbias[:], in_=CB),
                     inc=16)
    for e in ENGINES:
        p.s.add(e, lambda eng: eng.nop(), deps=(t_bp,), inc=1)
    emit_attn_phase(p, Qd, Kd, Vd, WoT, x2, x3, causal=True, scale=SCALE1,
                    C=C, cbias=cbias, cb_tok=cb_tok, bo_col=L1_BC["bo"],
                    bias_sb=bias_sb, cmask_dram=[cmk[r] for r in range(4)])
    return p.finalize()


P6_BC = {"sb1": 0, "rb1": 22, "sb2": 198, "rbias": 230}
P6_BP_W = 231
P6_CONSTS = ("onesbf", "onesrowf", "identF", "onehot")
DFF = 1408
NFC = 11


def emit_l1_moe(p, x3, sw1, sw2, rw1, rw2, rb2, centT, bias_sb, yT, C,
                deps=()):
    s = p.s
    psH = [p.psS[:, i * 512:(i + 1) * 512] for i in range(3)]
    psT = p.psS[:, 1536:2048]
    psO_ = [p.psG[:, 0:512], p.psG[:, 512:1024]]
    psX = p.psX

    xbf = p.sb("m_xbf", [128, NKC * 1024], BF16)
    xf32 = [p.sb(f"m_xf{i}", [128, 512], F32) for i in range(2)]
    xprev = xf32
    hpair = p.sb("m_h", [128, 2 * NFC * 1024], BF16)
    acc = p.sb("m_acc", [128, NKC * 1024], F32)
    w1b = [p.sb(f"m_w1{i}", [128, NKC * 128], BF16) for i in range(2)]
    w2b = [p.sb(f"m_w2{i}", [128, NFC * 128], BF16) for i in range(4)]
    centsb = p.sb("m_cent", [128, NKC * 16], F32)
    scor = p.sb("m_scor", [16, 512], F32)
    stok = p.sb("m_stok", [128, 16], F32)
    cmpb = p.sb("m_cmp", [128, 256], BF16)
    rnk = p.sb("m_rnk", [128, 64], BF16)
    gtok = p.sb("m_gtok", [128, 16], F32)
    grow = p.sb("m_grow", [128, 2], F32)
    ge16b = [p.sb(f"m_geb{i}", [16, 1024], BF16) for i in range(2)]
    rb2b = p.sb("m_rb2", [16, 2048], BF16)
    b2s = p.sb("m_b2s", [128, 16], F32)
    gb = [[p.sb(f"m_gb{i}_{t}", [128, 512], BF16) for t in range(2)]
          for i in range(2)]
    htmp = [p.sb(f"m_ht{i}", [128, 512], BF16) for i in range(2)]

    psY = p.psY
    pln = PostLN(p, "m_ln", yT, C, (psX, psY), own_buffer=False)

    t_ct = s.add("sync", lambda sy: sy.dma_start(
        out=centsb[:].rearrange("p (c e) -> p c e", e=16),
        in_=chunk3(centT).rearrange("c p e -> p c e")), deps=deps, inc=16)
    t_r2 = s.add("gpsimd", lambda g: g.dma_start(out=rb2b[:], in_=rb2),
                 deps=deps, inc=16)
    t_b2 = s.add("vector", lambda v: v.tensor_tensor(
        b2s[:], bias_sb[:, P6_BC["sb2"]:P6_BC["sb2"] + 16],
        bias_sb[:, P6_BC["sb2"] + 16:P6_BC["sb2"] + 32], op=ALU.add),
        deps=deps)

    free = {}
    stats_free = None
    t_done = None

    def emit_router(half):
        geb = ge16b[half]
        ge_toks = []
        for tth in range(2):
            tt = half * 2 + tth
            mm_d = None
            for kc in range(NKC):
                t_x = s.add("sync", (lambda kc_, tt_, b_: lambda sy:
                            sy.dma_start(out=b_[:],
                                         in_=x3[kc_ * 128:(kc_ + 1) * 128,
                                                tt_ * 512:(tt_ + 1) * 512]))
                            (kc, tt, xf32[kc % 2]),
                            deps=(free.get(("xf", kc % 2)), deps), inc=16)
                mm_d = s.add("tensor", (lambda kc_, b_, f, l: lambda t:
                             t.matmul(psX[0:16, :],
                                      centsb[:, kc_ * 16:(kc_ + 1) * 16],
                                      b_[:], start=f, stop=l))
                             (kc, xf32[kc % 2], kc == 0, kc == NKC - 1),
                             deps=(t_x, t_ct,
                                   free.get("psXr") if kc == 0 else None))
                free[("xf", kc % 2)] = mm_d
            t_sg = s.add("scalar", lambda sc: sc.activation(
                out=scor[:], in_=psX[0:16, :], func=AF.Sigmoid, bias=0.0,
                scale=1.0), deps=(mm_d, free.get("scor")))
            free["psXr"] = t_sg
            t_sb = s.add("vector", lambda v: v.tensor_scalar(
                out=scor[:], in0=scor[:],
                scalar1=bias_sb[0:16, P6_BC["rbias"]:P6_BC["rbias"] + 1],
                scalar2=None, op0=ALU.add), deps=(t_sg, deps))
            for tc in range(4):
                t_tr = s.add("tensor", (lambda tc_: lambda t:
                             t.transpose(psT[:, 0:16],
                                         scor[0:16, tc_ * 128:(tc_ + 1) * 128],
                                         C["identF"][0:16, 0:16]))(tc),
                             deps=(t_sb, free.get("psT")))
                t_st = s.add("vector", lambda v: v.tensor_copy(
                    stok[:], psT[:, 0:16]), deps=(t_tr, free.get("stok")))
                free["psT"] = t_st
                cps = []
                for e in range(16):
                    cps.append(s.add("vector", (lambda e_: lambda v:
                               v.tensor_scalar(
                                   out=cmpb[:, e_ * 16:(e_ + 1) * 16],
                                   in0=stok[:], scalar1=stok[:, e_:e_ + 1],
                                   scalar2=None, op0=ALU.is_lt))(e),
                               deps=(t_st, free.get("cmpb"))))
                t_a1 = s.add("vector", lambda v: v.tensor_tensor(
                    rnk[:, 0:64], cmpb[:, 0:64], cmpb[:, 64:128], op=ALU.add),
                    deps=cps[0:8])
                t_a2 = s.add("vector", lambda v: v.tensor_tensor(
                    rnk[:, 0:64], rnk[:, 0:64], cmpb[:, 128:192], op=ALU.add),
                    deps=(t_a1,) + tuple(cps[8:12]))
                t_a3 = s.add("vector", lambda v: v.tensor_tensor(
                    rnk[:, 0:64], rnk[:, 0:64], cmpb[:, 192:256], op=ALU.add),
                    deps=(t_a2,) + tuple(cps[12:16]))
                free["cmpb"] = t_a3
                t_a4 = s.add("vector", lambda v: v.tensor_tensor(
                    rnk[:, 0:32], rnk[:, 0:32], rnk[:, 32:64], op=ALU.add),
                    deps=(t_a3,))
                t_a5 = s.add("vector", lambda v: v.tensor_tensor(
                    rnk[:, 0:16], rnk[:, 0:16], rnk[:, 16:32], op=ALU.add),
                    deps=(t_a4,))
                t_k = s.add("vector", lambda v: v.tensor_scalar(
                    out=rnk[:, 16:32], in0=rnk[:, 0:16], scalar1=4.0,
                    scalar2=None, op0=ALU.is_lt), deps=(t_a5,))
                t_g0 = s.add("vector", lambda v: v.tensor_tensor(
                    gtok[:], stok[:], rnk[:, 16:32], op=ALU.mult),
                    deps=(t_k, free.get("gtok")))
                t_gs = s.add("vector", lambda v: v.reduce_sum(
                    grow[:, 0:1], gtok[:], axis=AX.X), deps=(t_g0,))
                t_gc = s.add("vector", lambda v: v.tensor_scalar(
                    out=grow[:, 0:1], in0=grow[:, 0:1], scalar1=1e-8,
                    scalar2=None, op0=ALU.max), deps=(t_gs,))
                t_gr = s.add("vector", lambda v: v.reciprocal(
                    grow[:, 1:2], grow[:, 0:1]), deps=(t_gc,))
                t_gn = s.add("vector", lambda v: v.tensor_scalar(
                    out=gtok[:], in0=gtok[:], scalar1=grow[:, 1:2],
                    scalar2=None, op0=ALU.mult), deps=(t_gr,))
                t_tb = s.add("tensor", (lambda: lambda t:
                             t.transpose(psT[0:16, 16:144], gtok[:],
                                         C["identF"][:, :]))(),
                             deps=(t_gn, free.get("psT2")))
                col = tth * 512 + tc * 128
                t_ev = s.add("scalar", (lambda c_, g_: lambda sc:
                             sc.activation(out=g_[0:16, c_:c_ + 128],
                                           in_=psT[0:16, 16:144],
                                           func=AF.Identity, bias=0.0,
                                           scale=1.0))(col, geb),
                             deps=(t_tb, free.get(("ge16", half))))
                free["psT2"] = t_ev
                free["stok"] = cps
                free["gtok"] = t_tb
                ge_toks.append(t_ev)
            free["scor"] = s.barrier_token("tensor")
        return tuple(ge_toks)

    ge_done = [emit_router(0), emit_router(1)]
    for half in range(2):
        th0 = half * 1024
        t_geb = ge_done[half]
        xtok = emit_load_xbf(p, x3, xbf, cols=(th0, 1024),
                             deps=(deps, free.get("xbf")))
        for tth in range(2):
            tt = half * 2 + tth
            for o in range(NKC):
                t_xp = s.add("sync", (lambda o_, tt_, b_: lambda sy:
                             sy.dma_start(out=b_[:],
                                          in_=x3[o_ * 128:(o_ + 1) * 128,
                                                 tt_ * 512:(tt_ + 1) * 512]))
                             (o, tt, xprev[o % 2]),
                             deps=(free.get(("xf", o % 2)),), inc=16)
                acc_sl = acc[:, o * 1024 + tth * 512:
                             o * 1024 + tth * 512 + 512]
                t_init = s.add("vector", (lambda o_, a_, xp_: lambda v:
                               v.tensor_scalar(
                                   out=a_, in0=xp_[:], scalar1=2.0,
                                   scalar2=b2s[:, o_:o_ + 1],
                                   op0=ALU.mult, op1=ALU.add))
                               (o, acc_sl, xprev[o % 2]),
                               deps=(t_xp, t_b2, free.get(("acci", o, tth))))
                free[("xf", o % 2)] = t_init
                free[("acc", o, tth)] = t_init

        for pi in range(9):
            members = []
            if pi == 0:
                members = [("s", 0), ("s", 1)]
            else:
                members = [("r", (pi - 1) * 2), ("r", (pi - 1) * 2 + 1)]
            gb_toks = {}
            for mi, (kind, j) in enumerate(members):
                if kind == "r":
                    for tth in range(2):
                        t_gb = s.add("tensor", (lambda j_, t2, g_: lambda t:
                                     t.matmul(psT[:, 0:512],
                                              C["onehot"][0:16,
                                                          j_ * 128:(j_ + 1) * 128],
                                              g_[0:16,
                                                 t2 * 512:(t2 + 1) * 512],
                                              start=True, stop=True))
                                     (j, tth, ge16b[half]),
                                     deps=(t_geb, free.get("psT2")))
                        t_ge = s.add("scalar", (lambda b_: lambda sc:
                                     sc.activation(out=b_[:], in_=psT[:, 0:512],
                                                   func=AF.Identity, bias=0.0,
                                                   scale=1.0))
                                     (gb[mi][tth]), deps=(t_gb,
                                                     free.get(("gb", mi, tth))))
                        free["psT2"] = t_ge
                        gb_toks[(mi, tth)] = t_ge
            for mi, (kind, j) in enumerate(members):
                if kind == "s":
                    w1ap = sw1[j]
                    bcol = P6_BC["sb1"] + j * NFC
                else:
                    w1ap = rw1[j]
                    bcol = P6_BC["rb1"] + j * NFC
                for f in range(NFC):
                    t_w = emit_wtile_load(p, w1ap[f], w1b[f % 2], NKC * 128,
                                          deps=(free.get(("w1", f % 2)),))
                    for tth in range(2):
                        g = f * 2 + tth
                        bank = psH[g % 3]
                        mmtok = emit_mm_group(
                            p, bank,
                            [(w1b[f % 2][:, k * 128:(k + 1) * 128],
                              xbf[:, k * 1024 + tth * 512:
                                  k * 1024 + tth * 512 + 512])
                             for k in range(NKC)],
                            deps=(xtok, t_w, free.get(("hb", g % 3))))
                        hsl = hpair[:, (mi * NFC + f) * 1024 + tth * 512:
                                    (mi * NFC + f) * 1024 + tth * 512 + 512]
                        if kind == "s":
                            t_hv = s.add("scalar",
                                         (lambda b_, o_, bc_, f_: lambda sc:
                                          sc.activation(
                                              out=o_, in_=b_, func=AF.Relu,
                                              bias=bias_sb[:, bc_ + f_:
                                                           bc_ + f_ + 1],
                                              scale=1.0))
                                         (bank, hsl, bcol, f),
                                         deps=(mmtok,
                                               free.get(("hs", mi, f, tth))))
                            free[("hb", g % 3)] = t_hv
                            t_hm = t_hv
                        else:
                            t_hv = s.add("scalar",
                                         (lambda b_, h_, bc_, f_: lambda sc:
                                          sc.activation(
                                              out=h_[:], in_=b_, func=AF.Relu,
                                              bias=bias_sb[:, bc_ + f_:
                                                           bc_ + f_ + 1],
                                              scale=1.0))
                                         (bank, htmp[g % 2], bcol, f),
                                         deps=(mmtok,
                                               free.get(("htmp", g % 2))))
                            free[("hb", g % 3)] = t_hv
                            t_hm = s.add("vector",
                                         (lambda h_, m_, o_: lambda v:
                                          v.tensor_tensor(o_, h_[:], m_[:],
                                                          op=ALU.mult))
                                         (htmp[g % 2], gb[mi][tth], hsl),
                                         deps=(t_hv, gb_toks[(mi, tth)],
                                               free.get(("hs", mi, f, tth))))
                            free[("htmp", g % 2)] = t_hm
                        free[("h", mi, f, tth)] = t_hm
                    free[("w1", f % 2)] = s.barrier_token("tensor")
            for o in range(NKC):
                wtoks = []
                for mi, (kind, j) in enumerate(members):
                    w2ap = sw2[j] if kind == "s" else rw2[j]
                    wtoks.append(emit_wtile_load(
                        p, w2ap[o], w2b[2 * (o % 2) + mi], NFC * 128,
                        deps=(free.get(("w2", 2 * (o % 2) + mi)),)))
                for tth in range(2):
                    tt = half * 2 + tth
                    bank = psO_[(o * 2 + tth) % 2]
                    pairs = []
                    for mi in range(2):
                        pairs += [(w2b[2 * (o % 2) + mi][:, k * 128:(k + 1) * 128],
                                   hpair[:, (mi * NFC + k) * 1024 + tth * 512:
                                         (mi * NFC + k) * 1024 + tth * 512 + 512])
                                  for k in range(NFC)]
                    if pi == 0:
                        pairs.append((rb2b[0:16, o * 128:(o + 1) * 128],
                                      ge16b[half][0:16,
                                            tth * 512:(tth + 1) * 512]))
                    hdeps = [free.get(("h", mi, k, tth))
                             for mi in range(2) for k in range(NFC)]
                    mmtok = emit_mm_group(
                        p, bank, pairs,
                        deps=[wtoks[0], wtoks[1], t_r2, t_geb,
                              free.get(("ob", (o * 2 + tth) % 2))] + hdeps)
                    acc_sl = acc[:, o * 1024 + tth * 512:
                                 o * 1024 + tth * 512 + 512]
                    t_add = s.add("vector", (lambda a_, b_: lambda v:
                                  v.tensor_tensor(a_, a_, b_, op=ALU.add))
                                  (acc_sl, bank),
                                  deps=(mmtok, free.get(("acc", o, tth))))
                    free[("ob", (o * 2 + tth) % 2)] = t_add
                    free[("acc", o, tth)] = t_add
                for mi in range(2):
                    free[("w2", 2 * (o % 2) + mi)] = s.barrier_token("tensor")
            bt = s.barrier_token("tensor")
            for mi in range(2):
                for f in range(NFC):
                    for tth in range(2):
                        free[("hs", mi, f, tth)] = bt
                for tth in range(2):
                    free[("gb", mi, tth)] = bt
            if pi == 8:
                free["xbf"] = s.barrier_token("tensor")

        for tth in range(2):
            tt = half * 2 + tth
            for o in range(NKC):
                acc_sl = acc[:, o * 1024 + tth * 512:
                             o * 1024 + tth * 512 + 512]
                sdep = (stats_free, free.get("psXr"))
                pln.add_chunk(o, free[("acc", o, tth)], stats_dep=sdep,
                              xa_ap=acc_sl)
            t_done, stats_free = pln.finish(tt)
            free["psXr"] = stats_free[0]
        for o in range(NKC):
            for tth in range(2):
                free[("acci", o, tth)] = pln.xa_free(o)
                free.pop(("acc", o, tth), None)
        if getattr(p, "dbg_ge", None) is not None:
            s.add("sync", (lambda h_, g_: lambda sy: sy.dma_start(
                  out=p.dbg_ge[:, h_ * 1024:(h_ + 1) * 1024], in_=g_[:]))
                  (half, ge16b[half]), deps=t_geb, inc=16)
        if getattr(p, "dbg_acc", None) is not None:
            for o in range(NKC):
                for tth in range(2):
                    t2 = half * 2 + tth
                    s.add("sync", (lambda o_, t2_, a_: lambda sy:
                          sy.dma_start(out=p.dbg_acc[o_ * 128:(o_ + 1) * 128,
                                                     t2_ * 512:(t2_ + 1) * 512],
                                       in_=a_))(o, t2,
                          acc[:, o * 1024 + tth * 512:o * 1024 + tth * 512 + 512]),
                          deps=(free.get(("acci", o, tth)),
                                s.barrier_token("vector")), inc=16)
        free["xbf"] = s.barrier_token("tensor")
        free[("ge16", half)] = s.barrier_token("tensor")
    return t_done


def build_l1_moe(debug=False):
    p = P()
    x3 = p.dram_in("x3", [D, T])
    sw1d = p.dram_in("sw1", [2, NFC, 128, D])
    sw2d = p.dram_in("sw2", [2, 16, 128, NFC * 128])
    rw1d = p.dram_in("rw1", [16, NFC, 128, D])
    rw2d = p.dram_in("rw2", [16, 16, 128, NFC * 128])
    rb2 = p.dram_in("rb2", [16, D])
    centT = p.dram_in("centT", [D, 16])
    BP = p.dram_in("BP", [128, P6_BP_W])
    yT = p.dram_out("yT", [D, T])
    if debug:
        p.dbg_ge = p.dram_out("dbg_ge", [16, T], BF16)
        p.dbg_acc = p.dram_out("dbg_acc", [D, T])
    C = load_consts(p, P6_CONSTS)
    bias_sb = p.sb("biasp", [128, P6_BP_W], F32)
    t_bp = p.s.add("sync", lambda sy: sy.dma_start(out=bias_sb[:], in_=BP),
                   inc=16)
    for e in ENGINES:
        p.s.add(e, lambda eng: eng.nop(), deps=(t_bp,), inc=1)
    emit_l1_moe(p, x3, [sw1d[i] for i in range(2)], [sw2d[i] for i in range(2)],
                [rw1d[i] for i in range(16)], [rw2d[i] for i in range(16)],
                rb2, centT, bias_sb, yT, C)
    return p.finalize()


_NC_CACHE = {}


def _get_nc(name, builder):
    if name not in _NC_CACHE:
        _NC_CACHE[name] = builder()
    return _NC_CACHE[name]


def kernel_unfused(**ins):
    consts = host_consts()
    ncores = 8
    cores = list(range(ncores))
    x = ins["L0_attn_x"]

    def ct(a):
        return np.ascontiguousarray(np.asarray(a, dtype=np.float32))

    xT = [np.ascontiguousarray(x[b].T) for b in range(ncores)]

    nc1 = _get_nc("p1", build_l0_proj)
    w1 = {"wqT": tile_wT(ins["L0_attn_wq"].T, 128),
          "wkT": tile_wT(ins["L0_attn_wk"].T, 128),
          "wvT": tile_wT(ins["L0_attn_wv"].T, 512)}
    cm1 = const_in_maps(consts, P1_CONSTS)
    r1 = run_bass_kernel_spmd(
        nc1, [dict(xT=xT[b], **w1, **cm1) for b in range(ncores)], cores)

    nc2 = _get_nc("p2", build_l0_attn)
    w2 = {"woT": tile_wT(ins["L0_attn_wo"].T, 128)}
    cm2 = const_in_maps(consts, P2_CONSTS)
    r2 = run_bass_kernel_spmd(
        nc2, [dict(Qd=r1.results[b]["Qd"], Kd=r1.results[b]["Kd"],
                   Vd=r1.results[b]["Vd"], CB=r1.results[b]["CB"],
                   xT=xT[b], **w2, **cm2) for b in range(ncores)], cores)

    nc3 = _get_nc("p3", build_l0_ffn)
    bp3 = pack_cols(ins["L0_ffn_b0a"], ins["L0_ffn_b0b"],
                    ins["L0_ffn_b1a"], ins["L0_ffn_b1b"])
    w3 = {"WgT": ct(ins["L0_ffn_Wg"].T),
          "W0aT": tile_wT(ins["L0_ffn_W0a"].T, 128),
          "W0bT": tile_wT(ins["L0_ffn_W0b"].T, 128, nk=32),
          "W1aT": tile_wT(ins["L0_ffn_W1a"].T, 128),
          "W1bT": tile_wT(ins["L0_ffn_W1b"].T, 128, nk=32), "BP": bp3}
    cm3 = const_in_maps(consts, P3_CONSTS)
    r3 = run_bass_kernel_spmd(
        nc3, [dict(x1=r2.results[b]["x1"], **w3, **cm3)
              for b in range(ncores)], cores)

    nc4 = _get_nc("p4", build_l1_proj)
    bp4 = pack_cols(ins["L1_attn_bq"], ins["L1_attn_bk"], ins["L1_attn_bo"])
    w4 = {"WqT": tile_wT(ins["L1_attn_Wq"].T, 128),
          "WkT": tile_wT(ins["L1_attn_Wk"].T, 128),
          "WvT": tile_wT(ins["L1_attn_Wv"].T, 512), "BP": bp4,
          "bv": ct(ins["L1_attn_bv"]).reshape(1, -1)}
    cm4 = const_in_maps(consts, P4_CONSTS)
    r4 = run_bass_kernel_spmd(
        nc4, [dict(x2=r3.results[b]["x2"], **w4, **cm4)
              for b in range(ncores)], cores)

    nc5 = _get_nc("p5", build_l1_attn)
    w5 = {"WoT": tile_wT(ins["L1_attn_Wo"].T, 128), "BP": bp4,
          "cmk": consts["cmaskK"]}
    cm5 = const_in_maps(consts, P5_CONSTS)
    r5 = run_bass_kernel_spmd(
        nc5, [dict(Qd=r4.results[b]["Qd"], Kd=r4.results[b]["Kd"],
                   Vd=r4.results[b]["Vd"], CB=r4.results[b]["CB"],
                   x2=r3.results[b]["x2"], **w5, **cm5)
              for b in range(ncores)], cores)

    nc6 = _get_nc("p6", build_l1_moe)
    bp6 = np.zeros((128, P6_BP_W), dtype=np.float32)
    sb1 = np.asarray(ins["L1_ffn_shared_b1"], dtype=np.float32)
    sb2 = np.asarray(ins["L1_ffn_shared_b2"], dtype=np.float32)
    rb1 = np.asarray(ins["L1_ffn_routed_b1"], dtype=np.float32)
    bp6[:, 0:22] = pack_cols(sb1[0], sb1[1])
    bp6[:, 22:198] = pack_cols(*[rb1[j] for j in range(16)])
    bp6[:, 198:230] = pack_cols(sb2[0], sb2[1])
    bp6[0:16, 230] = np.asarray(ins["L1_ffn_bias"], dtype=np.float32)
    w6 = {"sw1": np.stack([tile_wT(ins["L1_ffn_shared_w1"][n], 128)
                           for n in range(2)]),
          "sw2": np.stack([tile_wT(ins["L1_ffn_shared_w2"][n], 128, nk=NFC)
                           for n in range(2)]),
          "rw1": np.stack([tile_wT(ins["L1_ffn_routed_w1"][j], 128)
                           for j in range(16)]),
          "rw2": np.stack([tile_wT(ins["L1_ffn_routed_w2"][j], 128, nk=NFC)
                           for j in range(16)]),
          "rb2": ct(ins["L1_ffn_routed_b2"]),
          "centT": ct(ins["L1_ffn_centroids"].T), "BP": bp6}
    cm6 = const_in_maps(consts, P6_CONSTS)
    r6 = run_bass_kernel_spmd(
        nc6, [dict(x3=r5.results[b]["x3"], **w6, **cm6)
              for b in range(ncores)], cores)

    out = np.stack([r6.results[b]["yT"].T for b in range(ncores)], axis=0)
    return out.astype(np.float32)


FUSED_CONSTS = ("onesbf", "onesrowf", "onesrow1", "identF", "onehot")


def build_fused():
    p = P()
    xT = p.dram_in("xT", [D, T])
    wqT0 = p.dram_in("wqT0", [16, 128, D]); wkT0 = p.dram_in("wkT0", [16, 128, D])
    wvT0 = p.dram_in("wvT0", [4, 128, NKC * 512])
    woT0 = p.dram_in("woT0", [16, 128, D])
    WgT = p.dram_in("WgT", [D, 2])
    W0aT = p.dram_in("W0aT", [32, 128, D])
    W0bT = p.dram_in("W0bT", [16, 128, 32 * 128])
    W1aT = p.dram_in("W1aT", [32, 128, D])
    W1bT = p.dram_in("W1bT", [16, 128, 32 * 128])
    BP0 = p.dram_in("BP0", [128, L0_BP_W])
    wqT1 = p.dram_in("wqT1", [16, 128, D]); wkT1 = p.dram_in("wkT1", [16, 128, D])
    wvT1 = p.dram_in("wvT1", [4, 128, NKC * 512])
    woT1 = p.dram_in("woT1", [16, 128, D])
    BP1 = p.dram_in("BP1", [128, L1_BP_W])
    bvd = p.dram_in("bv", [1, D])
    sw1d = p.dram_in("sw1", [2, NFC, 128, D])
    sw2d = p.dram_in("sw2", [2, 16, 128, NFC * 128])
    rw1d = p.dram_in("rw1", [16, NFC, 128, D])
    rw2d = p.dram_in("rw2", [16, 16, 128, NFC * 128])
    rb2 = p.dram_in("rb2", [16, D])
    centT = p.dram_in("centT", [D, 16])
    BP6 = p.dram_in("BP6", [128, P6_BP_W])
    ropeCd = p.dram_in("c_ropeC", [64, T])
    ropeSd = p.dram_in("c_ropeS", [64, T])
    cmk = p.dram_in("cmk", [4, 128, 512])
    yT = p.dram_out("yT", [D, T])

    Qd = p.dram_tmp("Qs", [D, T], BF16)
    Kd = p.dram_tmp("Ks", [D, T], BF16)
    Vd = p.dram_tmp("Vs", [NH, T, HD], BF16)
    x1 = p.dram_tmp("x1s", [D, T])
    x2 = p.dram_tmp("x2s", [D, T])
    x3 = p.dram_tmp("x3s", [D, T])

    C = load_consts(p, FUSED_CONSTS)
    cb0 = p.sb("cb0", [128, NH], F32)
    cb1 = p.sb("cb1", [128, NH], F32)
    bias0 = p.sb("bias0", [128, L0_BP_W], F32)
    bias1 = p.sb("bias1", [128, L1_BP_W], F32)
    bias6 = p.sb("bias6", [128, P6_BP_W], F32)
    bvrow = p.sb("bvrow", [1, D], BF16)
    tb = [p.s.add("sync", (lambda d_, s_: lambda sy:
                  sy.dma_start(out=s_[:], in_=d_))(d, sbh), inc=16)
          for d, sbh in ((BP0, bias0), (BP1, bias1), (BP6, bias6))]
    tb.append(p.s.add("gpsimd", lambda g: g.dma_start(out=bvrow[0:1, :],
                                                      in_=bvd), inc=16))
    for e in ENGINES:
        p.s.add(e, lambda eng: eng.nop(), deps=tb, inc=1)

    p.push_scope()
    rC = p.sb("rC", [64, T], F32)
    rS = p.sb("rS", [64, T], F32)
    t1 = p.s.add("sync", lambda sy: sy.dma_start(out=rC[:], in_=ropeCd), inc=16)
    t2 = p.s.add("sync", lambda sy: sy.dma_start(out=rS[:], in_=ropeSd), inc=16)
    for e in ("vector",):
        p.s.add(e, lambda eng: eng.nop(), deps=(t1, t2), inc=1)
    emit_proj_phase(p, xT, [("q", wqT0, Qd), ("k", wkT0, Kd)], [(wvT0, Vd)],
                    C, ropeC_sb=rC, ropeS_sb=rS, cbias_out=cb0,
                    scale_for_c=SCALE0)
    p.phase_barrier()
    p.pop_scope()

    p.push_scope()
    cbt = p.s.barrier_token("scalar")
    emit_attn_phase(p, Qd, Kd, Vd, woT0, xT, x1, causal=False, scale=SCALE0,
                    C=C, cbias=cb0, cb_tok=cbt)
    p.phase_barrier()
    p.pop_scope()

    p.push_scope()
    emit_l0_ffn(p, x1, WgT, W0aT, W0bT, W1aT, W1bT, bias0, x2, C)
    p.phase_barrier()
    p.pop_scope()

    p.push_scope()
    emit_proj_phase(p, x2, [("q", wqT1, Qd), ("k", wkT1, Kd)], [(wvT1, Vd)],
                    C, bias_sb=bias1,
                    qk_bias_cols={"q": L1_BC["q"], "k": L1_BC["k"]},
                    v_bias_row=bvrow, cbias_out=cb1, scale_for_c=SCALE1)
    p.phase_barrier()
    p.pop_scope()

    p.push_scope()
    cbt1 = p.s.barrier_token("scalar")
    emit_attn_phase(p, Qd, Kd, Vd, woT1, x2, x3, causal=True, scale=SCALE1,
                    C=C, cbias=cb1, cb_tok=cbt1, bo_col=L1_BC["bo"],
                    bias_sb=bias1, cmask_dram=[cmk[r] for r in range(4)])
    p.phase_barrier()
    p.pop_scope()

    p.push_scope()
    emit_l1_moe(p, x3, [sw1d[i] for i in range(2)], [sw2d[i] for i in range(2)],
                [rw1d[i] for i in range(16)], [rw2d[i] for i in range(16)],
                rb2, centT, bias6, yT, C)
    p.pop_scope()
    return p.finalize()


def kernel_fused(**ins):
    consts = host_consts()
    ncores = 8
    cores = list(range(ncores))
    x = ins["L0_attn_x"]

    def ct(a):
        return np.ascontiguousarray(np.asarray(a, dtype=np.float32))

    nc = _get_nc("fused", build_fused)
    bp3 = pack_cols(ins["L0_ffn_b0a"], ins["L0_ffn_b0b"],
                    ins["L0_ffn_b1a"], ins["L0_ffn_b1b"])
    bp4 = pack_cols(ins["L1_attn_bq"], ins["L1_attn_bk"], ins["L1_attn_bo"])
    bp6 = np.zeros((128, P6_BP_W), dtype=np.float32)
    sb1 = np.asarray(ins["L1_ffn_shared_b1"], dtype=np.float32)
    sb2 = np.asarray(ins["L1_ffn_shared_b2"], dtype=np.float32)
    rb1 = np.asarray(ins["L1_ffn_routed_b1"], dtype=np.float32)
    bp6[:, 0:22] = pack_cols(sb1[0], sb1[1])
    bp6[:, 22:198] = pack_cols(*[rb1[j] for j in range(16)])
    bp6[:, 198:230] = pack_cols(sb2[0], sb2[1])
    bp6[0:16, 230] = np.asarray(ins["L1_ffn_bias"], dtype=np.float32)
    shared = {
        "wqT0": tile_wT(ins["L0_attn_wq"].T, 128),
        "wkT0": tile_wT(ins["L0_attn_wk"].T, 128),
        "wvT0": tile_wT(ins["L0_attn_wv"].T, 512),
        "woT0": tile_wT(ins["L0_attn_wo"].T, 128),
        "WgT": ct(ins["L0_ffn_Wg"].T),
        "W0aT": tile_wT(ins["L0_ffn_W0a"].T, 128),
        "W0bT": tile_wT(ins["L0_ffn_W0b"].T, 128, nk=32),
        "W1aT": tile_wT(ins["L0_ffn_W1a"].T, 128),
        "W1bT": tile_wT(ins["L0_ffn_W1b"].T, 128, nk=32),
        "BP0": bp3,
        "wqT1": tile_wT(ins["L1_attn_Wq"].T, 128),
        "wkT1": tile_wT(ins["L1_attn_Wk"].T, 128),
        "wvT1": tile_wT(ins["L1_attn_Wv"].T, 512),
        "woT1": tile_wT(ins["L1_attn_Wo"].T, 128),
        "BP1": bp4, "bv": ct(ins["L1_attn_bv"]).reshape(1, -1),
        "sw1": np.stack([tile_wT(ins["L1_ffn_shared_w1"][n], 128)
                         for n in range(2)]),
        "sw2": np.stack([tile_wT(ins["L1_ffn_shared_w2"][n], 128, nk=NFC)
                         for n in range(2)]),
        "rw1": np.stack([tile_wT(ins["L1_ffn_routed_w1"][j], 128)
                         for j in range(16)]),
        "rw2": np.stack([tile_wT(ins["L1_ffn_routed_w2"][j], 128, nk=NFC)
                         for j in range(16)]),
        "rb2": ct(ins["L1_ffn_routed_b2"]),
        "centT": ct(ins["L1_ffn_centroids"].T), "BP6": bp6,
        "c_ropeC": consts["ropeC"], "c_ropeS": consts["ropeS"],
        "cmk": consts["cmaskK"],
    }
    shared.update(const_in_maps(consts, FUSED_CONSTS))
    in_maps = [dict(xT=np.ascontiguousarray(x[b].T), **shared)
               for b in range(ncores)]
    res = run_bass_kernel_spmd(nc, in_maps, cores)
    out = np.stack([res.results[b]["yT"].T for b in range(ncores)], axis=0)
    return out.astype(np.float32)


kernel = kernel_fused


# version 08
# speedup vs baseline: 1.1485x; 1.1485x over previous
import numpy as np
import ml_dtypes
import concourse.bass as bass
import concourse.mybir as mybir
from concourse.bass_utils import run_bass_kernel_spmd

F32 = mybir.dt.float32
BF16 = mybir.dt.bfloat16
U8 = mybir.dt.uint8
AF = mybir.ActivationFunctionType
ALU = mybir.AluOpType
AX = mybir.AxisListType
BF = ml_dtypes.bfloat16

D = 2048
T = 2048
NH = 16
HD = 128
NKC = D // 128
NTT = T // 512
NTC = T // 128
EPS = 1e-5
SCALE0 = float(1.0 / np.sqrt(128.0))
SCALE1 = float(1.0 / np.sqrt(128.0))
FCONSTS = [1e-5, 1e-30, 0.5, -0.5, SCALE0, 2.0, 1e-8, -1.0]

ENGINES = ("tensor", "vector", "scalar", "gpsimd", "sync")
ESEM = {"tensor": "pe", "vector": "dv", "scalar": "sc", "gpsimd": "ql",
        "sync": "qs"}
NDSEM = 6
DMA_SEMS = {"gpsimd": ["qld%d" % i for i in range(NDSEM)],
            "sync": ["qsd%d" % i for i in range(NDSEM)]}
ALL_SEMS = ["pe", "dv", "sc", "ql", "qs"] + DMA_SEMS["gpsimd"] +     DMA_SEMS["sync"]


def _flat(deps):
    out = []
    for d in deps:
        if d is None:
            continue
        if isinstance(d, list) or isinstance(d, tuple):
            if len(d) == 2 and isinstance(d[0], str):
                out.append(d)
            else:
                out.extend(_flat(d))
        else:
            raise TypeError(f"bad dep {d!r}")
    return out


class Sched:

    def __init__(self, nc):
        self.nc = nc
        self.ops = {e: [] for e in ENGINES}
        self.cnt = {s: 0 for s in ALL_SEMS}
        self._rot = {"gpsimd": 0, "sync": 0}
        self._last_dma = {}

    def add(self, eng, fn, deps=(), inc=1):
        deps = tuple(_flat(deps))
        if inc == 16 and eng in DMA_SEMS:
            i = self._rot[eng] % NDSEM
            self._rot[eng] += 1
            sem = DMA_SEMS[eng][i]
            prev = self._last_dma.get(sem)
            if prev is not None:
                deps = deps + (prev,)
        else:
            sem = ESEM[eng]
        if inc:
            self.cnt[sem] += inc
        tok = (sem, self.cnt[sem]) if inc else None
        if inc == 16 and eng in DMA_SEMS:
            self._last_dma[sem] = tok
        self.ops[eng].append((fn, deps, sem, inc))
        return tok

    def barrier_token(self, eng):
        if eng in ("gpsimd", "sync"):
            toks = [(ESEM[eng], self.cnt[ESEM[eng]])]
            toks += [(s, self.cnt[s]) for s in DMA_SEMS[eng]]
            return [t for t in toks if t[1] > 0]
        return (ESEM[eng], self.cnt[ESEM[eng]])

    def all_tokens(self):
        out = []
        for e in ENGINES:
            t = self.barrier_token(e)
            out.extend(t if isinstance(t, list) else [t])
        return out

    def emit(self, block, sems):
        for e in ENGINES:
            ops = self.ops[e]
            if not ops:
                continue

            def make(section_ops):
                def body(eng):
                    seen = {}
                    for fn, deps, sem, inc in section_ops:
                        for dsem, dcnt in deps:
                            if seen.get(dsem, 0) < dcnt:
                                eng.wait_ge(sems[dsem], dcnt)
                                seen[dsem] = dcnt
                        ins = fn(eng)
                        if inc:
                            assert ins is not None, f"op on {sem} returned None"
                            ins.then_inc(sems[sem], inc)
                return body

            getattr(block, e)(make(ops))


class P:
    def __init__(self, name="mk"):
        self.nc = bass.Bass("TRN2", target_bir_lowering=False)
        self.s = Sched(self.nc)
        self._ctx = []
        self.sems = {}
        self.psS = self._psum("psS", [128, 2048], F32)
        self.psG = self._psum("psG", [128, 1024], F32)
        self.psY = self._psum("psY", [128, 512], F32)
        self.psX = self._psum("psX", [128, 512], F32)
        self.ps6 = [self.psS[:, i * 512:(i + 1) * 512] for i in range(4)] + \
                   [self.psG[:, 0:512], self.psG[:, 512:1024]]
        self.fc_sb = self.sb("fconsts", [128, len(FCONSTS)], F32)
        for i, v in enumerate(FCONSTS):
            self.nc.const_aps.register(F32, float(v), self.fc_sb.ap()[:, i:i + 1])

    def _enter(self, cm):
        h = cm.__enter__()
        if getattr(self, "_scopes", None):
            self._scopes[-1].append(cm)
        else:
            self._ctx.append(cm)
        return h

    def push_scope(self):
        self._scopes = getattr(self, "_scopes", [])
        self._scopes.append([])

    def pop_scope(self):
        for cm in reversed(self._scopes.pop()):
            cm.__exit__(None, None, None)

    def phase_barrier(self):
        toks = self.s.all_tokens()
        out = []
        for e in ENGINES:
            out.append(self.s.add(e, lambda eng: eng.nop(), deps=toks, inc=1))
        return out

    def sem(self, name):
        return self._enter(self.nc.semaphore(name))

    def sb(self, name, shape, dtype):
        self._sbser = getattr(self, "_sbser", 0) + 1
        return self._enter(self.nc.sbuf_tensor(f"{name}_{self._sbser}",
                                               shape, dtype))

    def _psum(self, name, shape, dtype=F32):
        return self._enter(self.nc.psum_tensor(name, shape, dtype))

    def dram_in(self, name, shape, dtype=F32):
        return self.nc.dram_tensor(name, shape, dtype, kind="ExternalInput").ap()

    def dram_out(self, name, shape, dtype=F32):
        return self.nc.dram_tensor(name, shape, dtype, kind="ExternalOutput").ap()

    def dram_tmp(self, name, shape, dtype=F32):
        return self.nc.dram_tensor(name, shape, dtype, kind="Internal").ap()

    def finalize(self):
        for s in ALL_SEMS:
            self.sems[s] = self.sem(s)
        toks = self.s.all_tokens()
        self.s.add("sync", lambda e: e.nop(), deps=toks, inc=1)
        with self.nc.Block() as block:
            self.s.emit(block, self.sems)
        for cm in reversed(self._ctx):
            cm.__exit__(None, None, None)
        return self.nc


def host_consts():
    c = {}
    inv_freq = 1.0 / 10000.0 ** (np.arange(0, 128, 2, dtype=np.float64) / 128.0)
    ang = np.arange(T, dtype=np.float64)[:, None] * inv_freq[None, :]
    c["ropeC"] = np.cos(ang).astype(np.float32).T.copy()
    sinc = np.sin(ang).astype(np.float32).T.copy()
    sinc[0:32] = -sinc[0:32]
    c["ropeS"] = np.ascontiguousarray(sinc)
    c["identB"] = np.eye(128, dtype=np.float32).astype(BF)
    c["identF"] = np.eye(128, dtype=np.float32)
    oh = np.zeros((16, 16 * 128), dtype=np.float32)
    for j in range(16):
        oh[j, j * 128:(j + 1) * 128] = 1.0
    c["onehot"] = oh.astype(BF)
    mm = np.zeros((4, 128, 512), dtype=np.float32)
    i = np.arange(128)[:, None]
    j = np.arange(512)[None, :]
    for r in range(4):
        mm[r] = np.where(i + 128 * r <= j, 0.0, -1e30)
    c["cmaskK"] = mm
    c["onesbf"] = np.ones((128, 1), dtype=BF)
    c["onesrowf"] = np.ones((1, 128), dtype=np.float32)
    c["onesrow1"] = np.ones((1, 128), dtype=BF)
    fc = np.zeros((128, len(FCONSTS)), dtype=np.float32)
    for i, v in enumerate(FCONSTS):
        fc[:, i] = v
    c["fconsts"] = fc
    return c


def tile_wT(wT, owid, nk=NKC):
    wT = np.ascontiguousarray(np.asarray(wT, dtype=np.float32))
    kk, oo = wT.shape
    assert kk == nk * 128 and oo % owid == 0
    nob = oo // owid
    a = wT.reshape(nk, 128, nob, owid).transpose(2, 1, 0, 3)
    return np.ascontiguousarray(a.reshape(nob, 128, nk * owid))


def pack_cols(*vecs):
    cols = []
    for v in vecs:
        v = np.asarray(v, dtype=np.float32)
        assert v.ndim == 1 and v.size % 128 == 0
        cols.append(v.reshape(-1, 128).T)
    return np.ascontiguousarray(np.concatenate(cols, axis=1))


def chunk3(ap, p=128):
    return ap.rearrange("(c p) n -> c p n", p=p)


def emit_load_xbf(p, xres, xbf, cols=None, deps=()):
    t0, w = cols if cols else (0, T)
    xv = chunk3(xres)

    def fn(g):
        return g.dma_start(
            out=xbf[:, 0:NKC * w].rearrange("p (c t) -> p c t", c=NKC),
            in_=xv[:, :, t0:t0 + w].rearrange("c p t -> p c t"))
    return p.s.add("gpsimd", fn, deps=deps, inc=16)


def emit_wcol_load(p, wT, wbuf, ostart, owid, deps=(), nk=NKC):
    wv = chunk3(wT)

    def fn(g):
        return g.dma_start(
            out=wbuf[:, 0:nk * owid].rearrange("p (c m) -> p c m", c=nk),
            in_=wv[:, :, ostart:ostart + owid].rearrange("c p o -> p c o"))
    return p.s.add("gpsimd", fn, deps=deps, inc=16)


def emit_wtile_load(p, wt_block, wbuf, width, deps=(), queue="gpsimd"):
    def fn(g):
        return g.dma_start(out=wbuf[:, 0:width], in_=wt_block)
    return p.s.add(queue, fn, deps=deps, inc=16)


def emit_mm_group(p, bank, pairs, deps=()):
    pairs = list(pairs)

    def fn(t):
        n = len(pairs)
        ins = None
        for i, (l, r) in enumerate(pairs):
            ins = t.matmul(bank, l, r, start=(i == 0), stop=(i == n - 1))
        return ins
    return p.s.add("tensor", fn, deps=deps, inc=1)


def emit_proj_phase(p, xres, qk_list, v_list, C, ropeC_sb=None, ropeS_sb=None,
                    bias_sb=None, qk_bias_cols=None, v_bias_row=None,
                    cbias_out=None, scale_for_c=1.0, deps=(), tab_tok=()):
    s = p.s
    xbf = p.sb("pj_xbf", [128, NKC * T], BF16)
    wbuf = [p.sb(f"pj_wb{i}", [128, NKC * 128], BF16) for i in range(3)]
    vwbuf = [p.sb(f"pj_vwb{i}", [128, NKC * 512], BF16) for i in range(2)]
    rt1 = [p.sb(f"pj_rt1_{i}", [64, 512], F32) for i in range(2)]
    rt2 = [p.sb(f"pj_rt2_{i}", [64, 512], F32) for i in range(2)]
    ost = [p.sb(f"pj_ost{i}", [128, 512], BF16) for i in range(6)]
    sqb = [p.sb(f"pj_sq{i}", [128, 512], BF16) for i in range(2)]
    nrm = p.sb("pj_nrm", [1, 2 * NH * NTT], F32)
    mrow = p.sb("pj_mrow", [1, 2 * NH], F32)
    crow = p.sb("pj_crow", [1, NH], F32)

    xtok = emit_load_xbf(p, xres, xbf, deps=deps)
    rope = ropeC_sb is not None

    bank_free = {}
    wb_free = {}
    ost_free = {}
    rt_free = {}
    sq_free = {}
    psX = p.psX
    nrm_free = None

    banks = p.ps6
    gidx = 0

    for pi, (tag, wT, outd) in enumerate(qk_list):
        for h in range(NH):
            wtok = emit_wtile_load(p, wT[h], wbuf[(pi * NH + h) % 3],
                                   NKC * 128,
                                   deps=(wb_free.get((pi * NH + h) % 3),))
            for tt in range(NTT):
                bk = gidx % 6
                bank = banks[bk]
                mmtok = emit_mm_group(
                    p, bank,
                    [(wbuf[(pi * NH + h) % 3][:, k * 128:(k + 1) * 128],
                      xbf[:, k * T + tt * 512: k * T + (tt + 1) * 512])
                     for k in range(NKC)],
                    deps=(xtok, wtok, bank_free.get(bk)))
                oi = gidx % 6
                ob = ost[oi]
                cs = slice(tt * 512, (tt + 1) * 512)
                if rope:
                    par = gidx % 2
                    t_m = s.add("vector", (lambda b_, pr, c_: lambda v:
                                v.tensor_tensor(rt1[pr][:, :], b_[0:64, :],
                                                ropeC_sb[:, c_], op=ALU.mult)
                                )(bank, par, cs),
                                deps=(mmtok, rt_free.get(par),
                                      ost_free.get(oi), tab_tok))
                    t_s1 = s.add("vector", (lambda b_, pr, c_: lambda v:
                                 v.tensor_tensor(rt2[pr][0:32, :], b_[32:64, :],
                                                 ropeS_sb[0:32, c_], op=ALU.mult)
                                 )(bank, par, cs), deps=(mmtok,))
                    t_s2 = s.add("vector", (lambda b_, pr, c_: lambda v:
                                 v.tensor_tensor(rt2[pr][32:64, :], b_[0:32, :],
                                                 ropeS_sb[32:64, c_], op=ALU.mult)
                                 )(bank, par, cs), deps=(mmtok,))
                    t_cp = s.add("scalar", (lambda b_, o_: lambda sc:
                                 sc.activation(out=o_[64:128, :],
                                               in_=b_[64:128, :],
                                               func=AF.Identity, bias=0.0,
                                               scale=1.0))(bank, ob),
                                 deps=(mmtok,))
                    t_add = s.add("vector", (lambda pr, o_: lambda v:
                                  v.tensor_tensor(o_[0:64, :], rt1[pr][:, :],
                                                  rt2[pr][:, :], op=ALU.add)
                                  )(par, ob), deps=(t_m, t_s1, t_s2))
                    bank_free[bk] = t_cp
                    rt_free[par] = t_add
                    done = (t_add, t_cp)
                else:
                    bias = 0.0
                    if qk_bias_cols is not None:
                        col = qk_bias_cols[tag] + h
                        bias = bias_sb[:, col:col + 1]
                    t_ev = s.add("scalar", (lambda b_, o_, bi: lambda sc:
                                 sc.activation(out=o_[:], in_=b_[:],
                                               func=AF.Identity, bias=bi,
                                               scale=1.0))(bank, ob, bias),
                                 deps=(mmtok, ost_free.get(oi)))
                    bank_free[bk] = t_ev
                    done = (t_ev,)
                t_st = s.add("sync", (lambda o_, h_, c_, od: lambda sy:
                             sy.dma_start(out=od[h_ * 128:(h_ + 1) * 128, c_],
                                          in_=o_[:]))(ob, h, cs, outd),
                             deps=done, inc=16)
                spar = gidx % 2
                t_sq = s.add("gpsimd", (lambda o_, q_: lambda g:
                             g.tensor_tensor(q_[:], o_[:], o_[:], op=ALU.mult))
                             (ob, sqb[spar]), deps=(done, sq_free.get(spar)))
                cell = (pi * NH + h) * NTT + tt
                t_nm = s.add("tensor", (lambda q_: lambda t:
                             t.matmul(psX[0:1, :], C["onesbf"][:, 0:1], q_[:],
                                      start=True, stop=True))(sqb[spar]),
                             deps=(t_sq, nrm_free))
                t_nr = s.add("vector", (lambda ce: lambda v:
                             v.reduce_max(nrm[0:1, ce:ce + 1], psX[0:1, 0:512],
                                          axis=AX.X))(cell), deps=(t_nm,))
                nrm_free = t_nr
                sq_free[spar] = t_nm
                ost_free[oi] = (t_st, t_sq)
                gidx += 1
            wb_free[(pi * NH + h) % 3] = s.barrier_token("tensor")

    t_mr = []
    for cell in range(2 * NH):
        t_mr.append(s.add("vector", (lambda ce: lambda v:
                    v.reduce_max(mrow[0:1, ce:ce + 1],
                                 nrm[0:1, ce * NTT:(ce + 1) * NTT],
                                 axis=AX.X))(cell), deps=(nrm_free,)))
    t_p = s.add("vector", lambda v: v.tensor_tensor(
        crow[0:1, :], mrow[0:1, 0:NH], mrow[0:1, NH:2 * NH], op=ALU.mult),
        deps=t_mr)
    t_ln = s.add("scalar", lambda sc: sc.activation(
        out=crow[0:1, :], in_=crow[0:1, :], func=AF.Ln, bias=1e-30, scale=1.0),
        deps=(t_p,))
    t_ex = s.add("scalar", lambda sc: sc.activation(
        out=crow[0:1, :], in_=crow[0:1, :], func=AF.Exp, bias=0.0, scale=0.5),
        deps=(t_ln,))
    t_sc = s.add("vector", lambda v: v.tensor_scalar(
        out=crow[0:1, :], in0=crow[0:1, :],
        scalar1=float(-1.04 * scale_for_c), scalar2=-0.5, op0=ALU.mult,
        op1=ALU.add), deps=(t_ex,))
    t_cb = []
    for h in range(NH):
        t_b = s.add("tensor", (lambda h_: lambda t:
                    t.matmul(psX[:, 0:1], C["onesrowf"][0:1, :],
                             crow[0:1, h_:h_ + 1], start=True, stop=True))(h),
                    deps=(t_sc, t_cb[-1] if t_cb else None))
        t_e = s.add("scalar", (lambda h_: lambda sc:
                    sc.activation(out=cbias_out[:, h_:h_ + 1],
                                  in_=psX[:, 0:1], func=AF.Identity,
                                  bias=0.0, scale=1.0))(h), deps=(t_b,))
        t_cb.append(t_e)
    cb_tok = t_cb

    v_toks = []
    for vi, (wvT, Vd) in enumerate(v_list):
        vst = [p.sb(f"pj_vst{vi}_{i}", [128, 512], BF16) for i in range(4)]
        vst_free = {}
        for ot in range(NTT):
            vwtok = emit_wtile_load(p, wvT[ot], vwbuf[ot % 2], NKC * 512,
                                    deps=(wb_free.get(("v", vi, ot % 2)),))
            for tc in range(NTC):
                gg = ot * NTC + tc
                bk = gg % 6
                bank = banks[bk]
                pairs = [(xbf[:, k * T + tc * 128: k * T + tc * 128 + 128],
                          vwbuf[ot % 2][:, k * 512:(k + 1) * 512])
                         for k in range(NKC)]
                if v_bias_row is not None:
                    pairs.append((C["onesrow1"][0:1, :],
                                  v_bias_row[0:1, ot * 512:(ot + 1) * 512]))
                mmtok = emit_mm_group(p, bank, pairs,
                                      deps=(xtok, vwtok, bank_free.get(bk)))
                ob = vst[gg % 4]
                t_ev = s.add("scalar", (lambda b_, o_: lambda sc:
                             sc.activation(out=o_[:], in_=b_[:],
                                           func=AF.Identity, bias=0.0,
                                           scale=1.0))(bank, ob),
                             deps=(mmtok, vst_free.get(gg % 4)))
                bank_free[bk] = t_ev
                last = None
                for hh in range(4):
                    h = ot * 4 + hh
                    last = s.add("sync", (lambda o_, h_, tc_, hh_, vd: lambda sy:
                                 sy.dma_start(
                                     out=vd[h_, tc_ * 128:(tc_ + 1) * 128, :],
                                     in_=o_[:, hh_ * 128:(hh_ + 1) * 128]))
                                 (ob, h, tc, hh, Vd),
                                 deps=(t_ev,), inc=16)
                vst_free[gg % 4] = last
                v_toks.append(last)
            wb_free[("v", vi, ot % 2)] = s.barrier_token("tensor")
    return cb_tok


class PostLN:

    def __init__(self, p, name, xres_out, C, stats_banks, own_buffer=True):
        self.p = p
        self.name = name
        self.xres_out = xres_out
        self.C = C
        self.bk1, self.bk2 = stats_banks
        self.xafull = p.sb(f"{name}_xa", [128, NKC * 512], F32) \
            if own_buffer else None
        self.xabf = [p.sb(f"{name}_xb{i}", [128, 512], BF16) for i in range(2)]
        self.xsq = [p.sb(f"{name}_xs{i}", [128, 512], BF16) for i in range(2)]
        self.rows = p.sb(f"{name}_rows", [1, 512 * 4], F32)
        self.rbsb = p.sb(f"{name}_rb", [128, 512], F32)
        self.mbsb = p.sb(f"{name}_mb", [128, 512], F32)
        self._free = {}
        self._reset()

    def _reset(self):
        self.chunk_toks = {}
        self.stat_toks = []
        self._nchunk = 0

    def xa_slice(self, o):
        return self.xafull[:, o * 512:(o + 1) * 512]

    def xa_free(self, o):
        return self._free.get(("xa", o))

    def add_chunk(self, o, xa_tok, stats_dep=None, xa_ap=None):
        p, s = self.p, self.p.s
        if xa_ap is None:
            xa_ap = self.xa_slice(o)
        self.cur_ap = getattr(self, "cur_ap", {})
        self.cur_ap[o] = xa_ap
        par = self._nchunk % 2
        xb, xq = self.xabf[par], self.xsq[par]
        t_c = s.add("vector", (lambda a_, b_: lambda v:
                    v.tensor_copy(b_[:], a_))(xa_ap, xb),
                    deps=(xa_tok, self._free.get(("c", par))))
        t_q = s.add("scalar", (lambda b_, q_: lambda sc:
                    sc.activation(out=q_[:], in_=b_[:], func=AF.Square,
                                  bias=0.0, scale=1.0))(xb, xq),
                    deps=(t_c, self._free.get(("q", par))))
        first = self._nchunk == 0
        last = self._nchunk == NKC - 1
        dep1 = (t_c, stats_dep if first else None)
        t_s1 = s.add("tensor", (lambda b_, f, l: lambda t:
                     t.matmul(self.bk1[0:1, :], self.C["onesbf"][:, 0:1], b_[:],
                              start=f, stop=l))(xb, first, last), deps=dep1)
        t_s2 = s.add("tensor", (lambda q_, f, l: lambda t:
                     t.matmul(self.bk2[0:1, :], self.C["onesbf"][:, 0:1], q_[:],
                              start=f, stop=l))(xq, first, last),
                     deps=(t_q, stats_dep if first else None))
        self._free[("c", par)] = t_s1
        self._free[("q", par)] = t_s2
        self.chunk_toks[o] = xa_tok
        self.stat_toks = [t_s1, t_s2]
        self._nchunk += 1

    def finish(self, qt):
        p, s = self.p, self.p.s
        assert self._nchunk == NKC
        R = self.rows
        mrow = R[0:1, 0:512]
        e2row = R[0:1, 512:1024]
        trow = R[0:1, 1024:1536]
        vrow = R[0:1, 1536:2048]
        rrow = e2row
        mrrow = trow
        d1, d2 = self.stat_toks
        t_m = s.add("vector", lambda v: v.tensor_scalar(
            out=mrow, in0=self.bk1[0:1, :], scalar1=1.0 / D, scalar2=None,
            op0=ALU.mult), deps=(d1,))
        t_e = s.add("vector", lambda v: v.tensor_scalar(
            out=e2row, in0=self.bk2[0:1, :], scalar1=1.0 / D, scalar2=None,
            op0=ALU.mult), deps=(d2,))
        t_sq = s.add("vector", lambda v: v.tensor_tensor(
            trow, mrow, mrow, op=ALU.mult), deps=(t_m,))
        t_v = s.add("vector", lambda v: v.tensor_tensor(
            vrow, e2row, trow, op=ALU.subtract), deps=(t_e, t_sq))
        t_ln = s.add("scalar", lambda sc: sc.activation(
            out=trow, in_=vrow, func=AF.Ln, bias=EPS, scale=1.0), deps=(t_v,))
        t_r = s.add("scalar", lambda sc: sc.activation(
            out=rrow, in_=trow, func=AF.Exp, bias=0.0, scale=-0.5), deps=(t_ln,))
        t_mr = s.add("vector", lambda v: v.tensor_tensor(
            mrrow, mrow, rrow, op=ALU.mult), deps=(t_r, t_m))
        t_rb = s.add("tensor", lambda t: t.matmul(
            self.bk1[:, :], self.C["onesrowf"][0:1, :], rrow,
            start=True, stop=True), deps=(t_r, d1, d2))
        t_mb = s.add("tensor", lambda t: t.matmul(
            self.bk2[:, :], self.C["onesrowf"][0:1, :], mrrow,
            start=True, stop=True), deps=(t_mr,))
        t_rb = s.add("scalar", lambda sc: sc.activation(
            out=self.rbsb[:], in_=self.bk1[:, :], func=AF.Identity,
            bias=0.0, scale=1.0), deps=(t_rb, self._free.get("rbsb")))
        t_mb = s.add("scalar", lambda sc: sc.activation(
            out=self.mbsb[:], in_=self.bk2[:, :], func=AF.Identity,
            bias=0.0, scale=1.0), deps=(t_mb, self._free.get("mbsb")))
        last = None
        for o in range(NKC):
            xa_ap = self.cur_ap[o]
            t_1 = s.add("vector", (lambda a_: lambda v:
                        v.tensor_tensor(a_, a_, self.rbsb[:], op=ALU.mult))
                        (xa_ap), deps=(t_rb, self.chunk_toks[o]))
            t_2 = s.add("vector", (lambda a_: lambda v:
                        v.tensor_tensor(a_, a_, self.mbsb[:],
                                        op=ALU.subtract))(xa_ap),
                        deps=(t_mb, t_1))
            t_st = s.add("sync", (lambda o_, a_, q_: lambda sy:
                         sy.dma_start(
                             out=self.xres_out[o_ * 128:(o_ + 1) * 128,
                                               q_ * 512:(q_ + 1) * 512],
                             in_=a_))(o, xa_ap, qt),
                         deps=(t_2,), inc=16)
            self._free[("xa", o)] = t_st
            last = t_st
        if getattr(self.p, "dbg_rows", None) is not None and qt == 0:
            s.add("sync", lambda sy: sy.dma_start(
                out=self.p.dbg_rows, in_=R[0:1, :]), deps=(t_mr, t_m, t_e, t_v),
                inc=16)
        vb = self.p.s.barrier_token("vector")
        self._free["rbsb"] = vb
        self._free["mbsb"] = vb
        stats_free = (t_rb, t_mb)
        norm_done = last
        self._reset()
        return norm_done, stats_free


def emit_attn_phase(p, Qd, Kd, Vd, woT, xprev_dram, xres_out, causal,
                    scale, C, cbias, cb_tok, bo_col=None, bias_sb=None,
                    cmask_dram=None, deps=()):
    s = p.s
    psS = p.psS
    psO = p.psG[:, 0:512]
    psA_ = [p.psG[:, 0:512], p.psG[:, 512:1024]]
    psX = p.psX
    psY = p.psY

    qtile = [p.sb(f"at_q{i}", [128, 512], BF16) for i in range(2)]
    ktile = [p.sb(f"at_k{i}", [128, T], BF16) for i in range(2)]
    vtile = [p.sb(f"at_v{i}", [128, T], BF16) for i in range(2)]
    Pt = [p.sb(f"at_p{i}", [128, 1024], BF16) for i in range(2)]
    Ofsb = p.sb("at_of", [128, NH * 512], BF16)
    rvrow = p.sb("at_rv", [1, 512], F32)
    rvb = [p.sb(f"at_rb{i}", [128, 512], F32) for i in range(2)]
    wbuf = [p.sb(f"at_wb{i}", [128, NKC * 128], BF16) for i in range(2)]
    xprev = [p.sb(f"at_xp{i}", [128, 512], F32) for i in range(2)]
    cmasks = [p.sb(f"at_cm{r}", [128, 512], F32) for r in range(4)] \
        if causal else None

    pln = PostLN(p, "at_ln", xres_out, C, (psY, psX))

    free = {}
    t_done = None
    stats_free = None
    if causal:
        cm_toks = []
        for r in range(4):
            cm_toks.append(s.add("sync", (lambda r_: lambda sy:
                           sy.dma_start(out=cmasks[r_][:],
                                        in_=cmask_dram[r_]))(r), inc=16))
        cmask_tok = tuple(cm_toks)
    else:
        cmask_tok = ()

    for qt in range(NTT):
        n_kc = (qt + 1) * 4 if causal else NKC
        KV = n_kc * 128
        for h in range(NH):
            bi = qt * NH + h
            t_q = s.add("sync", (lambda h_, q_, b_: lambda sy:
                        sy.dma_start(out=b_[:],
                                     in_=Qd[h_ * 128:(h_ + 1) * 128,
                                            q_ * 512:(q_ + 1) * 512]))
                        (h, qt, qtile[bi % 2]),
                        deps=(free.get(("q", bi % 2)), deps), inc=16)
            t_k = s.add("sync", (lambda h_, b_, w_: lambda sy:
                        sy.dma_start(out=b_[:, 0:w_],
                                     in_=Kd[h_ * 128:(h_ + 1) * 128, 0:w_]))
                        (h, ktile[bi % 2], KV),
                        deps=(free.get(("k", bi % 2)), deps), inc=16)
            t_v = s.add("sync", (lambda h_, b_, w_, nk_: lambda sy:
                        sy.dma_start(
                            out=b_[:, 0:w_].rearrange("p (c d) -> p c d", d=128),
                            in_=Vd[h_, 0:w_, :].rearrange("(c p) d -> p c d",
                                                          p=128)))
                        (h, vtile[bi % 2], KV, n_kc),
                        deps=(free.get(("v", bi % 2)), deps), inc=16)
            qb, kb, vb = qtile[bi % 2], ktile[bi % 2], vtile[bi % 2]
            e_toks = {}

            t_pv_last = None
            t_sm_last = None

            def emit_s(kc):
                bk = kc % 4
                bank = psS[:, bk * 512:(bk + 1) * 512]
                t_s = s.add("tensor", (lambda b_, kb_, qb_, kc_: lambda t:
                            t.matmul(b_, kb_[:, kc_ * 128:(kc_ + 1) * 128],
                                     qb_[:], start=True, stop=True))
                            (bank, kb, qb, kc),
                            deps=(t_q, t_k, free.get(("S", bk))))
                if causal and (kc >= n_kc - 4):
                    r = kc - (n_kc - 4)
                    t_s = s.add("vector", (lambda b_, r_: lambda v:
                                v.tensor_tensor(b_, b_, cmasks[r_][:],
                                                op=ALU.add))(bank, r),
                                deps=(t_s, cmask_tok))
                e_toks[("s", kc)] = t_s
                if kc % 2 == 1:
                    pr = (kc // 2) % 2
                    pb = psS[:, (bk - 1) * 512:(bk + 1) * 512]
                    pt = Pt[pr]
                    t_e = s.add("scalar", (lambda b_, p_, h_: lambda sc:
                                sc.activation(out=p_[0:128, 0:1024], in_=b_,
                                              func=AF.Exp,
                                              bias=cbias[:, h_:h_ + 1],
                                              scale=float(scale)))(pb, pt, h),
                                deps=(e_toks[("s", kc - 1)], t_s, cb_tok,
                                      free.get(("P", pr))))
                    free[("S", bk)] = t_e
                    free[("S", bk - 1)] = t_e
                    e_toks[kc] = t_e
                    e_toks[kc - 1] = t_e

            for kc in range(min(4, n_kc)):
                emit_s(kc)
            for kc in range(n_kc):
                if kc + 4 < n_kc:
                    emit_s(kc + 4)
                pt = Pt[(kc // 2) % 2][:, (kc % 2) * 512:(kc % 2 + 1) * 512]
                t_e = e_toks[kc]
                t_pv = s.add("tensor", (lambda kc_, f, l, vb_, p_: lambda t:
                             t.matmul(psO, vb_[:, kc_ * 128:(kc_ + 1) * 128],
                                      p_, start=f, stop=l))
                             (kc, kc == 0, kc == n_kc - 1, vb, pt),
                             deps=(t_e, t_v,
                                   free.get("O") if kc == 0 else None))
                t_sm = s.add("tensor", (lambda f, l, p_: lambda t:
                             t.matmul(psX[0:1, :], C["onesbf"][:, 0:1],
                                      p_, start=f, stop=l))
                             (kc == 0, kc == n_kc - 1, pt),
                             deps=(t_e, free.get("N") if kc == 0 else None))
                if kc % 2 == 1:
                    free[("P", (kc // 2) % 2)] = \
                        (t_pv, t_sm, free.get(("Ph", (kc // 2) % 2)))
                else:
                    free[("Ph", (kc // 2) % 2)] = (t_pv, t_sm)
                t_pv_last, t_sm_last = t_pv, t_sm
            t_rv = s.add("vector", lambda v: v.reciprocal(
                rvrow[0:1, :], psX[0:1, :]),
                deps=(t_sm_last, free.get("rvrow")))
            free["N"] = t_rv
            t_rb = s.add("tensor", lambda t: t.matmul(
                psY[:, :], C["onesrowf"][0:1, :], rvrow[0:1, :],
                start=True, stop=True),
                deps=(t_rv, free.get("rvbank")))
            t_re = s.add("vector", (lambda b_: lambda v:
                         v.tensor_copy(b_[:], psY[:, :]))
                         (rvb[bi % 2]), deps=(t_rb, free.get(("rvb", bi % 2))))
            free["rvbank"] = t_re
            free["rvrow"] = t_rb
            t_oe = s.add("vector", (lambda h_, b_: lambda v:
                         v.tensor_tensor(Ofsb[:, h_ * 512:(h_ + 1) * 512],
                                         psO, b_[:], op=ALU.mult))
                         (h, rvb[bi % 2]),
                         deps=(t_pv_last, t_re, free.get("ofr")))
            free[("rvb", bi % 2)] = t_oe
            free["O"] = t_oe
            free[("q", bi % 2)] = s.barrier_token("tensor")
            free[("k", bi % 2)] = s.barrier_token("tensor")
            free[("v", bi % 2)] = t_pv_last
        free["Olast"] = free["O"]

        for o in range(NKC):
            t_w = emit_wtile_load(p, woT[o], wbuf[o % 2], NKC * 128,
                                  deps=(free.get(("wo", o % 2)),))
            t_xp = s.add("sync", (lambda o_, q_, b_: lambda sy:
                         sy.dma_start(out=b_[:],
                                      in_=xprev_dram[o_ * 128:(o_ + 1) * 128,
                                                     q_ * 512:(q_ + 1) * 512]))
                         (o, qt, xprev[o % 2]),
                         deps=(free.get(("xp", o % 2)),), inc=16)
            psA = psA_[o % 2]
            t_a = emit_mm_group(
                p, psA,
                [(wbuf[o % 2][:, k * 128:(k + 1) * 128],
                  Ofsb[:, k * 512:(k + 1) * 512]) for k in range(NKC)],
                deps=(t_w, free["O"], free.get(("A", o % 2)),
                      free.get("Olast")))
            free[("wo", o % 2)] = t_a
            free["ofr"] = t_a
            xa = pln.xa_slice(o)
            if bo_col is not None:
                t_xa = s.add("vector", (lambda o_, xa_, xp_, pa_: lambda v:
                             v.scalar_tensor_tensor(
                                 out=xa_, in0=pa_,
                                 scalar=bias_sb[:, bo_col + o_:bo_col + o_ + 1],
                                 in1=xp_[:], op0=ALU.add, op1=ALU.add))
                             (o, xa, xprev[o % 2], psA),
                             deps=(t_a, t_xp, pln.xa_free(o)))
            else:
                t_xa = s.add("vector", (lambda xa_, xp_, pa_: lambda v:
                             v.tensor_tensor(xa_, pa_, xp_[:], op=ALU.add))
                             (xa, xprev[o % 2], psA),
                             deps=(t_a, t_xp, pln.xa_free(o)))
            free[("A", o % 2)] = t_xa
            free[("xp", o % 2)] = t_xa
            sdep = (stats_free, free.get("rvbank"), free.get("N"))
            pln.add_chunk(o, t_xa, stats_dep=sdep)
        free["O"] = free.get(("A", 0))
        t_done, stats_free = pln.finish(qt)
        free["rvbank"] = (free.get("rvbank"), stats_free[0])
        free["N"] = (free.get("N"), stats_free[1])
    return t_done


def load_consts(p, want=("onesbf", "onesrowf")):
    shapes = {
        "onesbf": ([128, 1], BF16),
        "onesrowf": ([1, 128], F32),
        "onesrow1": ([1, 128], BF16),
        "ropeC": ([64, T], F32),
        "ropeS": ([64, T], F32),
        "identB": ([128, 128], BF16),
        "identF": ([128, 128], F32),
        "onehot": ([16, 16 * 128], BF16),
    }
    C = {}
    toks = []
    fcd = p.dram_in("c_fconsts", [128, len(FCONSTS)], F32)
    toks.append(p.s.add("sync", lambda sy: sy.dma_start(out=p.fc_sb[:],
                                                        in_=fcd), inc=16))
    for name in want:
        shape, dt = shapes[name]
        d = p.dram_in("c_" + name, shape, dt)
        sbh = p.sb("cs_" + name, shape, dt)
        toks.append(p.s.add("sync", (lambda d_, s_: lambda sy:
                    sy.dma_start(out=s_[:], in_=d_))(d, sbh), inc=16))
        C[name] = sbh
    for e in ENGINES:
        p.s.add(e, lambda eng: eng.nop(), deps=toks, inc=1)
    return C


def const_in_maps(consts, want):
    m = {("c_" + n): consts[n] for n in want}
    m["c_fconsts"] = consts["fconsts"]
    return m


P1_CONSTS = ("onesbf", "onesrowf", "ropeC", "ropeS")


def build_l0_proj():
    p = P()
    xT = p.dram_in("xT", [D, T])
    wqT = p.dram_in("wqT", [16, 128, D])
    wkT = p.dram_in("wkT", [16, 128, D])
    wvT = p.dram_in("wvT", [4, 128, NKC * 512])
    Qd = p.dram_out("Qd", [D, T], BF16)
    Kd = p.dram_out("Kd", [D, T], BF16)
    Vd = p.dram_out("Vd", [NH, T, HD], BF16)
    CB = p.dram_out("CB", [128, NH])
    C = load_consts(p, P1_CONSTS)
    cbias = p.sb("cbias", [128, NH], F32)
    cb_tok = emit_proj_phase(
        p, xT, [("q", wqT, Qd), ("k", wkT, Kd)], [(wvT, Vd)], C,
        ropeC_sb=C["ropeC"], ropeS_sb=C["ropeS"], cbias_out=cbias,
        scale_for_c=SCALE0)
    p.s.add("sync", lambda sy: sy.dma_start(out=CB, in_=cbias[:]),
            deps=cb_tok, inc=16)
    return p.finalize()


P2_CONSTS = ("onesbf", "onesrowf")


def build_l0_attn():
    p = P()
    Qd = p.dram_in("Qd", [D, T], BF16)
    Kd = p.dram_in("Kd", [D, T], BF16)
    Vd = p.dram_in("Vd", [NH, T, HD], BF16)
    woT = p.dram_in("woT", [16, 128, D])
    xT = p.dram_in("xT", [D, T])
    CB = p.dram_in("CB", [128, NH])
    x1 = p.dram_out("x1", [D, T])
    C = load_consts(p, P2_CONSTS)
    cbias = p.sb("cbias", [128, NH], F32)
    cb_tok = p.s.add("sync", lambda sy: sy.dma_start(out=cbias[:], in_=CB),
                     inc=16)
    emit_attn_phase(p, Qd, Kd, Vd, woT, xT, x1, causal=False, scale=SCALE0,
                    C=C, cbias=cbias, cb_tok=cb_tok)
    return p.finalize()


L0_BC = {"b0a": 0, "b0b": 32, "b1a": 48, "b1b": 80}
L0_BP_W = 96


def emit_l0_ffn(p, x1, WgT, W0aT, W0bT, W1aT, W1bT, bias_sb, x2, C, deps=()):
    WaC = [p.dram_tmp(f"f_wac{i}", [32, 128, NKC * 128], BF16) for i in range(2)]
    WbC = [p.dram_tmp(f"f_wbc{i}", [16, 128, 32 * 128], BF16) for i in range(2)]
    """L0 FFN: hard-gated 2-expert MLP (gelu / relu) + residual + LN -> x2.
    Processed per 512-token tile; weights re-streamed per tile."""
    s = p.s
    psH = [p.psS[:, i * 512:(i + 1) * 512] for i in range(4)]
    psO_ = [p.psG[:, 0:512], p.psG[:, 512:1024]]
    psX = p.psX

    xbf = p.sb("f_xbf", [128, NKC * 512], BF16)
    xf32 = [p.sb(f"f_xf{i}", [128, 512], F32) for i in range(2)]
    xprev = [p.sb(f"f_xp{i}", [128, 512], F32) for i in range(2)]
    harena = p.sb("f_h", [128, 32 * 512], BF16)
    acc = p.sb("f_acc", [128, NKC * 512], F32)
    wab = [p.sb(f"f_wa{i}", [128, NKC * 128], BF16) for i in range(2)]
    wbb = [p.sb(f"f_wb{i}", [128, 32 * 128], BF16) for i in range(2)]
    wg = p.sb("f_wg", [128, 32], F32)
    wgd = p.sb("f_wgd", [128, 16], F32)
    mrow = p.sb("f_mrow", [1, 512], F32)
    mb = [p.sb(f"f_mb{i}", [128, 512], F32) for i in range(2)]
    m1b = [p.sb(f"f_m1b{i}", [128, 512], F32) for i in range(2)]
    htmp = [p.sb(f"f_ht{i}", [128, 512], F32) for i in range(2)]
    bd = p.sb("f_bd", [128, 16], F32)

    psY = p.psY
    pln = PostLN(p, "f_ln", x2, C, (psX, psY), own_buffer=False)

    t_wg = s.add("sync", lambda sy: sy.dma_start(
        out=wg[:].rearrange("p (c e) -> p c e", e=2),
        in_=chunk3(WgT).rearrange("c p e -> p c e")), deps=deps, inc=16)
    wgv = wg[:].rearrange("p (c e) -> p c e", e=2)
    t_wgd = s.add("vector", lambda v: v.tensor_tensor(
        wgd[:], wgv[:, :, 0], wgv[:, :, 1], op=ALU.subtract), deps=(t_wg,))
    t_bd = s.add("vector", lambda v: v.tensor_tensor(
        bd[:], bias_sb[:, L0_BC["b0b"]:L0_BC["b0b"] + 16],
        bias_sb[:, L0_BC["b1b"]:L0_BC["b1b"] + 16], op=ALU.subtract),
        deps=deps)

    free = {}
    stats_free = None
    t_done = None
    gate_toks = {}

    def emit_gate(tt):
        par = tt % 2
        mm_d = None
        for kc in range(NKC):
            t_x = s.add("sync", (lambda kc_, tt_, b_: lambda sy:
                        sy.dma_start(out=b_[:],
                                     in_=x1[kc_ * 128:(kc_ + 1) * 128,
                                            tt_ * 512:(tt_ + 1) * 512]))
                        (kc, tt, xf32[kc % 2]),
                        deps=(free.get(("xf", kc % 2)), deps), inc=16)
            mm_d = s.add("tensor", (lambda kc_, b_, f, l: lambda t:
                         t.matmul(psX[0:1, :], wgd[:, kc_:kc_ + 1], b_[:],
                                  start=f, stop=l))
                         (kc, xf32[kc % 2], kc == 0, kc == NKC - 1),
                         deps=(t_x, t_wgd,
                               free.get("psXg") if kc == 0 else None))
            free[("xf", kc % 2)] = mm_d
        t_m = s.add("vector", lambda v: v.tensor_scalar(
            out=mrow[0:1, :], in0=psX[0:1, :], scalar1=0.0, scalar2=None,
            op0=ALU.is_ge), deps=(mm_d, free.get("mrow")))
        t_b1 = s.add("tensor", lambda t: t.matmul(
            psX[:, :], C["onesrowf"][0:1, :], mrow[0:1, :],
            start=True, stop=True), deps=(t_m,))
        t_e1 = s.add("scalar", (lambda p_: lambda sc:
                     sc.activation(out=p_[:], in_=psX[:, :], func=AF.Identity,
                                   bias=0.0, scale=1.0))(mb[par]),
                     deps=(t_b1, free.get(("mb", par))))
        t_e2 = s.add("scalar", (lambda p_: lambda sc:
                     sc.activation(out=p_[:], in_=psX[:, :], func=AF.Identity,
                                   bias=1.0, scale=-1.0))(m1b[par]),
                     deps=(t_b1, free.get(("m1b", par))))
        free["psXg"] = (t_e1, t_e2)
        free["mrow"] = t_b1
        gate_toks[tt] = (t_e1, t_e2)

    emit_gate(0)
    for tt in range(NTT):
        par = tt % 2
        for o in range(NKC):
            t_xp = s.add("sync", (lambda o_, tt_, b_: lambda sy:
                         sy.dma_start(out=b_[:],
                                      in_=x1[o_ * 128:(o_ + 1) * 128,
                                             tt_ * 512:(tt_ + 1) * 512]))
                         (o, tt, xprev[o % 2]),
                         deps=(free.get(("xp", o % 2)),), inc=16)
            t_t1 = s.add("vector", (lambda o_, b_, p_: lambda v:
                         v.tensor_scalar(out=b_[:], in0=p_[:],
                                         scalar1=bd[:, o_:o_ + 1],
                                         scalar2=None, op0=ALU.mult))
                         (o, htmp[o % 2], mb[par]),
                         deps=(gate_toks[tt][0], t_bd,
                               free.get(("htmp", o % 2))))
            acc_sl = acc[:, o * 512:(o + 1) * 512]
            t_init = s.add("vector", (lambda o_, a_, xp_, ti_: lambda v:
                           v.scalar_tensor_tensor(
                               out=a_, in0=xp_[:],
                               scalar=bias_sb[:, L0_BC["b1b"] + o_:
                                              L0_BC["b1b"] + o_ + 1],
                               in1=ti_[:], op0=ALU.add, op1=ALU.add))
                           (o, acc_sl, xprev[o % 2], htmp[o % 2]),
                           deps=(t_xp, t_t1, free.get(("acci", o))))
            free[("xp", o % 2)] = t_init
            free[("htmp", o % 2)] = t_init
            free[("acc", o)] = t_init
        xtok = emit_load_xbf(p, x1, xbf, cols=(tt * 512, 512),
                             deps=(deps, free.get("xbf")))
        for ex in range(2):
            WaT = W0aT if ex == 0 else W1aT
            WbT = W0bT if ex == 0 else W1bT
            act = AF.Gelu if ex == 0 else AF.Relu
            bcol = L0_BC["b0a"] if ex == 0 else L0_BC["b1a"]
            mbx = mb[tt % 2] if ex == 0 else m1b[tt % 2]
            for f in range(32):
                t_w = emit_wtile_load(p, WaT[f], wab[f % 2], NKC * 128,
                                      deps=(free.get(("wa", f % 2)),))
                bank = psH[f % 4]
                mmtok = emit_mm_group(
                    p, bank,
                    [(wab[f % 2][:, k * 128:(k + 1) * 128],
                      xbf[:, k * 512:(k + 1) * 512]) for k in range(NKC)],
                    deps=(xtok, t_w, free.get(("hb", f % 4))))
                t_hv = s.add("scalar", (lambda b_, h_, f_, a_, bc_: lambda sc:
                             sc.activation(out=h_[:], in_=b_, func=a_,
                                           bias=bias_sb[:, bc_ + f_:
                                                        bc_ + f_ + 1],
                                           scale=1.0))
                             (bank, htmp[f % 2], f, act, bcol),
                             deps=(mmtok, free.get(("htmp", f % 2))))
                free[("hb", f % 4)] = t_hv
                hsl = harena[:, f * 512:(f + 1) * 512]
                t_hm = s.add("vector", (lambda h_, m_, o_: lambda v:
                             v.tensor_tensor(o_, h_[:], m_[:], op=ALU.mult))
                             (htmp[f % 2], mbx, hsl),
                             deps=(t_hv, gate_toks[tt][ex],
                                   free.get(("hs", f))))
                free[("htmp", f % 2)] = t_hm
                free[("h", f)] = t_hm
                free[("wa", f % 2)] = s.barrier_token("tensor")
            if ex == 1:
                free["xbf"] = s.barrier_token("tensor")
            if ex == 0 and tt + 1 < NTT:
                emit_gate(tt + 1)
            for o in range(NKC):
                if tt == 0:
                    t_w = emit_wtile_load(p, WbT[o], wbb[o % 2], 32 * 128,
                                          deps=(free.get(("wb", o % 2)),))
                    t_w = s.add("sync", (lambda o_, b_, ex_: lambda sy:
                                sy.dma_start(out=WbC[ex_][o_],
                                             in_=b_[:]))(o, wbb[o % 2], ex),
                                deps=(t_w,), inc=16)
                else:
                    t_w = emit_wtile_load(p, WbC[ex][o], wbb[o % 2], 32 * 128,
                                          deps=(free.get(("wb", o % 2)),),
                                          queue="sync")
                bank = psO_[o % 2]
                mmtok = emit_mm_group(
                    p, bank,
                    [(wbb[o % 2][:, k * 128:(k + 1) * 128],
                      harena[:, k * 512:(k + 1) * 512]) for k in range(32)],
                    deps=[t_w, free.get(("ob", o % 2))] +
                         [free.get(("h", k)) for k in range(32)])
                acc_sl = acc[:, o * 512:(o + 1) * 512]
                t_add = s.add("vector", (lambda a_, b_: lambda v:
                              v.tensor_tensor(a_, a_, b_, op=ALU.add))
                              (acc_sl, bank),
                              deps=(mmtok, free.get(("acc", o))))
                free[("ob", o % 2)] = t_add
                free[("acc", o)] = t_add
                free[("wb", o % 2)] = s.barrier_token("tensor")
            if getattr(p, "dbg_h0", None) is not None:
                dst = p.dbg_h0 if ex == 0 else p.dbg_h
                for f in range(32):
                    s.add("sync", (lambda f_, a_, d_, t2: lambda sy:
                          sy.dma_start(out=d_[f_ * 128:(f_ + 1) * 128,
                                              t2 * 512:(t2 + 1) * 512],
                                       in_=a_))(f,
                          harena[:, f * 512:(f + 1) * 512], dst, tt),
                          deps=(free[("h", f)],), inc=16)
            bt = s.barrier_token("tensor")
            for f in range(32):
                free[("hs", f)] = bt
        if getattr(p, "dbg_acc", None) is not None:
            for o in range(NKC):
                s.add("sync", (lambda o_, t2, a_: lambda sy:
                      sy.dma_start(out=p.dbg_acc[o_ * 128:(o_ + 1) * 128,
                                                 t2 * 512:(t2 + 1) * 512],
                                   in_=a_))(o, tt,
                      acc[:, o * 512:(o + 1) * 512]),
                      deps=(free[("acc", o)],), inc=16)
        for o in range(NKC):
            acc_sl = acc[:, o * 512:(o + 1) * 512]
            sdep = (stats_free, free.get("psXg"))
            pln.add_chunk(o, free[("acc", o)], stats_dep=sdep, xa_ap=acc_sl)
        t_done, stats_free = pln.finish(tt)
        free["psXg"] = stats_free[0]
        free["psY"] = stats_free[1]
        for o in range(NKC):
            free[("acci", o)] = pln.xa_free(o)
            free.pop(("acc", o), None)
        free[("mb", tt % 2)] = s.barrier_token("vector")
        free[("m1b", tt % 2)] = s.barrier_token("vector")
    return t_done


P3_CONSTS = ("onesbf", "onesrowf")


def build_l0_ffn(debug=False):
    p = P()
    x1 = p.dram_in("x1", [D, T])
    WgT = p.dram_in("WgT", [D, 2])
    W0aT = p.dram_in("W0aT", [32, 128, D])
    W0bT = p.dram_in("W0bT", [16, 128, 32 * 128])
    W1aT = p.dram_in("W1aT", [32, 128, D])
    W1bT = p.dram_in("W1bT", [16, 128, 32 * 128])
    BP = p.dram_in("BP", [128, L0_BP_W])
    x2 = p.dram_out("x2", [D, T])
    if debug:
        p.dbg_acc = p.dram_out("dbg_acc", [D, T])
        p.dbg_h = p.dram_out("dbg_h", [2 * D, T], BF16)
        p.dbg_rows = p.dram_out("dbg_rows", [1, 2048])
        p.dbg_h0 = p.dram_out("dbg_h0", [2 * D, T], BF16)
    C = load_consts(p, P3_CONSTS)
    bias_sb = p.sb("biasp", [128, L0_BP_W], F32)
    t_bp = p.s.add("sync", lambda sy: sy.dma_start(out=bias_sb[:], in_=BP),
                   inc=16)
    emit_l0_ffn(p, x1, WgT, W0aT, W0bT, W1aT, W1bT, bias_sb, x2, C,
                deps=(t_bp,))
    return p.finalize()


L1_BC = {"q": 0, "k": 16, "bo": 32}
L1_BP_W = 48
P4_CONSTS = ("onesbf", "onesrowf", "onesrow1")


def build_l1_proj():
    p = P()
    x2 = p.dram_in("x2", [D, T])
    WqT = p.dram_in("WqT", [16, 128, D])
    WkT = p.dram_in("WkT", [16, 128, D])
    WvT = p.dram_in("WvT", [4, 128, NKC * 512])
    BP = p.dram_in("BP", [128, L1_BP_W])
    bvd = p.dram_in("bv", [1, D])
    Qd = p.dram_out("Qd", [D, T], BF16)
    Kd = p.dram_out("Kd", [D, T], BF16)
    Vd = p.dram_out("Vd", [NH, T, HD], BF16)
    CB = p.dram_out("CB", [128, NH])
    C = load_consts(p, P4_CONSTS)
    bias_sb = p.sb("biasp", [128, L1_BP_W], F32)
    bvrow = p.sb("bvrow", [1, D], BF16)
    t_bp = p.s.add("sync", lambda sy: sy.dma_start(out=bias_sb[:], in_=BP),
                   inc=16)
    t_bv = p.s.add("gpsimd", lambda g: g.dma_start(out=bvrow[0:1, :], in_=bvd),
                   inc=16)
    cbias = p.sb("cbias", [128, NH], F32)
    for e in ENGINES:
        p.s.add(e, lambda eng: eng.nop(), deps=(t_bp, t_bv), inc=1)
    emit_proj_phase(
        p, x2, [("q", WqT, Qd), ("k", WkT, Kd)], [(WvT, Vd)], C,
        bias_sb=bias_sb, qk_bias_cols={"q": L1_BC["q"], "k": L1_BC["k"]},
        v_bias_row=bvrow, cbias_out=cbias, scale_for_c=SCALE1)
    p.s.add("sync", lambda sy: sy.dma_start(out=CB, in_=cbias[:]),
            deps=(p.s.barrier_token("scalar"),), inc=16)
    return p.finalize()


P5_CONSTS = ("onesbf", "onesrowf")


def build_l1_attn():
    p = P()
    Qd = p.dram_in("Qd", [D, T], BF16)
    Kd = p.dram_in("Kd", [D, T], BF16)
    Vd = p.dram_in("Vd", [NH, T, HD], BF16)
    WoT = p.dram_in("WoT", [16, 128, D])
    x2 = p.dram_in("x2", [D, T])
    CB = p.dram_in("CB", [128, NH])
    BP = p.dram_in("BP", [128, L1_BP_W])
    cmk = p.dram_in("cmk", [4, 128, 512])
    x3 = p.dram_out("x3", [D, T])
    C = load_consts(p, P5_CONSTS)
    bias_sb = p.sb("biasp", [128, L1_BP_W], F32)
    t_bp = p.s.add("sync", lambda sy: sy.dma_start(out=bias_sb[:], in_=BP),
                   inc=16)
    cbias = p.sb("cbias", [128, NH], F32)
    cb_tok = p.s.add("sync", lambda sy: sy.dma_start(out=cbias[:], in_=CB),
                     inc=16)
    for e in ENGINES:
        p.s.add(e, lambda eng: eng.nop(), deps=(t_bp,), inc=1)
    emit_attn_phase(p, Qd, Kd, Vd, WoT, x2, x3, causal=True, scale=SCALE1,
                    C=C, cbias=cbias, cb_tok=cb_tok, bo_col=L1_BC["bo"],
                    bias_sb=bias_sb, cmask_dram=[cmk[r] for r in range(4)])
    return p.finalize()


P6_BC = {"sb1": 0, "rb1": 22, "sb2": 198, "rbias": 230}
P6_BP_W = 231
P6_CONSTS = ("onesbf", "onesrowf", "identF", "onehot")
DFF = 1408
NFC = 11


def emit_l1_moe(p, x3, sw1, sw2, rw1, rw2, rb2, centT, bias_sb, yT, C,
                deps=()):
    s = p.s
    psH = [p.psS[:, i * 512:(i + 1) * 512] for i in range(3)]
    psT = p.psS[:, 1536:2048]
    psO_ = [p.psG[:, 0:512], p.psG[:, 512:1024]]
    psX = p.psX

    xbf = p.sb("m_xbf", [128, NKC * 1024], BF16)
    xf32 = [p.sb(f"m_xf{i}", [128, 512], F32) for i in range(2)]
    xprev = xf32
    hpair = p.sb("m_h", [128, 2 * NFC * 1024], BF16)
    acc = p.sb("m_acc", [128, NKC * 1024], F32)
    w1b = [p.sb(f"m_w1{i}", [128, NKC * 128], BF16) for i in range(2)]
    w2b = [p.sb(f"m_w2{i}", [128, NFC * 128], BF16) for i in range(4)]
    centsb = p.sb("m_cent", [128, NKC * 16], F32)
    scor = p.sb("m_scor", [16, 512], F32)
    stok = p.sb("m_stok", [128, 16], F32)
    cmpb = p.sb("m_cmp", [128, 256], BF16)
    rnk = p.sb("m_rnk", [128, 64], BF16)
    gtok = p.sb("m_gtok", [128, 16], F32)
    grow = p.sb("m_grow", [128, 2], F32)
    ge16b = [p.sb(f"m_geb{i}", [16, 1024], BF16) for i in range(2)]
    rb2b = p.sb("m_rb2", [16, 2048], BF16)
    b2s = p.sb("m_b2s", [128, 16], F32)
    gb = [[p.sb(f"m_gb{i}_{t}", [128, 512], BF16) for t in range(2)]
          for i in range(2)]
    htmp = [p.sb(f"m_ht{i}", [128, 512], BF16) for i in range(2)]

    psY = p.psY
    pln = PostLN(p, "m_ln", yT, C, (psX, psY), own_buffer=False)

    t_ct = s.add("sync", lambda sy: sy.dma_start(
        out=centsb[:].rearrange("p (c e) -> p c e", e=16),
        in_=chunk3(centT).rearrange("c p e -> p c e")), deps=deps, inc=16)
    t_r2 = s.add("gpsimd", lambda g: g.dma_start(out=rb2b[:], in_=rb2),
                 deps=deps, inc=16)
    t_b2 = s.add("vector", lambda v: v.tensor_tensor(
        b2s[:], bias_sb[:, P6_BC["sb2"]:P6_BC["sb2"] + 16],
        bias_sb[:, P6_BC["sb2"] + 16:P6_BC["sb2"] + 32], op=ALU.add),
        deps=deps)

    free = {}
    stats_free = None
    t_done = None

    def emit_router(half):
        geb = ge16b[half]
        ge_toks = []
        for tth in range(2):
            tt = half * 2 + tth
            mm_d = None
            for kc in range(NKC):
                t_x = s.add("sync", (lambda kc_, tt_, b_: lambda sy:
                            sy.dma_start(out=b_[:],
                                         in_=x3[kc_ * 128:(kc_ + 1) * 128,
                                                tt_ * 512:(tt_ + 1) * 512]))
                            (kc, tt, xf32[kc % 2]),
                            deps=(free.get(("xf", kc % 2)), deps), inc=16)
                mm_d = s.add("tensor", (lambda kc_, b_, f, l: lambda t:
                             t.matmul(psX[0:16, :],
                                      centsb[:, kc_ * 16:(kc_ + 1) * 16],
                                      b_[:], start=f, stop=l))
                             (kc, xf32[kc % 2], kc == 0, kc == NKC - 1),
                             deps=(t_x, t_ct,
                                   free.get("psXr") if kc == 0 else None))
                free[("xf", kc % 2)] = mm_d
            t_sg = s.add("scalar", lambda sc: sc.activation(
                out=scor[:], in_=psX[0:16, :], func=AF.Sigmoid, bias=0.0,
                scale=1.0), deps=(mm_d, free.get("scor")))
            free["psXr"] = t_sg
            t_sb = s.add("vector", lambda v: v.tensor_scalar(
                out=scor[:], in0=scor[:],
                scalar1=bias_sb[0:16, P6_BC["rbias"]:P6_BC["rbias"] + 1],
                scalar2=None, op0=ALU.add), deps=(t_sg, deps))
            for tc in range(4):
                t_tr = s.add("tensor", (lambda tc_: lambda t:
                             t.transpose(psT[:, 0:16],
                                         scor[0:16, tc_ * 128:(tc_ + 1) * 128],
                                         C["identF"][0:16, 0:16]))(tc),
                             deps=(t_sb, free.get("psT")))
                t_st = s.add("vector", lambda v: v.tensor_copy(
                    stok[:], psT[:, 0:16]), deps=(t_tr, free.get("stok")))
                free["psT"] = t_st
                cps = []
                for e in range(16):
                    cps.append(s.add("vector", (lambda e_: lambda v:
                               v.tensor_scalar(
                                   out=cmpb[:, e_ * 16:(e_ + 1) * 16],
                                   in0=stok[:], scalar1=stok[:, e_:e_ + 1],
                                   scalar2=None, op0=ALU.is_lt))(e),
                               deps=(t_st, free.get("cmpb"))))
                t_a1 = s.add("vector", lambda v: v.tensor_tensor(
                    rnk[:, 0:64], cmpb[:, 0:64], cmpb[:, 64:128], op=ALU.add),
                    deps=cps[0:8])
                t_a2 = s.add("vector", lambda v: v.tensor_tensor(
                    rnk[:, 0:64], rnk[:, 0:64], cmpb[:, 128:192], op=ALU.add),
                    deps=(t_a1,) + tuple(cps[8:12]))
                t_a3 = s.add("vector", lambda v: v.tensor_tensor(
                    rnk[:, 0:64], rnk[:, 0:64], cmpb[:, 192:256], op=ALU.add),
                    deps=(t_a2,) + tuple(cps[12:16]))
                free["cmpb"] = t_a3
                t_a4 = s.add("vector", lambda v: v.tensor_tensor(
                    rnk[:, 0:32], rnk[:, 0:32], rnk[:, 32:64], op=ALU.add),
                    deps=(t_a3,))
                t_a5 = s.add("vector", lambda v: v.tensor_tensor(
                    rnk[:, 0:16], rnk[:, 0:16], rnk[:, 16:32], op=ALU.add),
                    deps=(t_a4,))
                t_k = s.add("vector", lambda v: v.tensor_scalar(
                    out=rnk[:, 16:32], in0=rnk[:, 0:16], scalar1=4.0,
                    scalar2=None, op0=ALU.is_lt), deps=(t_a5,))
                t_g0 = s.add("vector", lambda v: v.tensor_tensor(
                    gtok[:], stok[:], rnk[:, 16:32], op=ALU.mult),
                    deps=(t_k, free.get("gtok")))
                t_gs = s.add("vector", lambda v: v.reduce_sum(
                    grow[:, 0:1], gtok[:], axis=AX.X), deps=(t_g0,))
                t_gc = s.add("vector", lambda v: v.tensor_scalar(
                    out=grow[:, 0:1], in0=grow[:, 0:1], scalar1=1e-8,
                    scalar2=None, op0=ALU.max), deps=(t_gs,))
                t_gr = s.add("vector", lambda v: v.reciprocal(
                    grow[:, 1:2], grow[:, 0:1]), deps=(t_gc,))
                t_gn = s.add("vector", lambda v: v.tensor_scalar(
                    out=gtok[:], in0=gtok[:], scalar1=grow[:, 1:2],
                    scalar2=None, op0=ALU.mult), deps=(t_gr,))
                t_tb = s.add("tensor", (lambda: lambda t:
                             t.transpose(psT[0:16, 16:144], gtok[:],
                                         C["identF"][:, :]))(),
                             deps=(t_gn, free.get("psT2")))
                col = tth * 512 + tc * 128
                t_ev = s.add("scalar", (lambda c_, g_: lambda sc:
                             sc.activation(out=g_[0:16, c_:c_ + 128],
                                           in_=psT[0:16, 16:144],
                                           func=AF.Identity, bias=0.0,
                                           scale=1.0))(col, geb),
                             deps=(t_tb, free.get(("ge16", half))))
                free["psT2"] = t_ev
                free["stok"] = cps
                free["gtok"] = t_tb
                ge_toks.append(t_ev)
            free["scor"] = s.barrier_token("tensor")
        return tuple(ge_toks)

    ge_done = [emit_router(0), emit_router(1)]
    for half in range(2):
        th0 = half * 1024
        t_geb = ge_done[half]
        xtok = emit_load_xbf(p, x3, xbf, cols=(th0, 1024),
                             deps=(deps, free.get("xbf")))
        for tth in range(2):
            tt = half * 2 + tth
            for o in range(NKC):
                t_xp = s.add("sync", (lambda o_, tt_, b_: lambda sy:
                             sy.dma_start(out=b_[:],
                                          in_=x3[o_ * 128:(o_ + 1) * 128,
                                                 tt_ * 512:(tt_ + 1) * 512]))
                             (o, tt, xprev[o % 2]),
                             deps=(free.get(("xf", o % 2)),), inc=16)
                acc_sl = acc[:, o * 1024 + tth * 512:
                             o * 1024 + tth * 512 + 512]
                t_init = s.add("vector", (lambda o_, a_, xp_: lambda v:
                               v.tensor_scalar(
                                   out=a_, in0=xp_[:], scalar1=2.0,
                                   scalar2=b2s[:, o_:o_ + 1],
                                   op0=ALU.mult, op1=ALU.add))
                               (o, acc_sl, xprev[o % 2]),
                               deps=(t_xp, t_b2, free.get(("acci", o, tth))))
                free[("xf", o % 2)] = t_init
                free[("acc", o, tth)] = t_init

        for pi in range(9):
            members = []
            if pi == 0:
                members = [("s", 0), ("s", 1)]
            else:
                members = [("r", (pi - 1) * 2), ("r", (pi - 1) * 2 + 1)]
            gb_toks = {}
            for mi, (kind, j) in enumerate(members):
                if kind == "r":
                    for tth in range(2):
                        t_gb = s.add("tensor", (lambda j_, t2, g_: lambda t:
                                     t.matmul(psT[:, 0:512],
                                              C["onehot"][0:16,
                                                          j_ * 128:(j_ + 1) * 128],
                                              g_[0:16,
                                                 t2 * 512:(t2 + 1) * 512],
                                              start=True, stop=True))
                                     (j, tth, ge16b[half]),
                                     deps=(t_geb, free.get("psT2")))
                        t_ge = s.add("scalar", (lambda b_: lambda sc:
                                     sc.activation(out=b_[:], in_=psT[:, 0:512],
                                                   func=AF.Identity, bias=0.0,
                                                   scale=1.0))
                                     (gb[mi][tth]), deps=(t_gb,
                                                     free.get(("gb", mi, tth))))
                        free["psT2"] = t_ge
                        gb_toks[(mi, tth)] = t_ge
            for mi, (kind, j) in enumerate(members):
                if kind == "s":
                    w1ap = sw1[j]
                    bcol = P6_BC["sb1"] + j * NFC
                else:
                    w1ap = rw1[j]
                    bcol = P6_BC["rb1"] + j * NFC
                for f in range(NFC):
                    t_w = emit_wtile_load(p, w1ap[f], w1b[f % 2], NKC * 128,
                                          deps=(free.get(("w1", f % 2)),))
                    for tth in range(2):
                        g = f * 2 + tth
                        bank = psH[g % 3]
                        mmtok = emit_mm_group(
                            p, bank,
                            [(w1b[f % 2][:, k * 128:(k + 1) * 128],
                              xbf[:, k * 1024 + tth * 512:
                                  k * 1024 + tth * 512 + 512])
                             for k in range(NKC)],
                            deps=(xtok, t_w, free.get(("hb", g % 3))))
                        hsl = hpair[:, (mi * NFC + f) * 1024 + tth * 512:
                                    (mi * NFC + f) * 1024 + tth * 512 + 512]
                        if kind == "s":
                            t_hv = s.add("scalar",
                                         (lambda b_, o_, bc_, f_: lambda sc:
                                          sc.activation(
                                              out=o_, in_=b_, func=AF.Relu,
                                              bias=bias_sb[:, bc_ + f_:
                                                           bc_ + f_ + 1],
                                              scale=1.0))
                                         (bank, hsl, bcol, f),
                                         deps=(mmtok,
                                               free.get(("hs", mi, f, tth))))
                            free[("hb", g % 3)] = t_hv
                            t_hm = t_hv
                        else:
                            t_hv = s.add("scalar",
                                         (lambda b_, h_, bc_, f_: lambda sc:
                                          sc.activation(
                                              out=h_[:], in_=b_, func=AF.Relu,
                                              bias=bias_sb[:, bc_ + f_:
                                                           bc_ + f_ + 1],
                                              scale=1.0))
                                         (bank, htmp[g % 2], bcol, f),
                                         deps=(mmtok,
                                               free.get(("htmp", g % 2))))
                            free[("hb", g % 3)] = t_hv
                            t_hm = s.add("vector",
                                         (lambda h_, m_, o_: lambda v:
                                          v.tensor_tensor(o_, h_[:], m_[:],
                                                          op=ALU.mult))
                                         (htmp[g % 2], gb[mi][tth], hsl),
                                         deps=(t_hv, gb_toks[(mi, tth)],
                                               free.get(("hs", mi, f, tth))))
                            free[("htmp", g % 2)] = t_hm
                        free[("h", mi, f, tth)] = t_hm
                    free[("w1", f % 2)] = s.barrier_token("tensor")
            for o in range(NKC):
                wtoks = []
                for mi, (kind, j) in enumerate(members):
                    w2ap = sw2[j] if kind == "s" else rw2[j]
                    wtoks.append(emit_wtile_load(
                        p, w2ap[o], w2b[2 * (o % 2) + mi], NFC * 128,
                        deps=(free.get(("w2", 2 * (o % 2) + mi)),)))
                for tth in range(2):
                    tt = half * 2 + tth
                    bank = psO_[(o * 2 + tth) % 2]
                    pairs = []
                    for mi in range(2):
                        pairs += [(w2b[2 * (o % 2) + mi][:, k * 128:(k + 1) * 128],
                                   hpair[:, (mi * NFC + k) * 1024 + tth * 512:
                                         (mi * NFC + k) * 1024 + tth * 512 + 512])
                                  for k in range(NFC)]
                    if pi == 0:
                        pairs.append((rb2b[0:16, o * 128:(o + 1) * 128],
                                      ge16b[half][0:16,
                                            tth * 512:(tth + 1) * 512]))
                    hdeps = [free.get(("h", mi, k, tth))
                             for mi in range(2) for k in range(NFC)]
                    mmtok = emit_mm_group(
                        p, bank, pairs,
                        deps=[wtoks[0], wtoks[1], t_r2, t_geb,
                              free.get(("ob", (o * 2 + tth) % 2))] + hdeps)
                    acc_sl = acc[:, o * 1024 + tth * 512:
                                 o * 1024 + tth * 512 + 512]
                    t_add = s.add("vector", (lambda a_, b_: lambda v:
                                  v.tensor_tensor(a_, a_, b_, op=ALU.add))
                                  (acc_sl, bank),
                                  deps=(mmtok, free.get(("acc", o, tth))))
                    free[("ob", (o * 2 + tth) % 2)] = t_add
                    free[("acc", o, tth)] = t_add
                for mi in range(2):
                    free[("w2", 2 * (o % 2) + mi)] = s.barrier_token("tensor")
            bt = s.barrier_token("tensor")
            for mi in range(2):
                for f in range(NFC):
                    for tth in range(2):
                        free[("hs", mi, f, tth)] = bt
                for tth in range(2):
                    free[("gb", mi, tth)] = bt
            if pi == 8:
                free["xbf"] = s.barrier_token("tensor")

        for tth in range(2):
            tt = half * 2 + tth
            for o in range(NKC):
                acc_sl = acc[:, o * 1024 + tth * 512:
                             o * 1024 + tth * 512 + 512]
                sdep = (stats_free, free.get("psXr"))
                pln.add_chunk(o, free[("acc", o, tth)], stats_dep=sdep,
                              xa_ap=acc_sl)
            t_done, stats_free = pln.finish(tt)
            free["psXr"] = stats_free[0]
        for o in range(NKC):
            for tth in range(2):
                free[("acci", o, tth)] = pln.xa_free(o)
                free.pop(("acc", o, tth), None)
        if getattr(p, "dbg_ge", None) is not None:
            s.add("sync", (lambda h_, g_: lambda sy: sy.dma_start(
                  out=p.dbg_ge[:, h_ * 1024:(h_ + 1) * 1024], in_=g_[:]))
                  (half, ge16b[half]), deps=t_geb, inc=16)
        if getattr(p, "dbg_acc", None) is not None:
            for o in range(NKC):
                for tth in range(2):
                    t2 = half * 2 + tth
                    s.add("sync", (lambda o_, t2_, a_: lambda sy:
                          sy.dma_start(out=p.dbg_acc[o_ * 128:(o_ + 1) * 128,
                                                     t2_ * 512:(t2_ + 1) * 512],
                                       in_=a_))(o, t2,
                          acc[:, o * 1024 + tth * 512:o * 1024 + tth * 512 + 512]),
                          deps=(free.get(("acci", o, tth)),
                                s.barrier_token("vector")), inc=16)
        free["xbf"] = s.barrier_token("tensor")
        free[("ge16", half)] = s.barrier_token("tensor")
    return t_done


def build_l1_moe(debug=False):
    p = P()
    x3 = p.dram_in("x3", [D, T])
    sw1d = p.dram_in("sw1", [2, NFC, 128, D])
    sw2d = p.dram_in("sw2", [2, 16, 128, NFC * 128])
    rw1d = p.dram_in("rw1", [16, NFC, 128, D])
    rw2d = p.dram_in("rw2", [16, 16, 128, NFC * 128])
    rb2 = p.dram_in("rb2", [16, D])
    centT = p.dram_in("centT", [D, 16])
    BP = p.dram_in("BP", [128, P6_BP_W])
    yT = p.dram_out("yT", [D, T])
    if debug:
        p.dbg_ge = p.dram_out("dbg_ge", [16, T], BF16)
        p.dbg_acc = p.dram_out("dbg_acc", [D, T])
    C = load_consts(p, P6_CONSTS)
    bias_sb = p.sb("biasp", [128, P6_BP_W], F32)
    t_bp = p.s.add("sync", lambda sy: sy.dma_start(out=bias_sb[:], in_=BP),
                   inc=16)
    for e in ENGINES:
        p.s.add(e, lambda eng: eng.nop(), deps=(t_bp,), inc=1)
    emit_l1_moe(p, x3, [sw1d[i] for i in range(2)], [sw2d[i] for i in range(2)],
                [rw1d[i] for i in range(16)], [rw2d[i] for i in range(16)],
                rb2, centT, bias_sb, yT, C)
    return p.finalize()


_NC_CACHE = {}


def _get_nc(name, builder):
    if name not in _NC_CACHE:
        _NC_CACHE[name] = builder()
    return _NC_CACHE[name]


def kernel_unfused(**ins):
    consts = host_consts()
    ncores = 8
    cores = list(range(ncores))
    x = ins["L0_attn_x"]

    def ct(a):
        return np.ascontiguousarray(np.asarray(a, dtype=np.float32))

    xT = [np.ascontiguousarray(x[b].T) for b in range(ncores)]

    nc1 = _get_nc("p1", build_l0_proj)
    w1 = {"wqT": tile_wT(ins["L0_attn_wq"].T, 128),
          "wkT": tile_wT(ins["L0_attn_wk"].T, 128),
          "wvT": tile_wT(ins["L0_attn_wv"].T, 512)}
    cm1 = const_in_maps(consts, P1_CONSTS)
    r1 = run_bass_kernel_spmd(
        nc1, [dict(xT=xT[b], **w1, **cm1) for b in range(ncores)], cores)

    nc2 = _get_nc("p2", build_l0_attn)
    w2 = {"woT": tile_wT(ins["L0_attn_wo"].T, 128)}
    cm2 = const_in_maps(consts, P2_CONSTS)
    r2 = run_bass_kernel_spmd(
        nc2, [dict(Qd=r1.results[b]["Qd"], Kd=r1.results[b]["Kd"],
                   Vd=r1.results[b]["Vd"], CB=r1.results[b]["CB"],
                   xT=xT[b], **w2, **cm2) for b in range(ncores)], cores)

    nc3 = _get_nc("p3", build_l0_ffn)
    bp3 = pack_cols(ins["L0_ffn_b0a"], ins["L0_ffn_b0b"],
                    ins["L0_ffn_b1a"], ins["L0_ffn_b1b"])
    w3 = {"WgT": ct(ins["L0_ffn_Wg"].T),
          "W0aT": tile_wT(ins["L0_ffn_W0a"].T, 128),
          "W0bT": tile_wT(ins["L0_ffn_W0b"].T, 128, nk=32),
          "W1aT": tile_wT(ins["L0_ffn_W1a"].T, 128),
          "W1bT": tile_wT(ins["L0_ffn_W1b"].T, 128, nk=32), "BP": bp3}
    cm3 = const_in_maps(consts, P3_CONSTS)
    r3 = run_bass_kernel_spmd(
        nc3, [dict(x1=r2.results[b]["x1"], **w3, **cm3)
              for b in range(ncores)], cores)

    nc4 = _get_nc("p4", build_l1_proj)
    bp4 = pack_cols(ins["L1_attn_bq"], ins["L1_attn_bk"], ins["L1_attn_bo"])
    w4 = {"WqT": tile_wT(ins["L1_attn_Wq"].T, 128),
          "WkT": tile_wT(ins["L1_attn_Wk"].T, 128),
          "WvT": tile_wT(ins["L1_attn_Wv"].T, 512), "BP": bp4,
          "bv": ct(ins["L1_attn_bv"]).reshape(1, -1)}
    cm4 = const_in_maps(consts, P4_CONSTS)
    r4 = run_bass_kernel_spmd(
        nc4, [dict(x2=r3.results[b]["x2"], **w4, **cm4)
              for b in range(ncores)], cores)

    nc5 = _get_nc("p5", build_l1_attn)
    w5 = {"WoT": tile_wT(ins["L1_attn_Wo"].T, 128), "BP": bp4,
          "cmk": consts["cmaskK"]}
    cm5 = const_in_maps(consts, P5_CONSTS)
    r5 = run_bass_kernel_spmd(
        nc5, [dict(Qd=r4.results[b]["Qd"], Kd=r4.results[b]["Kd"],
                   Vd=r4.results[b]["Vd"], CB=r4.results[b]["CB"],
                   x2=r3.results[b]["x2"], **w5, **cm5)
              for b in range(ncores)], cores)

    nc6 = _get_nc("p6", build_l1_moe)
    bp6 = np.zeros((128, P6_BP_W), dtype=np.float32)
    sb1 = np.asarray(ins["L1_ffn_shared_b1"], dtype=np.float32)
    sb2 = np.asarray(ins["L1_ffn_shared_b2"], dtype=np.float32)
    rb1 = np.asarray(ins["L1_ffn_routed_b1"], dtype=np.float32)
    bp6[:, 0:22] = pack_cols(sb1[0], sb1[1])
    bp6[:, 22:198] = pack_cols(*[rb1[j] for j in range(16)])
    bp6[:, 198:230] = pack_cols(sb2[0], sb2[1])
    bp6[0:16, 230] = np.asarray(ins["L1_ffn_bias"], dtype=np.float32)
    w6 = {"sw1": np.stack([tile_wT(ins["L1_ffn_shared_w1"][n], 128)
                           for n in range(2)]),
          "sw2": np.stack([tile_wT(ins["L1_ffn_shared_w2"][n], 128, nk=NFC)
                           for n in range(2)]),
          "rw1": np.stack([tile_wT(ins["L1_ffn_routed_w1"][j], 128)
                           for j in range(16)]),
          "rw2": np.stack([tile_wT(ins["L1_ffn_routed_w2"][j], 128, nk=NFC)
                           for j in range(16)]),
          "rb2": ct(ins["L1_ffn_routed_b2"]),
          "centT": ct(ins["L1_ffn_centroids"].T), "BP": bp6}
    cm6 = const_in_maps(consts, P6_CONSTS)
    r6 = run_bass_kernel_spmd(
        nc6, [dict(x3=r5.results[b]["x3"], **w6, **cm6)
              for b in range(ncores)], cores)

    out = np.stack([r6.results[b]["yT"].T for b in range(ncores)], axis=0)
    return out.astype(np.float32)


FUSED_CONSTS = ("onesbf", "onesrowf", "onesrow1", "identF", "onehot")


def build_fused():
    p = P()
    xT = p.dram_in("xT", [D, T])
    wqT0 = p.dram_in("wqT0", [16, 128, D]); wkT0 = p.dram_in("wkT0", [16, 128, D])
    wvT0 = p.dram_in("wvT0", [4, 128, NKC * 512])
    woT0 = p.dram_in("woT0", [16, 128, D])
    WgT = p.dram_in("WgT", [D, 2])
    W0aT = p.dram_in("W0aT", [32, 128, D])
    W0bT = p.dram_in("W0bT", [16, 128, 32 * 128])
    W1aT = p.dram_in("W1aT", [32, 128, D])
    W1bT = p.dram_in("W1bT", [16, 128, 32 * 128])
    BP0 = p.dram_in("BP0", [128, L0_BP_W])
    wqT1 = p.dram_in("wqT1", [16, 128, D]); wkT1 = p.dram_in("wkT1", [16, 128, D])
    wvT1 = p.dram_in("wvT1", [4, 128, NKC * 512])
    woT1 = p.dram_in("woT1", [16, 128, D])
    BP1 = p.dram_in("BP1", [128, L1_BP_W])
    bvd = p.dram_in("bv", [1, D])
    sw1d = p.dram_in("sw1", [2, NFC, 128, D])
    sw2d = p.dram_in("sw2", [2, 16, 128, NFC * 128])
    rw1d = p.dram_in("rw1", [16, NFC, 128, D])
    rw2d = p.dram_in("rw2", [16, 16, 128, NFC * 128])
    rb2 = p.dram_in("rb2", [16, D])
    centT = p.dram_in("centT", [D, 16])
    BP6 = p.dram_in("BP6", [128, P6_BP_W])
    ropeCd = p.dram_in("c_ropeC", [64, T])
    ropeSd = p.dram_in("c_ropeS", [64, T])
    cmk = p.dram_in("cmk", [4, 128, 512])
    yT = p.dram_out("yT", [D, T])

    Qd = p.dram_tmp("Qs", [D, T], BF16)
    Kd = p.dram_tmp("Ks", [D, T], BF16)
    Vd = p.dram_tmp("Vs", [NH, T, HD], BF16)
    x1 = p.dram_tmp("x1s", [D, T])
    x2 = p.dram_tmp("x2s", [D, T])
    x3 = p.dram_tmp("x3s", [D, T])

    C = load_consts(p, FUSED_CONSTS)
    cb0 = p.sb("cb0", [128, NH], F32)
    cb1 = p.sb("cb1", [128, NH], F32)
    bias0 = p.sb("bias0", [128, L0_BP_W], F32)
    bias1 = p.sb("bias1", [128, L1_BP_W], F32)
    bias6 = p.sb("bias6", [128, P6_BP_W], F32)
    bvrow = p.sb("bvrow", [1, D], BF16)
    tb = [p.s.add("sync", (lambda d_, s_: lambda sy:
                  sy.dma_start(out=s_[:], in_=d_))(d, sbh), inc=16)
          for d, sbh in ((BP0, bias0), (BP1, bias1), (BP6, bias6))]
    tb.append(p.s.add("gpsimd", lambda g: g.dma_start(out=bvrow[0:1, :],
                                                      in_=bvd), inc=16))
    for e in ENGINES:
        p.s.add(e, lambda eng: eng.nop(), deps=tb, inc=1)

    p.push_scope()
    rC = p.sb("rC", [64, T], F32)
    rS = p.sb("rS", [64, T], F32)
    t1 = p.s.add("sync", lambda sy: sy.dma_start(out=rC[:], in_=ropeCd), inc=16)
    t2 = p.s.add("sync", lambda sy: sy.dma_start(out=rS[:], in_=ropeSd), inc=16)
    for e in ("vector",):
        p.s.add(e, lambda eng: eng.nop(), deps=(t1, t2), inc=1)
    emit_proj_phase(p, xT, [("q", wqT0, Qd), ("k", wkT0, Kd)], [(wvT0, Vd)],
                    C, ropeC_sb=rC, ropeS_sb=rS, cbias_out=cb0,
                    scale_for_c=SCALE0)
    p.phase_barrier()
    p.pop_scope()

    p.push_scope()
    cbt = p.s.barrier_token("scalar")
    emit_attn_phase(p, Qd, Kd, Vd, woT0, xT, x1, causal=False, scale=SCALE0,
                    C=C, cbias=cb0, cb_tok=cbt)
    p.phase_barrier()
    p.pop_scope()

    p.push_scope()
    emit_l0_ffn(p, x1, WgT, W0aT, W0bT, W1aT, W1bT, bias0, x2, C)
    p.phase_barrier()
    p.pop_scope()

    p.push_scope()
    emit_proj_phase(p, x2, [("q", wqT1, Qd), ("k", wkT1, Kd)], [(wvT1, Vd)],
                    C, bias_sb=bias1,
                    qk_bias_cols={"q": L1_BC["q"], "k": L1_BC["k"]},
                    v_bias_row=bvrow, cbias_out=cb1, scale_for_c=SCALE1)
    p.phase_barrier()
    p.pop_scope()

    p.push_scope()
    cbt1 = p.s.barrier_token("scalar")
    emit_attn_phase(p, Qd, Kd, Vd, woT1, x2, x3, causal=True, scale=SCALE1,
                    C=C, cbias=cb1, cb_tok=cbt1, bo_col=L1_BC["bo"],
                    bias_sb=bias1, cmask_dram=[cmk[r] for r in range(4)])
    p.phase_barrier()
    p.pop_scope()

    p.push_scope()
    emit_l1_moe(p, x3, [sw1d[i] for i in range(2)], [sw2d[i] for i in range(2)],
                [rw1d[i] for i in range(16)], [rw2d[i] for i in range(16)],
                rb2, centT, bias6, yT, C)
    p.pop_scope()
    return p.finalize()


def kernel_fused(**ins):
    consts = host_consts()
    ncores = 8
    cores = list(range(ncores))
    x = ins["L0_attn_x"]

    def ct(a):
        return np.ascontiguousarray(np.asarray(a, dtype=np.float32))

    nc = _get_nc("fused", build_fused)
    bp3 = pack_cols(ins["L0_ffn_b0a"], ins["L0_ffn_b0b"],
                    ins["L0_ffn_b1a"], ins["L0_ffn_b1b"])
    bp4 = pack_cols(ins["L1_attn_bq"], ins["L1_attn_bk"], ins["L1_attn_bo"])
    bp6 = np.zeros((128, P6_BP_W), dtype=np.float32)
    sb1 = np.asarray(ins["L1_ffn_shared_b1"], dtype=np.float32)
    sb2 = np.asarray(ins["L1_ffn_shared_b2"], dtype=np.float32)
    rb1 = np.asarray(ins["L1_ffn_routed_b1"], dtype=np.float32)
    bp6[:, 0:22] = pack_cols(sb1[0], sb1[1])
    bp6[:, 22:198] = pack_cols(*[rb1[j] for j in range(16)])
    bp6[:, 198:230] = pack_cols(sb2[0], sb2[1])
    bp6[0:16, 230] = np.asarray(ins["L1_ffn_bias"], dtype=np.float32)
    shared = {
        "wqT0": tile_wT(ins["L0_attn_wq"].T, 128),
        "wkT0": tile_wT(ins["L0_attn_wk"].T, 128),
        "wvT0": tile_wT(ins["L0_attn_wv"].T, 512),
        "woT0": tile_wT(ins["L0_attn_wo"].T, 128),
        "WgT": ct(ins["L0_ffn_Wg"].T),
        "W0aT": tile_wT(ins["L0_ffn_W0a"].T, 128),
        "W0bT": tile_wT(ins["L0_ffn_W0b"].T, 128, nk=32),
        "W1aT": tile_wT(ins["L0_ffn_W1a"].T, 128),
        "W1bT": tile_wT(ins["L0_ffn_W1b"].T, 128, nk=32),
        "BP0": bp3,
        "wqT1": tile_wT(ins["L1_attn_Wq"].T, 128),
        "wkT1": tile_wT(ins["L1_attn_Wk"].T, 128),
        "wvT1": tile_wT(ins["L1_attn_Wv"].T, 512),
        "woT1": tile_wT(ins["L1_attn_Wo"].T, 128),
        "BP1": bp4, "bv": ct(ins["L1_attn_bv"]).reshape(1, -1),
        "sw1": np.stack([tile_wT(ins["L1_ffn_shared_w1"][n], 128)
                         for n in range(2)]),
        "sw2": np.stack([tile_wT(ins["L1_ffn_shared_w2"][n], 128, nk=NFC)
                         for n in range(2)]),
        "rw1": np.stack([tile_wT(ins["L1_ffn_routed_w1"][j], 128)
                         for j in range(16)]),
        "rw2": np.stack([tile_wT(ins["L1_ffn_routed_w2"][j], 128, nk=NFC)
                         for j in range(16)]),
        "rb2": ct(ins["L1_ffn_routed_b2"]),
        "centT": ct(ins["L1_ffn_centroids"].T), "BP6": bp6,
        "c_ropeC": consts["ropeC"], "c_ropeS": consts["ropeS"],
        "cmk": consts["cmaskK"],
    }
    shared.update(const_in_maps(consts, FUSED_CONSTS))
    in_maps = [dict(xT=np.ascontiguousarray(x[b].T), **shared)
               for b in range(ncores)]
    res = run_bass_kernel_spmd(nc, in_maps, cores)
    out = np.stack([res.results[b]["yT"].T for b in range(ncores)], axis=0)
    return out.astype(np.float32)


kernel = kernel_fused


# version 09
# speedup vs baseline: 1.2462x; 1.0851x over previous
import numpy as np
import ml_dtypes
import concourse.bass as bass
import concourse.mybir as mybir
from concourse.bass_utils import run_bass_kernel_spmd

F32 = mybir.dt.float32
BF16 = mybir.dt.bfloat16
U8 = mybir.dt.uint8
AF = mybir.ActivationFunctionType
ALU = mybir.AluOpType
AX = mybir.AxisListType
BF = ml_dtypes.bfloat16

D = 2048
T = 2048
NH = 16
HD = 128
NKC = D // 128
NTT = T // 512
NTC = T // 128
EPS = 1e-5
SCALE0 = float(1.0 / np.sqrt(128.0))
SCALE1 = float(1.0 / np.sqrt(128.0))
FCONSTS = [1e-5, 1e-30, 0.5, -0.5, SCALE0, 2.0, 1e-8, -1.0]

ENGINES = ("tensor", "vector", "scalar", "gpsimd", "sync")
ESEM = {"tensor": "pe", "vector": "dv", "scalar": "sc", "gpsimd": "ql",
        "sync": "qs"}
NDSEM = 6
DMA_SEMS = {"gpsimd": ["qld%d" % i for i in range(NDSEM)],
            "sync": ["qsd%d" % i for i in range(NDSEM)]}
ALL_SEMS = ["pe", "dv", "sc", "ql", "qs"] + DMA_SEMS["gpsimd"] +     DMA_SEMS["sync"]


def _flat(deps):
    out = []
    for d in deps:
        if d is None:
            continue
        if isinstance(d, list) or isinstance(d, tuple):
            if len(d) == 2 and isinstance(d[0], str):
                out.append(d)
            else:
                out.extend(_flat(d))
        else:
            raise TypeError(f"bad dep {d!r}")
    return out


class Sched:

    def __init__(self, nc):
        self.nc = nc
        self.ops = {e: [] for e in ENGINES}
        self.cnt = {s: 0 for s in ALL_SEMS}
        self._rot = {"gpsimd": 0, "sync": 0}
        self._last_dma = {}

    def add(self, eng, fn, deps=(), inc=1):
        deps = tuple(_flat(deps))
        if inc == 16 and eng in DMA_SEMS:
            i = self._rot[eng] % NDSEM
            self._rot[eng] += 1
            sem = DMA_SEMS[eng][i]
            prev = self._last_dma.get(sem)
            if prev is not None:
                deps = deps + (prev,)
        else:
            sem = ESEM[eng]
        if inc:
            self.cnt[sem] += inc
        tok = (sem, self.cnt[sem]) if inc else None
        if inc == 16 and eng in DMA_SEMS:
            self._last_dma[sem] = tok
        self.ops[eng].append((fn, deps, sem, inc))
        return tok

    def barrier_token(self, eng):
        if eng in ("gpsimd", "sync"):
            toks = [(ESEM[eng], self.cnt[ESEM[eng]])]
            toks += [(s, self.cnt[s]) for s in DMA_SEMS[eng]]
            return [t for t in toks if t[1] > 0]
        return (ESEM[eng], self.cnt[ESEM[eng]])

    def all_tokens(self):
        out = []
        for e in ENGINES:
            t = self.barrier_token(e)
            out.extend(t if isinstance(t, list) else [t])
        return out

    def emit(self, block, sems):
        for e in ENGINES:
            ops = self.ops[e]
            if not ops:
                continue

            def make(section_ops):
                def body(eng):
                    seen = {}
                    for fn, deps, sem, inc in section_ops:
                        for dsem, dcnt in deps:
                            if seen.get(dsem, 0) < dcnt:
                                eng.wait_ge(sems[dsem], dcnt)
                                seen[dsem] = dcnt
                        ins = fn(eng)
                        if inc:
                            assert ins is not None, f"op on {sem} returned None"
                            ins.then_inc(sems[sem], inc)
                return body

            getattr(block, e)(make(ops))


class P:
    def __init__(self, name="mk"):
        self.nc = bass.Bass("TRN2", target_bir_lowering=False)
        self.s = Sched(self.nc)
        self._ctx = []
        self.sems = {}
        self.psS = self._psum("psS", [128, 2048], F32)
        self.psG = self._psum("psG", [128, 1024], F32)
        self.psY = self._psum("psY", [128, 512], F32)
        self.psX = self._psum("psX", [128, 512], F32)
        self.ps6 = [self.psS[:, i * 512:(i + 1) * 512] for i in range(4)] + \
                   [self.psG[:, 0:512], self.psG[:, 512:1024]]
        self.fc_sb = self.sb("fconsts", [128, len(FCONSTS)], F32)
        for i, v in enumerate(FCONSTS):
            self.nc.const_aps.register(F32, float(v), self.fc_sb.ap()[:, i:i + 1])

    def _enter(self, cm):
        h = cm.__enter__()
        if getattr(self, "_scopes", None):
            self._scopes[-1].append(cm)
        else:
            self._ctx.append(cm)
        return h

    def push_scope(self):
        self._scopes = getattr(self, "_scopes", [])
        self._scopes.append([])

    def pop_scope(self):
        for cm in reversed(self._scopes.pop()):
            cm.__exit__(None, None, None)

    def phase_barrier(self):
        toks = self.s.all_tokens()
        out = []
        for e in ENGINES:
            out.append(self.s.add(e, lambda eng: eng.nop(), deps=toks, inc=1))
        return out

    def sem(self, name):
        return self._enter(self.nc.semaphore(name))

    def sb(self, name, shape, dtype):
        self._sbser = getattr(self, "_sbser", 0) + 1
        return self._enter(self.nc.sbuf_tensor(f"{name}_{self._sbser}",
                                               shape, dtype))

    def _psum(self, name, shape, dtype=F32):
        return self._enter(self.nc.psum_tensor(name, shape, dtype))

    def dram_in(self, name, shape, dtype=F32):
        return self.nc.dram_tensor(name, shape, dtype, kind="ExternalInput").ap()

    def dram_out(self, name, shape, dtype=F32):
        return self.nc.dram_tensor(name, shape, dtype, kind="ExternalOutput").ap()

    def dram_tmp(self, name, shape, dtype=F32):
        return self.nc.dram_tensor(name, shape, dtype, kind="Internal").ap()

    def finalize(self):
        for s in ALL_SEMS:
            self.sems[s] = self.sem(s)
        toks = self.s.all_tokens()
        self.s.add("sync", lambda e: e.nop(), deps=toks, inc=1)
        with self.nc.Block() as block:
            self.s.emit(block, self.sems)
        for cm in reversed(self._ctx):
            cm.__exit__(None, None, None)
        return self.nc


def host_consts():
    c = {}
    inv_freq = 1.0 / 10000.0 ** (np.arange(0, 128, 2, dtype=np.float64) / 128.0)
    ang = np.arange(T, dtype=np.float64)[:, None] * inv_freq[None, :]
    c["ropeC"] = np.cos(ang).astype(np.float32).T.copy()
    sinc = np.sin(ang).astype(np.float32).T.copy()
    sinc[0:32] = -sinc[0:32]
    c["ropeS"] = np.ascontiguousarray(sinc)
    c["identB"] = np.eye(128, dtype=np.float32).astype(BF)
    c["identF"] = np.eye(128, dtype=np.float32)
    oh = np.zeros((16, 16 * 128), dtype=np.float32)
    for j in range(16):
        oh[j, j * 128:(j + 1) * 128] = 1.0
    c["onehot"] = oh.astype(BF)
    mm = np.zeros((4, 128, 512), dtype=np.float32)
    i = np.arange(128)[:, None]
    j = np.arange(512)[None, :]
    for r in range(4):
        mm[r] = np.where(i + 128 * r <= j, 0.0, -1e30)
    c["cmaskK"] = mm
    c["onesbf"] = np.ones((128, 1), dtype=BF)
    c["onesrowf"] = np.ones((1, 128), dtype=np.float32)
    c["onesrow1"] = np.ones((1, 128), dtype=BF)
    fc = np.zeros((128, len(FCONSTS)), dtype=np.float32)
    for i, v in enumerate(FCONSTS):
        fc[:, i] = v
    c["fconsts"] = fc
    return c


def tile_wT(wT, owid, nk=NKC):
    wT = np.ascontiguousarray(np.asarray(wT, dtype=np.float32))
    kk, oo = wT.shape
    assert kk == nk * 128 and oo % owid == 0
    nob = oo // owid
    a = wT.reshape(nk, 128, nob, owid).transpose(2, 1, 0, 3)
    return np.ascontiguousarray(a.reshape(nob, 128, nk * owid))


def pack_cols(*vecs):
    cols = []
    for v in vecs:
        v = np.asarray(v, dtype=np.float32)
        assert v.ndim == 1 and v.size % 128 == 0
        cols.append(v.reshape(-1, 128).T)
    return np.ascontiguousarray(np.concatenate(cols, axis=1))


def chunk3(ap, p=128):
    return ap.rearrange("(c p) n -> c p n", p=p)


def emit_load_xbf(p, xres, xbf, cols=None, deps=()):
    t0, w = cols if cols else (0, T)
    xv = chunk3(xres)

    def fn(g):
        return g.dma_start(
            out=xbf[:, 0:NKC * w].rearrange("p (c t) -> p c t", c=NKC),
            in_=xv[:, :, t0:t0 + w].rearrange("c p t -> p c t"))
    return p.s.add("gpsimd", fn, deps=deps, inc=16)


def emit_wcol_load(p, wT, wbuf, ostart, owid, deps=(), nk=NKC):
    wv = chunk3(wT)

    def fn(g):
        return g.dma_start(
            out=wbuf[:, 0:nk * owid].rearrange("p (c m) -> p c m", c=nk),
            in_=wv[:, :, ostart:ostart + owid].rearrange("c p o -> p c o"))
    return p.s.add("gpsimd", fn, deps=deps, inc=16)


def emit_wtile_load(p, wt_block, wbuf, width, deps=(), queue="gpsimd"):
    def fn(g):
        return g.dma_start(out=wbuf[:, 0:width], in_=wt_block)
    return p.s.add(queue, fn, deps=deps, inc=16)


def emit_mm_group(p, bank, pairs, deps=()):
    pairs = list(pairs)

    def fn(t):
        n = len(pairs)
        ins = None
        for i, (l, r) in enumerate(pairs):
            ins = t.matmul(bank, l, r, start=(i == 0), stop=(i == n - 1))
        return ins
    return p.s.add("tensor", fn, deps=deps, inc=1)


def emit_proj_phase(p, xres, qk_list, v_list, C, ropeC_sb=None, ropeS_sb=None,
                    bias_sb=None, qk_bias_cols=None, v_bias_row=None,
                    cbias_out=None, scale_for_c=1.0, deps=(), tab_tok=()):
    s = p.s
    xbf = p.sb("pj_xbf", [128, NKC * T], BF16)
    wbuf = [p.sb(f"pj_wb{i}", [128, NKC * 128], BF16) for i in range(3)]
    vwbuf = [p.sb(f"pj_vwb{i}", [128, NKC * 512], BF16) for i in range(2)]
    rt1 = [p.sb(f"pj_rt1_{i}", [64, 512], F32) for i in range(2)]
    rt2 = [p.sb(f"pj_rt2_{i}", [64, 512], F32) for i in range(2)]
    ost = [p.sb(f"pj_ost{i}", [128, 512], BF16) for i in range(6)]
    sqb = [p.sb(f"pj_sq{i}", [128, 512], BF16) for i in range(2)]
    nrm = p.sb("pj_nrm", [1, 2 * NH * NTT], F32)
    mrow = p.sb("pj_mrow", [1, 2 * NH], F32)
    crow = p.sb("pj_crow", [1, NH], F32)

    xtok = emit_load_xbf(p, xres, xbf, deps=deps)
    rope = ropeC_sb is not None

    bank_free = {}
    wb_free = {}
    ost_free = {}
    rt_free = {}
    sq_free = {}
    pend_nrm = []
    psX = p.psX
    nrm_free = None

    banks = p.ps6
    gidx = 0

    for pi, (tag, wT, outd) in enumerate(qk_list):
        for h in range(NH):
            wtok = emit_wtile_load(p, wT[h], wbuf[(pi * NH + h) % 3],
                                   NKC * 128,
                                   deps=(wb_free.get((pi * NH + h) % 3),))
            for tt in range(NTT):
                bk = gidx % 6
                bank = banks[bk]
                mmtok = emit_mm_group(
                    p, bank,
                    [(wbuf[(pi * NH + h) % 3][:, k * 128:(k + 1) * 128],
                      xbf[:, k * T + tt * 512: k * T + (tt + 1) * 512])
                     for k in range(NKC)],
                    deps=(xtok, wtok, bank_free.get(bk)))
                oi = gidx % 6
                ob = ost[oi]
                cs = slice(tt * 512, (tt + 1) * 512)
                if rope:
                    par = gidx % 2
                    t_m = s.add("vector", (lambda b_, pr, c_: lambda v:
                                v.tensor_tensor(rt1[pr][:, :], b_[0:64, :],
                                                ropeC_sb[:, c_], op=ALU.mult)
                                )(bank, par, cs),
                                deps=(mmtok, rt_free.get(par),
                                      ost_free.get(oi), tab_tok))
                    t_s1 = s.add("vector", (lambda b_, pr, c_: lambda v:
                                 v.tensor_tensor(rt2[pr][0:32, :], b_[32:64, :],
                                                 ropeS_sb[0:32, c_], op=ALU.mult)
                                 )(bank, par, cs), deps=(mmtok,))
                    t_s2 = s.add("vector", (lambda b_, pr, c_: lambda v:
                                 v.tensor_tensor(rt2[pr][32:64, :], b_[0:32, :],
                                                 ropeS_sb[32:64, c_], op=ALU.mult)
                                 )(bank, par, cs), deps=(mmtok,))
                    t_cp = s.add("scalar", (lambda b_, o_: lambda sc:
                                 sc.activation(out=o_[64:128, :],
                                               in_=b_[64:128, :],
                                               func=AF.Identity, bias=0.0,
                                               scale=1.0))(bank, ob),
                                 deps=(mmtok,))
                    t_add = s.add("vector", (lambda pr, o_: lambda v:
                                  v.tensor_tensor(o_[0:64, :], rt1[pr][:, :],
                                                  rt2[pr][:, :], op=ALU.add)
                                  )(par, ob), deps=(t_m, t_s1, t_s2))
                    bank_free[bk] = t_cp
                    rt_free[par] = t_add
                    done = (t_add, t_cp)
                else:
                    bias = 0.0
                    if qk_bias_cols is not None:
                        col = qk_bias_cols[tag] + h
                        bias = bias_sb[:, col:col + 1]
                    t_ev = s.add("scalar", (lambda b_, o_, bi: lambda sc:
                                 sc.activation(out=o_[:], in_=b_[:],
                                               func=AF.Identity, bias=bi,
                                               scale=1.0))(bank, ob, bias),
                                 deps=(mmtok, ost_free.get(oi)))
                    bank_free[bk] = t_ev
                    done = (t_ev,)
                t_st = s.add("sync", (lambda o_, h_, c_, od: lambda sy:
                             sy.dma_start(out=od[h_ * 128:(h_ + 1) * 128, c_],
                                          in_=o_[:]))(ob, h, cs, outd),
                             deps=done, inc=16)
                spar = gidx % 2
                t_sq = s.add("scalar", (lambda o_, q_: lambda sc:
                             sc.activation(out=q_[:], in_=o_[:],
                                           func=AF.Square, bias=0.0,
                                           scale=1.0))
                             (ob, sqb[spar]), deps=(done, sq_free.get(spar)))
                cell = (pi * NH + h) * NTT + tt
                pend_nrm.append((sqb[spar], cell, t_sq))
                if len(pend_nrm) > 1:
                    q_, ce_, tq_ = pend_nrm.pop(0)
                    t_nm = s.add("tensor", (lambda q2: lambda t:
                                 t.matmul(psX[0:1, :], C["onesbf"][:, 0:1],
                                          q2[:], start=True, stop=True))(q_),
                                 deps=(tq_, nrm_free))
                    t_nr = s.add("vector", (lambda ce2: lambda v:
                                 v.reduce_max(nrm[0:1, ce2:ce2 + 1],
                                              psX[0:1, 0:512],
                                              axis=AX.X))(ce_), deps=(t_nm,))
                    nrm_free = t_nr
                    sq_free[(0,)] = t_nm
                sq_free[spar] = sq_free.get((0,))
                ost_free[oi] = (t_st, t_sq)
                gidx += 1
            wb_free[(pi * NH + h) % 3] = s.barrier_token("tensor")

    while pend_nrm:
        q_, ce_, tq_ = pend_nrm.pop(0)
        t_nm = s.add("tensor", (lambda q2: lambda t:
                     t.matmul(psX[0:1, :], C["onesbf"][:, 0:1],
                              q2[:], start=True, stop=True))(q_),
                     deps=(tq_, nrm_free))
        t_nr = s.add("vector", (lambda ce2: lambda v:
                     v.reduce_max(nrm[0:1, ce2:ce2 + 1], psX[0:1, 0:512],
                                  axis=AX.X))(ce_), deps=(t_nm,))
        nrm_free = t_nr
    t_mr = []
    for cell in range(2 * NH):
        t_mr.append(s.add("vector", (lambda ce: lambda v:
                    v.reduce_max(mrow[0:1, ce:ce + 1],
                                 nrm[0:1, ce * NTT:(ce + 1) * NTT],
                                 axis=AX.X))(cell), deps=(nrm_free,)))
    t_p = s.add("vector", lambda v: v.tensor_tensor(
        crow[0:1, :], mrow[0:1, 0:NH], mrow[0:1, NH:2 * NH], op=ALU.mult),
        deps=t_mr)
    t_ln = s.add("scalar", lambda sc: sc.activation(
        out=crow[0:1, :], in_=crow[0:1, :], func=AF.Ln, bias=1e-30, scale=1.0),
        deps=(t_p,))
    t_ex = s.add("scalar", lambda sc: sc.activation(
        out=crow[0:1, :], in_=crow[0:1, :], func=AF.Exp, bias=0.0, scale=0.5),
        deps=(t_ln,))
    t_sc = s.add("vector", lambda v: v.tensor_scalar(
        out=crow[0:1, :], in0=crow[0:1, :],
        scalar1=float(-1.04 * scale_for_c), scalar2=-0.5, op0=ALU.mult,
        op1=ALU.add), deps=(t_ex,))
    t_cb = []
    for h in range(NH):
        t_b = s.add("tensor", (lambda h_: lambda t:
                    t.matmul(psX[:, 0:1], C["onesrowf"][0:1, :],
                             crow[0:1, h_:h_ + 1], start=True, stop=True))(h),
                    deps=(t_sc, t_cb[-1] if t_cb else None))
        t_e = s.add("scalar", (lambda h_: lambda sc:
                    sc.activation(out=cbias_out[:, h_:h_ + 1],
                                  in_=psX[:, 0:1], func=AF.Identity,
                                  bias=0.0, scale=1.0))(h), deps=(t_b,))
        t_cb.append(t_e)
    cb_tok = t_cb

    v_toks = []
    for vi, (wvT, Vd) in enumerate(v_list):
        vst = [p.sb(f"pj_vst{vi}_{i}", [128, 512], BF16) for i in range(4)]
        vst_free = {}
        for ot in range(NTT):
            vwtok = emit_wtile_load(p, wvT[ot], vwbuf[ot % 2], NKC * 512,
                                    deps=(wb_free.get(("v", vi, ot % 2)),))
            for tc in range(NTC):
                gg = ot * NTC + tc
                bk = gg % 6
                bank = banks[bk]
                pairs = [(xbf[:, k * T + tc * 128: k * T + tc * 128 + 128],
                          vwbuf[ot % 2][:, k * 512:(k + 1) * 512])
                         for k in range(NKC)]
                if v_bias_row is not None:
                    pairs.append((C["onesrow1"][0:1, :],
                                  v_bias_row[0:1, ot * 512:(ot + 1) * 512]))
                mmtok = emit_mm_group(p, bank, pairs,
                                      deps=(xtok, vwtok, bank_free.get(bk)))
                ob = vst[gg % 4]
                t_ev = s.add("scalar", (lambda b_, o_: lambda sc:
                             sc.activation(out=o_[:], in_=b_[:],
                                           func=AF.Identity, bias=0.0,
                                           scale=1.0))(bank, ob),
                             deps=(mmtok, vst_free.get(gg % 4)))
                bank_free[bk] = t_ev
                last = None
                for hh in range(4):
                    h = ot * 4 + hh
                    last = s.add("sync", (lambda o_, h_, tc_, hh_, vd: lambda sy:
                                 sy.dma_start(
                                     out=vd[h_, tc_ * 128:(tc_ + 1) * 128, :],
                                     in_=o_[:, hh_ * 128:(hh_ + 1) * 128]))
                                 (ob, h, tc, hh, Vd),
                                 deps=(t_ev,), inc=16)
                vst_free[gg % 4] = last
                v_toks.append(last)
            wb_free[("v", vi, ot % 2)] = s.barrier_token("tensor")
    return cb_tok


class PostLN:

    def __init__(self, p, name, xres_out, C, stats_banks, own_buffer=True):
        self.p = p
        self.name = name
        self.xres_out = xres_out
        self.C = C
        self.bk1, self.bk2 = stats_banks
        self.xafull = p.sb(f"{name}_xa", [128, NKC * 512], F32) \
            if own_buffer else None
        self.xabf = [p.sb(f"{name}_xb{i}", [128, 512], BF16) for i in range(2)]
        self.xsq = [p.sb(f"{name}_xs{i}", [128, 512], BF16) for i in range(2)]
        self.rows = p.sb(f"{name}_rows", [1, 512 * 4], F32)
        self.rbsb = p.sb(f"{name}_rb", [128, 512], F32)
        self.mbsb = p.sb(f"{name}_mb", [128, 512], F32)
        self._free = {}
        self._reset()

    def _reset(self):
        self.chunk_toks = {}
        self.stat_toks = []
        self._nchunk = 0

    def xa_slice(self, o):
        return self.xafull[:, o * 512:(o + 1) * 512]

    def xa_free(self, o):
        return self._free.get(("xa", o))

    def add_chunk(self, o, xa_tok, stats_dep=None, xa_ap=None):
        p, s = self.p, self.p.s
        if xa_ap is None:
            xa_ap = self.xa_slice(o)
        self.cur_ap = getattr(self, "cur_ap", {})
        self.cur_ap[o] = xa_ap
        par = self._nchunk % 2
        xb, xq = self.xabf[par], self.xsq[par]
        t_c = s.add("vector", (lambda a_, b_: lambda v:
                    v.tensor_copy(b_[:], a_))(xa_ap, xb),
                    deps=(xa_tok, self._free.get(("c", par))))
        t_q = s.add("scalar", (lambda b_, q_: lambda sc:
                    sc.activation(out=q_[:], in_=b_[:], func=AF.Square,
                                  bias=0.0, scale=1.0))(xb, xq),
                    deps=(t_c, self._free.get(("q", par))))
        first = self._nchunk == 0
        last = self._nchunk == NKC - 1
        dep1 = (t_c, stats_dep if first else None)
        t_s1 = s.add("tensor", (lambda b_, f, l: lambda t:
                     t.matmul(self.bk1[0:1, :], self.C["onesbf"][:, 0:1], b_[:],
                              start=f, stop=l))(xb, first, last), deps=dep1)
        t_s2 = s.add("tensor", (lambda q_, f, l: lambda t:
                     t.matmul(self.bk2[0:1, :], self.C["onesbf"][:, 0:1], q_[:],
                              start=f, stop=l))(xq, first, last),
                     deps=(t_q, stats_dep if first else None))
        self._free[("c", par)] = t_s1
        self._free[("q", par)] = t_s2
        self.chunk_toks[o] = xa_tok
        self.stat_toks = [t_s1, t_s2]
        self._nchunk += 1

    def finish(self, qt):
        p, s = self.p, self.p.s
        assert self._nchunk == NKC
        R = self.rows
        mrow = R[0:1, 0:512]
        e2row = R[0:1, 512:1024]
        trow = R[0:1, 1024:1536]
        vrow = R[0:1, 1536:2048]
        rrow = e2row
        mrrow = trow
        d1, d2 = self.stat_toks
        t_m = s.add("vector", lambda v: v.tensor_scalar(
            out=mrow, in0=self.bk1[0:1, :], scalar1=1.0 / D, scalar2=None,
            op0=ALU.mult), deps=(d1,))
        t_e = s.add("vector", lambda v: v.tensor_scalar(
            out=e2row, in0=self.bk2[0:1, :], scalar1=1.0 / D, scalar2=None,
            op0=ALU.mult), deps=(d2,))
        t_sq = s.add("vector", lambda v: v.tensor_tensor(
            trow, mrow, mrow, op=ALU.mult), deps=(t_m,))
        t_v = s.add("vector", lambda v: v.tensor_tensor(
            vrow, e2row, trow, op=ALU.subtract), deps=(t_e, t_sq))
        t_ln = s.add("scalar", lambda sc: sc.activation(
            out=trow, in_=vrow, func=AF.Ln, bias=EPS, scale=1.0), deps=(t_v,))
        t_r = s.add("scalar", lambda sc: sc.activation(
            out=rrow, in_=trow, func=AF.Exp, bias=0.0, scale=-0.5), deps=(t_ln,))
        t_mr = s.add("vector", lambda v: v.tensor_tensor(
            mrrow, mrow, rrow, op=ALU.mult), deps=(t_r, t_m))
        t_rb = s.add("tensor", lambda t: t.matmul(
            self.bk1[:, :], self.C["onesrowf"][0:1, :], rrow,
            start=True, stop=True), deps=(t_r, d1, d2))
        t_mb = s.add("tensor", lambda t: t.matmul(
            self.bk2[:, :], self.C["onesrowf"][0:1, :], mrrow,
            start=True, stop=True), deps=(t_mr,))
        t_rb = s.add("scalar", lambda sc: sc.activation(
            out=self.rbsb[:], in_=self.bk1[:, :], func=AF.Identity,
            bias=0.0, scale=1.0), deps=(t_rb, self._free.get("rbsb")))
        t_mb = s.add("scalar", lambda sc: sc.activation(
            out=self.mbsb[:], in_=self.bk2[:, :], func=AF.Identity,
            bias=0.0, scale=1.0), deps=(t_mb, self._free.get("mbsb")))
        last = None
        for o in range(NKC):
            xa_ap = self.cur_ap[o]
            t_1 = s.add("vector", (lambda a_: lambda v:
                        v.tensor_tensor(a_, a_, self.rbsb[:], op=ALU.mult))
                        (xa_ap), deps=(t_rb, self.chunk_toks[o]))
            t_2 = s.add("vector", (lambda a_: lambda v:
                        v.tensor_tensor(a_, a_, self.mbsb[:],
                                        op=ALU.subtract))(xa_ap),
                        deps=(t_mb, t_1))
            t_st = s.add("sync", (lambda o_, a_, q_: lambda sy:
                         sy.dma_start(
                             out=self.xres_out[o_ * 128:(o_ + 1) * 128,
                                               q_ * 512:(q_ + 1) * 512],
                             in_=a_))(o, xa_ap, qt),
                         deps=(t_2,), inc=16)
            self._free[("xa", o)] = t_st
            last = t_st
        if getattr(self.p, "dbg_rows", None) is not None and qt == 0:
            s.add("sync", lambda sy: sy.dma_start(
                out=self.p.dbg_rows, in_=R[0:1, :]), deps=(t_mr, t_m, t_e, t_v),
                inc=16)
        vb = self.p.s.barrier_token("vector")
        self._free["rbsb"] = vb
        self._free["mbsb"] = vb
        stats_free = (t_rb, t_mb)
        norm_done = last
        self._reset()
        return norm_done, stats_free


def emit_attn_phase(p, Qd, Kd, Vd, woT, xprev_dram, xres_out, causal,
                    scale, C, cbias, cb_tok, bo_col=None, bias_sb=None,
                    cmask_dram=None, deps=()):
    s = p.s
    psS = p.psS
    psO_ = [p.psG[:, 0:512], p.psG[:, 512:1024]]
    psA_ = [p.psG[:, 0:512], p.psG[:, 512:1024]]
    psX = p.psX
    psY = p.psY

    qtile = [p.sb(f"at_q{i}", [128, 512], BF16) for i in range(2)]
    ktile = [p.sb(f"at_k{i}", [128, T], BF16) for i in range(2)]
    vtile = [p.sb(f"at_v{i}", [128, T], BF16) for i in range(2)]
    Pt = [p.sb(f"at_p{i}", [128, 1024], BF16) for i in range(2)]
    Ofsb = p.sb("at_of", [128, NH * 512], BF16)
    rvrow = [p.sb(f"at_rv{i}", [1, 512], F32) for i in range(2)]
    rvb = [p.sb(f"at_rb{i}", [128, 512], F32) for i in range(2)]
    wbuf = [p.sb(f"at_wb{i}", [128, NKC * 128], BF16) for i in range(2)]
    xprev = [p.sb(f"at_xp{i}", [128, 512], F32) for i in range(2)]
    cmasks = [p.sb(f"at_cm{r}", [128, 512], F32) for r in range(4)] \
        if causal else None

    pln = PostLN(p, "at_ln", xres_out, C, (psY, psX))

    free = {}
    t_done = None
    stats_free = None
    pend_norm = []

    def flush_norm():
        t_rv, t_pv_l, bi_, h_, psO = pend_norm.pop(0)
        t_rb = s.add("tensor", (lambda r_: lambda t: t.matmul(
            psY[:, :], C["onesrowf"][0:1, :], r_[0:1, :],
            start=True, stop=True))(rvrow[bi_ % 2]),
            deps=(t_rv, free.get("rvbank")))
        free[("rvrow", bi_ % 2)] = t_rb
        t_re = s.add("vector", (lambda b_: lambda v:
                     v.tensor_copy(b_[:], psY[:, :]))
                     (rvb[bi_ % 2]), deps=(t_rb, free.get(("rvb", bi_ % 2))))
        free["rvbank"] = t_re
        free["rvrow"] = t_rb
        t_oe = s.add("vector", (lambda h2, b_, po_: lambda v:
                     v.tensor_tensor(Ofsb[:, h2 * 512:(h2 + 1) * 512],
                                     po_, b_[:], op=ALU.mult))
                     (h_, rvb[bi_ % 2], psO),
                     deps=(t_pv_l, t_re, free.get("ofr")))
        free[("rvb", bi_ % 2)] = t_oe
        free[("O", bi_ % 2)] = t_oe
        free["O"] = t_oe

    if causal:
        cm_toks = []
        for r in range(4):
            cm_toks.append(s.add("sync", (lambda r_: lambda sy:
                           sy.dma_start(out=cmasks[r_][:],
                                        in_=cmask_dram[r_]))(r), inc=16))
        cmask_tok = tuple(cm_toks)
    else:
        cmask_tok = ()

    for qt in range(NTT):
        n_kc = (qt + 1) * 4 if causal else NKC
        KV = n_kc * 128
        for h in range(NH):
            bi = qt * NH + h
            t_q = s.add("sync", (lambda h_, q_, b_: lambda sy:
                        sy.dma_start(out=b_[:],
                                     in_=Qd[h_ * 128:(h_ + 1) * 128,
                                            q_ * 512:(q_ + 1) * 512]))
                        (h, qt, qtile[bi % 2]),
                        deps=(free.get(("q", bi % 2)), deps), inc=16)
            t_k = s.add("sync", (lambda h_, b_, w_: lambda sy:
                        sy.dma_start(out=b_[:, 0:w_],
                                     in_=Kd[h_ * 128:(h_ + 1) * 128, 0:w_]))
                        (h, ktile[bi % 2], KV),
                        deps=(free.get(("k", bi % 2)), deps), inc=16)
            t_v = s.add("sync", (lambda h_, b_, w_, nk_: lambda sy:
                        sy.dma_start(
                            out=b_[:, 0:w_].rearrange("p (c d) -> p c d", d=128),
                            in_=Vd[h_, 0:w_, :].rearrange("(c p) d -> p c d",
                                                          p=128)))
                        (h, vtile[bi % 2], KV, n_kc),
                        deps=(free.get(("v", bi % 2)), deps), inc=16)
            qb, kb, vb = qtile[bi % 2], ktile[bi % 2], vtile[bi % 2]
            psO = psO_[bi % 2]
            e_toks = {}

            t_pv_last = None
            t_sm_last = None

            def emit_s(kc):
                bk = kc % 4
                bank = psS[:, bk * 512:(bk + 1) * 512]
                t_s = s.add("tensor", (lambda b_, kb_, qb_, kc_: lambda t:
                            t.matmul(b_, kb_[:, kc_ * 128:(kc_ + 1) * 128],
                                     qb_[:], start=True, stop=True))
                            (bank, kb, qb, kc),
                            deps=(t_q, t_k, free.get(("S", bk))))
                if causal and (kc >= n_kc - 4):
                    r = kc - (n_kc - 4)
                    t_s = s.add("vector", (lambda b_, r_: lambda v:
                                v.tensor_tensor(b_, b_, cmasks[r_][:],
                                                op=ALU.add))(bank, r),
                                deps=(t_s, cmask_tok))
                e_toks[("s", kc)] = t_s
                if kc % 2 == 1:
                    pr = (kc // 2) % 2
                    pb = psS[:, (bk - 1) * 512:(bk + 1) * 512]
                    pt = Pt[pr]
                    t_e = s.add("scalar", (lambda b_, p_, h_: lambda sc:
                                sc.activation(out=p_[0:128, 0:1024], in_=b_,
                                              func=AF.Exp,
                                              bias=cbias[:, h_:h_ + 1],
                                              scale=float(scale)))(pb, pt, h),
                                deps=(e_toks[("s", kc - 1)], t_s, cb_tok,
                                      free.get(("P", pr))))
                    free[("S", bk)] = t_e
                    free[("S", bk - 1)] = t_e
                    e_toks[kc] = t_e
                    e_toks[kc - 1] = t_e

            for kc in range(min(4, n_kc)):
                emit_s(kc)
            for kc in range(n_kc):
                if kc + 4 < n_kc:
                    emit_s(kc + 4)
                pt = Pt[(kc // 2) % 2][:, (kc % 2) * 512:(kc % 2 + 1) * 512]
                t_e = e_toks[kc]
                t_pv = s.add("tensor", (lambda kc_, f, l, vb_, p_, po_:
                             lambda t:
                             t.matmul(po_, vb_[:, kc_ * 128:(kc_ + 1) * 128],
                                      p_, start=f, stop=l))
                             (kc, kc == 0, kc == n_kc - 1, vb, pt, psO),
                             deps=(t_e, t_v,
                                   free.get(("O", bi % 2)) if kc == 0
                                   else None))
                t_sm = s.add("tensor", (lambda f, l, p_: lambda t:
                             t.matmul(psX[0:1, :], C["onesbf"][:, 0:1],
                                      p_, start=f, stop=l))
                             (kc == 0, kc == n_kc - 1, pt),
                             deps=(t_e, free.get("N") if kc == 0 else None))
                if kc % 2 == 1:
                    free[("P", (kc // 2) % 2)] = \
                        (t_pv, t_sm, free.get(("Ph", (kc // 2) % 2)))
                else:
                    free[("Ph", (kc // 2) % 2)] = (t_pv, t_sm)
                t_pv_last, t_sm_last = t_pv, t_sm
            t_rv = s.add("vector", (lambda r_: lambda v: v.reciprocal(
                r_[0:1, :], psX[0:1, :]))(rvrow[bi % 2]),
                deps=(t_sm_last, free.get(("rvrow", bi % 2))))
            free["N"] = t_rv
            pend_norm.append((t_rv, t_pv_last, bi, h, psO))
            if len(pend_norm) > 1:
                flush_norm()
            free[("q", bi % 2)] = s.barrier_token("tensor")
            free[("k", bi % 2)] = s.barrier_token("tensor")
            free[("v", bi % 2)] = t_pv_last
        flush_norm()
        free["Olast"] = free["O"]

        for o in range(NKC):
            t_w = emit_wtile_load(p, woT[o], wbuf[o % 2], NKC * 128,
                                  deps=(free.get(("wo", o % 2)),))
            t_xp = s.add("sync", (lambda o_, q_, b_: lambda sy:
                         sy.dma_start(out=b_[:],
                                      in_=xprev_dram[o_ * 128:(o_ + 1) * 128,
                                                     q_ * 512:(q_ + 1) * 512]))
                         (o, qt, xprev[o % 2]),
                         deps=(free.get(("xp", o % 2)),), inc=16)
            psA = psA_[o % 2]
            t_a = emit_mm_group(
                p, psA,
                [(wbuf[o % 2][:, k * 128:(k + 1) * 128],
                  Ofsb[:, k * 512:(k + 1) * 512]) for k in range(NKC)],
                deps=(t_w, free["O"], free.get(("A", o % 2)),
                      free.get("Olast")))
            free[("wo", o % 2)] = t_a
            free["ofr"] = t_a
            xa = pln.xa_slice(o)
            if bo_col is not None:
                t_xa = s.add("vector", (lambda o_, xa_, xp_, pa_: lambda v:
                             v.scalar_tensor_tensor(
                                 out=xa_, in0=pa_,
                                 scalar=bias_sb[:, bo_col + o_:bo_col + o_ + 1],
                                 in1=xp_[:], op0=ALU.add, op1=ALU.add))
                             (o, xa, xprev[o % 2], psA),
                             deps=(t_a, t_xp, pln.xa_free(o)))
            else:
                t_xa = s.add("vector", (lambda xa_, xp_, pa_: lambda v:
                             v.tensor_tensor(xa_, pa_, xp_[:], op=ALU.add))
                             (xa, xprev[o % 2], psA),
                             deps=(t_a, t_xp, pln.xa_free(o)))
            free[("A", o % 2)] = t_xa
            free[("xp", o % 2)] = t_xa
            sdep = (stats_free, free.get("rvbank"), free.get("N"))
            pln.add_chunk(o, t_xa, stats_dep=sdep)
        free["O"] = free.get(("A", 0))
        t_done, stats_free = pln.finish(qt)
        free["rvbank"] = (free.get("rvbank"), stats_free[0])
        free["N"] = (free.get("N"), stats_free[1])
    return t_done


def load_consts(p, want=("onesbf", "onesrowf")):
    shapes = {
        "onesbf": ([128, 1], BF16),
        "onesrowf": ([1, 128], F32),
        "onesrow1": ([1, 128], BF16),
        "ropeC": ([64, T], F32),
        "ropeS": ([64, T], F32),
        "identB": ([128, 128], BF16),
        "identF": ([128, 128], F32),
        "onehot": ([16, 16 * 128], BF16),
    }
    C = {}
    toks = []
    fcd = p.dram_in("c_fconsts", [128, len(FCONSTS)], F32)
    toks.append(p.s.add("sync", lambda sy: sy.dma_start(out=p.fc_sb[:],
                                                        in_=fcd), inc=16))
    for name in want:
        shape, dt = shapes[name]
        d = p.dram_in("c_" + name, shape, dt)
        sbh = p.sb("cs_" + name, shape, dt)
        toks.append(p.s.add("sync", (lambda d_, s_: lambda sy:
                    sy.dma_start(out=s_[:], in_=d_))(d, sbh), inc=16))
        C[name] = sbh
    for e in ENGINES:
        p.s.add(e, lambda eng: eng.nop(), deps=toks, inc=1)
    return C


def const_in_maps(consts, want):
    m = {("c_" + n): consts[n] for n in want}
    m["c_fconsts"] = consts["fconsts"]
    return m


P1_CONSTS = ("onesbf", "onesrowf", "ropeC", "ropeS")


def build_l0_proj():
    p = P()
    xT = p.dram_in("xT", [D, T])
    wqT = p.dram_in("wqT", [16, 128, D])
    wkT = p.dram_in("wkT", [16, 128, D])
    wvT = p.dram_in("wvT", [4, 128, NKC * 512])
    Qd = p.dram_out("Qd", [D, T], BF16)
    Kd = p.dram_out("Kd", [D, T], BF16)
    Vd = p.dram_out("Vd", [NH, T, HD], BF16)
    CB = p.dram_out("CB", [128, NH])
    C = load_consts(p, P1_CONSTS)
    cbias = p.sb("cbias", [128, NH], F32)
    cb_tok = emit_proj_phase(
        p, xT, [("q", wqT, Qd), ("k", wkT, Kd)], [(wvT, Vd)], C,
        ropeC_sb=C["ropeC"], ropeS_sb=C["ropeS"], cbias_out=cbias,
        scale_for_c=SCALE0)
    p.s.add("sync", lambda sy: sy.dma_start(out=CB, in_=cbias[:]),
            deps=cb_tok, inc=16)
    return p.finalize()


P2_CONSTS = ("onesbf", "onesrowf")


def build_l0_attn():
    p = P()
    Qd = p.dram_in("Qd", [D, T], BF16)
    Kd = p.dram_in("Kd", [D, T], BF16)
    Vd = p.dram_in("Vd", [NH, T, HD], BF16)
    woT = p.dram_in("woT", [16, 128, D])
    xT = p.dram_in("xT", [D, T])
    CB = p.dram_in("CB", [128, NH])
    x1 = p.dram_out("x1", [D, T])
    C = load_consts(p, P2_CONSTS)
    cbias = p.sb("cbias", [128, NH], F32)
    cb_tok = p.s.add("sync", lambda sy: sy.dma_start(out=cbias[:], in_=CB),
                     inc=16)
    emit_attn_phase(p, Qd, Kd, Vd, woT, xT, x1, causal=False, scale=SCALE0,
                    C=C, cbias=cbias, cb_tok=cb_tok)
    return p.finalize()


L0_BC = {"b0a": 0, "b0b": 32, "b1a": 48, "b1b": 80}
L0_BP_W = 96


def emit_l0_ffn(p, x1, WgT, W0aT, W0bT, W1aT, W1bT, bias_sb, x2, C, deps=()):
    WaC = [p.dram_tmp(f"f_wac{i}", [32, 128, NKC * 128], BF16) for i in range(2)]
    WbC = [p.dram_tmp(f"f_wbc{i}", [16, 128, 32 * 128], BF16) for i in range(2)]
    """L0 FFN: hard-gated 2-expert MLP (gelu / relu) + residual + LN -> x2.
    Processed per 512-token tile; weights re-streamed per tile."""
    s = p.s
    psH = [p.psS[:, i * 512:(i + 1) * 512] for i in range(4)]
    psO_ = [p.psG[:, 0:512], p.psG[:, 512:1024]]
    psX = p.psX

    xbf = p.sb("f_xbf", [128, NKC * 512], BF16)
    xf32 = [p.sb(f"f_xf{i}", [128, 512], F32) for i in range(2)]
    xprev = [p.sb(f"f_xp{i}", [128, 512], F32) for i in range(2)]
    harena = p.sb("f_h", [128, 32 * 512], BF16)
    acc = p.sb("f_acc", [128, NKC * 512], F32)
    wab = [p.sb(f"f_wa{i}", [128, NKC * 128], BF16) for i in range(2)]
    wbb = [p.sb(f"f_wb{i}", [128, 32 * 128], BF16) for i in range(2)]
    wg = p.sb("f_wg", [128, 32], F32)
    wgd = p.sb("f_wgd", [128, 16], F32)
    mrow = p.sb("f_mrow", [1, 512], F32)
    mb = [p.sb(f"f_mb{i}", [128, 512], F32) for i in range(2)]
    m1b = [p.sb(f"f_m1b{i}", [128, 512], F32) for i in range(2)]
    htmp = [p.sb(f"f_ht{i}", [128, 512], F32) for i in range(2)]
    bd = p.sb("f_bd", [128, 16], F32)

    psY = p.psY
    pln = PostLN(p, "f_ln", x2, C, (psX, psY), own_buffer=False)

    t_wg = s.add("sync", lambda sy: sy.dma_start(
        out=wg[:].rearrange("p (c e) -> p c e", e=2),
        in_=chunk3(WgT).rearrange("c p e -> p c e")), deps=deps, inc=16)
    wgv = wg[:].rearrange("p (c e) -> p c e", e=2)
    t_wgd = s.add("vector", lambda v: v.tensor_tensor(
        wgd[:], wgv[:, :, 0], wgv[:, :, 1], op=ALU.subtract), deps=(t_wg,))
    t_bd = s.add("vector", lambda v: v.tensor_tensor(
        bd[:], bias_sb[:, L0_BC["b0b"]:L0_BC["b0b"] + 16],
        bias_sb[:, L0_BC["b1b"]:L0_BC["b1b"] + 16], op=ALU.subtract),
        deps=deps)

    free = {}
    stats_free = None
    t_done = None
    gate_toks = {}

    def emit_gate(tt):
        par = tt % 2
        mm_d = None
        for kc in range(NKC):
            t_x = s.add("sync", (lambda kc_, tt_, b_: lambda sy:
                        sy.dma_start(out=b_[:],
                                     in_=x1[kc_ * 128:(kc_ + 1) * 128,
                                            tt_ * 512:(tt_ + 1) * 512]))
                        (kc, tt, xf32[kc % 2]),
                        deps=(free.get(("xf", kc % 2)), deps), inc=16)
            mm_d = s.add("tensor", (lambda kc_, b_, f, l: lambda t:
                         t.matmul(psX[0:1, :], wgd[:, kc_:kc_ + 1], b_[:],
                                  start=f, stop=l))
                         (kc, xf32[kc % 2], kc == 0, kc == NKC - 1),
                         deps=(t_x, t_wgd,
                               free.get("psXg") if kc == 0 else None))
            free[("xf", kc % 2)] = mm_d
        t_m = s.add("vector", lambda v: v.tensor_scalar(
            out=mrow[0:1, :], in0=psX[0:1, :], scalar1=0.0, scalar2=None,
            op0=ALU.is_ge), deps=(mm_d, free.get("mrow")))
        t_b1 = s.add("tensor", lambda t: t.matmul(
            psX[:, :], C["onesrowf"][0:1, :], mrow[0:1, :],
            start=True, stop=True), deps=(t_m,))
        t_e1 = s.add("scalar", (lambda p_: lambda sc:
                     sc.activation(out=p_[:], in_=psX[:, :], func=AF.Identity,
                                   bias=0.0, scale=1.0))(mb[par]),
                     deps=(t_b1, free.get(("mb", par))))
        t_e2 = s.add("scalar", (lambda p_: lambda sc:
                     sc.activation(out=p_[:], in_=psX[:, :], func=AF.Identity,
                                   bias=1.0, scale=-1.0))(m1b[par]),
                     deps=(t_b1, free.get(("m1b", par))))
        free["psXg"] = (t_e1, t_e2)
        free["mrow"] = t_b1
        gate_toks[tt] = (t_e1, t_e2)

    emit_gate(0)
    for tt in range(NTT):
        par = tt % 2
        for o in range(NKC):
            t_xp = s.add("sync", (lambda o_, tt_, b_: lambda sy:
                         sy.dma_start(out=b_[:],
                                      in_=x1[o_ * 128:(o_ + 1) * 128,
                                             tt_ * 512:(tt_ + 1) * 512]))
                         (o, tt, xprev[o % 2]),
                         deps=(free.get(("xp", o % 2)),), inc=16)
            t_t1 = s.add("vector", (lambda o_, b_, p_: lambda v:
                         v.tensor_scalar(out=b_[:], in0=p_[:],
                                         scalar1=bd[:, o_:o_ + 1],
                                         scalar2=None, op0=ALU.mult))
                         (o, htmp[o % 2], mb[par]),
                         deps=(gate_toks[tt][0], t_bd,
                               free.get(("htmp", o % 2))))
            acc_sl = acc[:, o * 512:(o + 1) * 512]
            t_init = s.add("vector", (lambda o_, a_, xp_, ti_: lambda v:
                           v.scalar_tensor_tensor(
                               out=a_, in0=xp_[:],
                               scalar=bias_sb[:, L0_BC["b1b"] + o_:
                                              L0_BC["b1b"] + o_ + 1],
                               in1=ti_[:], op0=ALU.add, op1=ALU.add))
                           (o, acc_sl, xprev[o % 2], htmp[o % 2]),
                           deps=(t_xp, t_t1, free.get(("acci", o))))
            free[("xp", o % 2)] = t_init
            free[("htmp", o % 2)] = t_init
            free[("acc", o)] = t_init
        xtok = emit_load_xbf(p, x1, xbf, cols=(tt * 512, 512),
                             deps=(deps, free.get("xbf")))
        for ex in range(2):
            WaT = W0aT if ex == 0 else W1aT
            WbT = W0bT if ex == 0 else W1bT
            act = AF.Gelu if ex == 0 else AF.Relu
            bcol = L0_BC["b0a"] if ex == 0 else L0_BC["b1a"]
            mbx = mb[tt % 2] if ex == 0 else m1b[tt % 2]
            for f in range(32):
                t_w = emit_wtile_load(p, WaT[f], wab[f % 2], NKC * 128,
                                      deps=(free.get(("wa", f % 2)),))
                bank = psH[f % 4]
                mmtok = emit_mm_group(
                    p, bank,
                    [(wab[f % 2][:, k * 128:(k + 1) * 128],
                      xbf[:, k * 512:(k + 1) * 512]) for k in range(NKC)],
                    deps=(xtok, t_w, free.get(("hb", f % 4))))
                t_hv = s.add("scalar", (lambda b_, h_, f_, a_, bc_: lambda sc:
                             sc.activation(out=h_[:], in_=b_, func=a_,
                                           bias=bias_sb[:, bc_ + f_:
                                                        bc_ + f_ + 1],
                                           scale=1.0))
                             (bank, htmp[f % 2], f, act, bcol),
                             deps=(mmtok, free.get(("htmp", f % 2))))
                free[("hb", f % 4)] = t_hv
                hsl = harena[:, f * 512:(f + 1) * 512]
                t_hm = s.add("vector", (lambda h_, m_, o_: lambda v:
                             v.tensor_tensor(o_, h_[:], m_[:], op=ALU.mult))
                             (htmp[f % 2], mbx, hsl),
                             deps=(t_hv, gate_toks[tt][ex],
                                   free.get(("hs", f))))
                free[("htmp", f % 2)] = t_hm
                free[("h", f)] = t_hm
                free[("wa", f % 2)] = s.barrier_token("tensor")
            if ex == 1:
                free["xbf"] = s.barrier_token("tensor")
            if ex == 0 and tt + 1 < NTT:
                emit_gate(tt + 1)
            for o in range(NKC):
                if tt == 0:
                    t_w = emit_wtile_load(p, WbT[o], wbb[o % 2], 32 * 128,
                                          deps=(free.get(("wb", o % 2)),))
                    t_w = s.add("sync", (lambda o_, b_, ex_: lambda sy:
                                sy.dma_start(out=WbC[ex_][o_],
                                             in_=b_[:]))(o, wbb[o % 2], ex),
                                deps=(t_w,), inc=16)
                else:
                    t_w = emit_wtile_load(p, WbC[ex][o], wbb[o % 2], 32 * 128,
                                          deps=(free.get(("wb", o % 2)),),
                                          queue="sync")
                bank = psO_[o % 2]
                mmtok = emit_mm_group(
                    p, bank,
                    [(wbb[o % 2][:, k * 128:(k + 1) * 128],
                      harena[:, k * 512:(k + 1) * 512]) for k in range(32)],
                    deps=[t_w, free.get(("ob", o % 2))] +
                         [free.get(("h", k)) for k in range(32)])
                acc_sl = acc[:, o * 512:(o + 1) * 512]
                t_add = s.add("vector", (lambda a_, b_: lambda v:
                              v.tensor_tensor(a_, a_, b_, op=ALU.add))
                              (acc_sl, bank),
                              deps=(mmtok, free.get(("acc", o))))
                free[("ob", o % 2)] = t_add
                free[("acc", o)] = t_add
                free[("wb", o % 2)] = s.barrier_token("tensor")
            if getattr(p, "dbg_h0", None) is not None:
                dst = p.dbg_h0 if ex == 0 else p.dbg_h
                for f in range(32):
                    s.add("sync", (lambda f_, a_, d_, t2: lambda sy:
                          sy.dma_start(out=d_[f_ * 128:(f_ + 1) * 128,
                                              t2 * 512:(t2 + 1) * 512],
                                       in_=a_))(f,
                          harena[:, f * 512:(f + 1) * 512], dst, tt),
                          deps=(free[("h", f)],), inc=16)
            bt = s.barrier_token("tensor")
            for f in range(32):
                free[("hs", f)] = bt
        if getattr(p, "dbg_acc", None) is not None:
            for o in range(NKC):
                s.add("sync", (lambda o_, t2, a_: lambda sy:
                      sy.dma_start(out=p.dbg_acc[o_ * 128:(o_ + 1) * 128,
                                                 t2 * 512:(t2 + 1) * 512],
                                   in_=a_))(o, tt,
                      acc[:, o * 512:(o + 1) * 512]),
                      deps=(free[("acc", o)],), inc=16)
        for o in range(NKC):
            acc_sl = acc[:, o * 512:(o + 1) * 512]
            sdep = (stats_free, free.get("psXg"))
            pln.add_chunk(o, free[("acc", o)], stats_dep=sdep, xa_ap=acc_sl)
        t_done, stats_free = pln.finish(tt)
        free["psXg"] = stats_free[0]
        free["psY"] = stats_free[1]
        for o in range(NKC):
            free[("acci", o)] = pln.xa_free(o)
            free.pop(("acc", o), None)
        free[("mb", tt % 2)] = s.barrier_token("vector")
        free[("m1b", tt % 2)] = s.barrier_token("vector")
    return t_done


P3_CONSTS = ("onesbf", "onesrowf")


def build_l0_ffn(debug=False):
    p = P()
    x1 = p.dram_in("x1", [D, T])
    WgT = p.dram_in("WgT", [D, 2])
    W0aT = p.dram_in("W0aT", [32, 128, D])
    W0bT = p.dram_in("W0bT", [16, 128, 32 * 128])
    W1aT = p.dram_in("W1aT", [32, 128, D])
    W1bT = p.dram_in("W1bT", [16, 128, 32 * 128])
    BP = p.dram_in("BP", [128, L0_BP_W])
    x2 = p.dram_out("x2", [D, T])
    if debug:
        p.dbg_acc = p.dram_out("dbg_acc", [D, T])
        p.dbg_h = p.dram_out("dbg_h", [2 * D, T], BF16)
        p.dbg_rows = p.dram_out("dbg_rows", [1, 2048])
        p.dbg_h0 = p.dram_out("dbg_h0", [2 * D, T], BF16)
    C = load_consts(p, P3_CONSTS)
    bias_sb = p.sb("biasp", [128, L0_BP_W], F32)
    t_bp = p.s.add("sync", lambda sy: sy.dma_start(out=bias_sb[:], in_=BP),
                   inc=16)
    emit_l0_ffn(p, x1, WgT, W0aT, W0bT, W1aT, W1bT, bias_sb, x2, C,
                deps=(t_bp,))
    return p.finalize()


L1_BC = {"q": 0, "k": 16, "bo": 32}
L1_BP_W = 48
P4_CONSTS = ("onesbf", "onesrowf", "onesrow1")


def build_l1_proj():
    p = P()
    x2 = p.dram_in("x2", [D, T])
    WqT = p.dram_in("WqT", [16, 128, D])
    WkT = p.dram_in("WkT", [16, 128, D])
    WvT = p.dram_in("WvT", [4, 128, NKC * 512])
    BP = p.dram_in("BP", [128, L1_BP_W])
    bvd = p.dram_in("bv", [1, D])
    Qd = p.dram_out("Qd", [D, T], BF16)
    Kd = p.dram_out("Kd", [D, T], BF16)
    Vd = p.dram_out("Vd", [NH, T, HD], BF16)
    CB = p.dram_out("CB", [128, NH])
    C = load_consts(p, P4_CONSTS)
    bias_sb = p.sb("biasp", [128, L1_BP_W], F32)
    bvrow = p.sb("bvrow", [1, D], BF16)
    t_bp = p.s.add("sync", lambda sy: sy.dma_start(out=bias_sb[:], in_=BP),
                   inc=16)
    t_bv = p.s.add("gpsimd", lambda g: g.dma_start(out=bvrow[0:1, :], in_=bvd),
                   inc=16)
    cbias = p.sb("cbias", [128, NH], F32)
    for e in ENGINES:
        p.s.add(e, lambda eng: eng.nop(), deps=(t_bp, t_bv), inc=1)
    emit_proj_phase(
        p, x2, [("q", WqT, Qd), ("k", WkT, Kd)], [(WvT, Vd)], C,
        bias_sb=bias_sb, qk_bias_cols={"q": L1_BC["q"], "k": L1_BC["k"]},
        v_bias_row=bvrow, cbias_out=cbias, scale_for_c=SCALE1)
    p.s.add("sync", lambda sy: sy.dma_start(out=CB, in_=cbias[:]),
            deps=(p.s.barrier_token("scalar"),), inc=16)
    return p.finalize()


P5_CONSTS = ("onesbf", "onesrowf")


def build_l1_attn():
    p = P()
    Qd = p.dram_in("Qd", [D, T], BF16)
    Kd = p.dram_in("Kd", [D, T], BF16)
    Vd = p.dram_in("Vd", [NH, T, HD], BF16)
    WoT = p.dram_in("WoT", [16, 128, D])
    x2 = p.dram_in("x2", [D, T])
    CB = p.dram_in("CB", [128, NH])
    BP = p.dram_in("BP", [128, L1_BP_W])
    cmk = p.dram_in("cmk", [4, 128, 512])
    x3 = p.dram_out("x3", [D, T])
    C = load_consts(p, P5_CONSTS)
    bias_sb = p.sb("biasp", [128, L1_BP_W], F32)
    t_bp = p.s.add("sync", lambda sy: sy.dma_start(out=bias_sb[:], in_=BP),
                   inc=16)
    cbias = p.sb("cbias", [128, NH], F32)
    cb_tok = p.s.add("sync", lambda sy: sy.dma_start(out=cbias[:], in_=CB),
                     inc=16)
    for e in ENGINES:
        p.s.add(e, lambda eng: eng.nop(), deps=(t_bp,), inc=1)
    emit_attn_phase(p, Qd, Kd, Vd, WoT, x2, x3, causal=True, scale=SCALE1,
                    C=C, cbias=cbias, cb_tok=cb_tok, bo_col=L1_BC["bo"],
                    bias_sb=bias_sb, cmask_dram=[cmk[r] for r in range(4)])
    return p.finalize()


P6_BC = {"sb1": 0, "rb1": 22, "sb2": 198, "rbias": 230}
P6_BP_W = 231
P6_CONSTS = ("onesbf", "onesrowf", "identF", "onehot")
DFF = 1408
NFC = 11


def emit_l1_moe(p, x3, sw1, sw2, rw1, rw2, rb2, centT, bias_sb, yT, C,
                deps=()):
    s = p.s
    psH = [p.psS[:, i * 512:(i + 1) * 512] for i in range(3)]
    psT = p.psS[:, 1536:2048]
    psO_ = [p.psG[:, 0:512], p.psG[:, 512:1024]]
    psX = p.psX

    xbf = p.sb("m_xbf", [128, NKC * 1024], BF16)
    xf32 = [p.sb(f"m_xf{i}", [128, 512], F32) for i in range(2)]
    xprev = xf32
    hpair = p.sb("m_h", [128, 2 * NFC * 1024], BF16)
    acc = p.sb("m_acc", [128, NKC * 1024], F32)
    w1b = [p.sb(f"m_w1{i}", [128, NKC * 128], BF16) for i in range(2)]
    w2b = [p.sb(f"m_w2{i}", [128, NFC * 128], BF16) for i in range(4)]
    centsb = p.sb("m_cent", [128, NKC * 16], F32)
    scor = p.sb("m_scor", [16, 512], F32)
    stok = p.sb("m_stok", [128, 16], F32)
    cmpb = p.sb("m_cmp", [128, 256], BF16)
    rnk = p.sb("m_rnk", [128, 64], BF16)
    gtok = p.sb("m_gtok", [128, 16], F32)
    grow = p.sb("m_grow", [128, 2], F32)
    ge16b = [p.sb(f"m_geb{i}", [16, 1024], BF16) for i in range(2)]
    rb2b = p.sb("m_rb2", [16, 2048], BF16)
    b2s = p.sb("m_b2s", [128, 16], F32)
    gb = [[p.sb(f"m_gb{i}_{t}", [128, 512], BF16) for t in range(2)]
          for i in range(2)]
    htmp = [p.sb(f"m_ht{i}", [128, 512], BF16) for i in range(2)]

    psY = p.psY
    pln = PostLN(p, "m_ln", yT, C, (psX, psY), own_buffer=False)

    t_ct = s.add("sync", lambda sy: sy.dma_start(
        out=centsb[:].rearrange("p (c e) -> p c e", e=16),
        in_=chunk3(centT).rearrange("c p e -> p c e")), deps=deps, inc=16)
    t_r2 = s.add("gpsimd", lambda g: g.dma_start(out=rb2b[:], in_=rb2),
                 deps=deps, inc=16)
    t_b2 = s.add("vector", lambda v: v.tensor_tensor(
        b2s[:], bias_sb[:, P6_BC["sb2"]:P6_BC["sb2"] + 16],
        bias_sb[:, P6_BC["sb2"] + 16:P6_BC["sb2"] + 32], op=ALU.add),
        deps=deps)

    free = {}
    stats_free = None
    t_done = None

    def emit_router(half):
        geb = ge16b[half]
        ge_toks = []
        for tth in range(2):
            tt = half * 2 + tth
            mm_d = None
            for kc in range(NKC):
                t_x = s.add("sync", (lambda kc_, tt_, b_: lambda sy:
                            sy.dma_start(out=b_[:],
                                         in_=x3[kc_ * 128:(kc_ + 1) * 128,
                                                tt_ * 512:(tt_ + 1) * 512]))
                            (kc, tt, xf32[kc % 2]),
                            deps=(free.get(("xf", kc % 2)), deps), inc=16)
                mm_d = s.add("tensor", (lambda kc_, b_, f, l: lambda t:
                             t.matmul(psX[0:16, :],
                                      centsb[:, kc_ * 16:(kc_ + 1) * 16],
                                      b_[:], start=f, stop=l))
                             (kc, xf32[kc % 2], kc == 0, kc == NKC - 1),
                             deps=(t_x, t_ct,
                                   free.get("psXr") if kc == 0 else None))
                free[("xf", kc % 2)] = mm_d
            t_sg = s.add("scalar", lambda sc: sc.activation(
                out=scor[:], in_=psX[0:16, :], func=AF.Sigmoid, bias=0.0,
                scale=1.0), deps=(mm_d, free.get("scor")))
            free["psXr"] = t_sg
            t_sb = s.add("vector", lambda v: v.tensor_scalar(
                out=scor[:], in0=scor[:],
                scalar1=bias_sb[0:16, P6_BC["rbias"]:P6_BC["rbias"] + 1],
                scalar2=None, op0=ALU.add), deps=(t_sg, deps))
            for tc in range(4):
                t_tr = s.add("tensor", (lambda tc_: lambda t:
                             t.transpose(psT[:, 0:16],
                                         scor[0:16, tc_ * 128:(tc_ + 1) * 128],
                                         C["identF"][0:16, 0:16]))(tc),
                             deps=(t_sb, free.get("psT")))
                t_st = s.add("vector", lambda v: v.tensor_copy(
                    stok[:], psT[:, 0:16]), deps=(t_tr, free.get("stok")))
                free["psT"] = t_st
                cps = []
                for e in range(16):
                    cps.append(s.add("vector", (lambda e_: lambda v:
                               v.tensor_scalar(
                                   out=cmpb[:, e_ * 16:(e_ + 1) * 16],
                                   in0=stok[:], scalar1=stok[:, e_:e_ + 1],
                                   scalar2=None, op0=ALU.is_lt))(e),
                               deps=(t_st, free.get("cmpb"))))
                t_a1 = s.add("vector", lambda v: v.tensor_tensor(
                    rnk[:, 0:64], cmpb[:, 0:64], cmpb[:, 64:128], op=ALU.add),
                    deps=cps[0:8])
                t_a2 = s.add("vector", lambda v: v.tensor_tensor(
                    rnk[:, 0:64], rnk[:, 0:64], cmpb[:, 128:192], op=ALU.add),
                    deps=(t_a1,) + tuple(cps[8:12]))
                t_a3 = s.add("vector", lambda v: v.tensor_tensor(
                    rnk[:, 0:64], rnk[:, 0:64], cmpb[:, 192:256], op=ALU.add),
                    deps=(t_a2,) + tuple(cps[12:16]))
                free["cmpb"] = t_a3
                t_a4 = s.add("vector", lambda v: v.tensor_tensor(
                    rnk[:, 0:32], rnk[:, 0:32], rnk[:, 32:64], op=ALU.add),
                    deps=(t_a3,))
                t_a5 = s.add("vector", lambda v: v.tensor_tensor(
                    rnk[:, 0:16], rnk[:, 0:16], rnk[:, 16:32], op=ALU.add),
                    deps=(t_a4,))
                t_k = s.add("vector", lambda v: v.tensor_scalar(
                    out=rnk[:, 16:32], in0=rnk[:, 0:16], scalar1=4.0,
                    scalar2=None, op0=ALU.is_lt), deps=(t_a5,))
                t_g0 = s.add("vector", lambda v: v.tensor_tensor(
                    gtok[:], stok[:], rnk[:, 16:32], op=ALU.mult),
                    deps=(t_k, free.get("gtok")))
                t_gs = s.add("vector", lambda v: v.reduce_sum(
                    grow[:, 0:1], gtok[:], axis=AX.X), deps=(t_g0,))
                t_gc = s.add("vector", lambda v: v.tensor_scalar(
                    out=grow[:, 0:1], in0=grow[:, 0:1], scalar1=1e-8,
                    scalar2=None, op0=ALU.max), deps=(t_gs,))
                t_gr = s.add("vector", lambda v: v.reciprocal(
                    grow[:, 1:2], grow[:, 0:1]), deps=(t_gc,))
                t_gn = s.add("vector", lambda v: v.tensor_scalar(
                    out=gtok[:], in0=gtok[:], scalar1=grow[:, 1:2],
                    scalar2=None, op0=ALU.mult), deps=(t_gr,))
                t_tb = s.add("tensor", (lambda: lambda t:
                             t.transpose(psT[0:16, 16:144], gtok[:],
                                         C["identF"][:, :]))(),
                             deps=(t_gn, free.get("psT2")))
                col = tth * 512 + tc * 128
                t_ev = s.add("scalar", (lambda c_, g_: lambda sc:
                             sc.activation(out=g_[0:16, c_:c_ + 128],
                                           in_=psT[0:16, 16:144],
                                           func=AF.Identity, bias=0.0,
                                           scale=1.0))(col, geb),
                             deps=(t_tb, free.get(("ge16", half))))
                free["psT2"] = t_ev
                free["stok"] = cps
                free["gtok"] = t_tb
                ge_toks.append(t_ev)
            free["scor"] = s.barrier_token("tensor")
        return tuple(ge_toks)

    ge_done = [emit_router(0), emit_router(1)]
    for half in range(2):
        th0 = half * 1024
        t_geb = ge_done[half]
        xtok = emit_load_xbf(p, x3, xbf, cols=(th0, 1024),
                             deps=(deps, free.get("xbf")))
        for tth in range(2):
            tt = half * 2 + tth
            for o in range(NKC):
                t_xp = s.add("sync", (lambda o_, tt_, b_: lambda sy:
                             sy.dma_start(out=b_[:],
                                          in_=x3[o_ * 128:(o_ + 1) * 128,
                                                 tt_ * 512:(tt_ + 1) * 512]))
                             (o, tt, xprev[o % 2]),
                             deps=(free.get(("xf", o % 2)),), inc=16)
                acc_sl = acc[:, o * 1024 + tth * 512:
                             o * 1024 + tth * 512 + 512]
                t_init = s.add("vector", (lambda o_, a_, xp_: lambda v:
                               v.tensor_scalar(
                                   out=a_, in0=xp_[:], scalar1=2.0,
                                   scalar2=b2s[:, o_:o_ + 1],
                                   op0=ALU.mult, op1=ALU.add))
                               (o, acc_sl, xprev[o % 2]),
                               deps=(t_xp, t_b2, free.get(("acci", o, tth))))
                free[("xf", o % 2)] = t_init
                free[("acc", o, tth)] = t_init

        for pi in range(9):
            members = []
            if pi == 0:
                members = [("s", 0), ("s", 1)]
            else:
                members = [("r", (pi - 1) * 2), ("r", (pi - 1) * 2 + 1)]
            gb_toks = {}
            for mi, (kind, j) in enumerate(members):
                if kind == "r":
                    for tth in range(2):
                        t_gb = s.add("tensor", (lambda j_, t2, g_: lambda t:
                                     t.matmul(psT[:, 0:512],
                                              C["onehot"][0:16,
                                                          j_ * 128:(j_ + 1) * 128],
                                              g_[0:16,
                                                 t2 * 512:(t2 + 1) * 512],
                                              start=True, stop=True))
                                     (j, tth, ge16b[half]),
                                     deps=(t_geb, free.get("psT2")))
                        t_ge = s.add("scalar", (lambda b_: lambda sc:
                                     sc.activation(out=b_[:], in_=psT[:, 0:512],
                                                   func=AF.Identity, bias=0.0,
                                                   scale=1.0))
                                     (gb[mi][tth]), deps=(t_gb,
                                                     free.get(("gb", mi, tth))))
                        free["psT2"] = t_ge
                        gb_toks[(mi, tth)] = t_ge
            for mi, (kind, j) in enumerate(members):
                if kind == "s":
                    w1ap = sw1[j]
                    bcol = P6_BC["sb1"] + j * NFC
                else:
                    w1ap = rw1[j]
                    bcol = P6_BC["rb1"] + j * NFC
                for f in range(NFC):
                    t_w = emit_wtile_load(p, w1ap[f], w1b[f % 2], NKC * 128,
                                          deps=(free.get(("w1", f % 2)),))
                    for tth in range(2):
                        g = f * 2 + tth
                        bank = psH[g % 3]
                        mmtok = emit_mm_group(
                            p, bank,
                            [(w1b[f % 2][:, k * 128:(k + 1) * 128],
                              xbf[:, k * 1024 + tth * 512:
                                  k * 1024 + tth * 512 + 512])
                             for k in range(NKC)],
                            deps=(xtok, t_w, free.get(("hb", g % 3))))
                        hsl = hpair[:, (mi * NFC + f) * 1024 + tth * 512:
                                    (mi * NFC + f) * 1024 + tth * 512 + 512]
                        if kind == "s":
                            t_hv = s.add("scalar",
                                         (lambda b_, o_, bc_, f_: lambda sc:
                                          sc.activation(
                                              out=o_, in_=b_, func=AF.Relu,
                                              bias=bias_sb[:, bc_ + f_:
                                                           bc_ + f_ + 1],
                                              scale=1.0))
                                         (bank, hsl, bcol, f),
                                         deps=(mmtok,
                                               free.get(("hs", mi, f, tth))))
                            free[("hb", g % 3)] = t_hv
                            t_hm = t_hv
                        else:
                            t_hv = s.add("scalar",
                                         (lambda b_, h_, bc_, f_: lambda sc:
                                          sc.activation(
                                              out=h_[:], in_=b_, func=AF.Relu,
                                              bias=bias_sb[:, bc_ + f_:
                                                           bc_ + f_ + 1],
                                              scale=1.0))
                                         (bank, htmp[g % 2], bcol, f),
                                         deps=(mmtok,
                                               free.get(("htmp", g % 2))))
                            free[("hb", g % 3)] = t_hv
                            t_hm = s.add("vector",
                                         (lambda h_, m_, o_: lambda v:
                                          v.tensor_tensor(o_, h_[:], m_[:],
                                                          op=ALU.mult))
                                         (htmp[g % 2], gb[mi][tth], hsl),
                                         deps=(t_hv, gb_toks[(mi, tth)],
                                               free.get(("hs", mi, f, tth))))
                            free[("htmp", g % 2)] = t_hm
                        free[("h", mi, f, tth)] = t_hm
                    free[("w1", f % 2)] = s.barrier_token("tensor")
            for o in range(NKC):
                wtoks = []
                for mi, (kind, j) in enumerate(members):
                    w2ap = sw2[j] if kind == "s" else rw2[j]
                    wtoks.append(emit_wtile_load(
                        p, w2ap[o], w2b[2 * (o % 2) + mi], NFC * 128,
                        deps=(free.get(("w2", 2 * (o % 2) + mi)),)))
                for tth in range(2):
                    tt = half * 2 + tth
                    bank = psO_[(o * 2 + tth) % 2]
                    pairs = []
                    for mi in range(2):
                        pairs += [(w2b[2 * (o % 2) + mi][:, k * 128:(k + 1) * 128],
                                   hpair[:, (mi * NFC + k) * 1024 + tth * 512:
                                         (mi * NFC + k) * 1024 + tth * 512 + 512])
                                  for k in range(NFC)]
                    if pi == 0:
                        pairs.append((rb2b[0:16, o * 128:(o + 1) * 128],
                                      ge16b[half][0:16,
                                            tth * 512:(tth + 1) * 512]))
                    hdeps = [free.get(("h", mi, k, tth))
                             for mi in range(2) for k in range(NFC)]
                    mmtok = emit_mm_group(
                        p, bank, pairs,
                        deps=[wtoks[0], wtoks[1], t_r2, t_geb,
                              free.get(("ob", (o * 2 + tth) % 2))] + hdeps)
                    acc_sl = acc[:, o * 1024 + tth * 512:
                                 o * 1024 + tth * 512 + 512]
                    t_add = s.add("vector", (lambda a_, b_: lambda v:
                                  v.tensor_tensor(a_, a_, b_, op=ALU.add))
                                  (acc_sl, bank),
                                  deps=(mmtok, free.get(("acc", o, tth))))
                    free[("ob", (o * 2 + tth) % 2)] = t_add
                    free[("acc", o, tth)] = t_add
                for mi in range(2):
                    free[("w2", 2 * (o % 2) + mi)] = s.barrier_token("tensor")
            bt = s.barrier_token("tensor")
            for mi in range(2):
                for f in range(NFC):
                    for tth in range(2):
                        free[("hs", mi, f, tth)] = bt
                for tth in range(2):
                    free[("gb", mi, tth)] = bt
            if pi == 8:
                free["xbf"] = s.barrier_token("tensor")

        for tth in range(2):
            tt = half * 2 + tth
            for o in range(NKC):
                acc_sl = acc[:, o * 1024 + tth * 512:
                             o * 1024 + tth * 512 + 512]
                sdep = (stats_free, free.get("psXr"))
                pln.add_chunk(o, free[("acc", o, tth)], stats_dep=sdep,
                              xa_ap=acc_sl)
            t_done, stats_free = pln.finish(tt)
            free["psXr"] = stats_free[0]
        for o in range(NKC):
            for tth in range(2):
                free[("acci", o, tth)] = pln.xa_free(o)
                free.pop(("acc", o, tth), None)
        if getattr(p, "dbg_ge", None) is not None:
            s.add("sync", (lambda h_, g_: lambda sy: sy.dma_start(
                  out=p.dbg_ge[:, h_ * 1024:(h_ + 1) * 1024], in_=g_[:]))
                  (half, ge16b[half]), deps=t_geb, inc=16)
        if getattr(p, "dbg_acc", None) is not None:
            for o in range(NKC):
                for tth in range(2):
                    t2 = half * 2 + tth
                    s.add("sync", (lambda o_, t2_, a_: lambda sy:
                          sy.dma_start(out=p.dbg_acc[o_ * 128:(o_ + 1) * 128,
                                                     t2_ * 512:(t2_ + 1) * 512],
                                       in_=a_))(o, t2,
                          acc[:, o * 1024 + tth * 512:o * 1024 + tth * 512 + 512]),
                          deps=(free.get(("acci", o, tth)),
                                s.barrier_token("vector")), inc=16)
        free["xbf"] = s.barrier_token("tensor")
        free[("ge16", half)] = s.barrier_token("tensor")
    return t_done


def build_l1_moe(debug=False):
    p = P()
    x3 = p.dram_in("x3", [D, T])
    sw1d = p.dram_in("sw1", [2, NFC, 128, D])
    sw2d = p.dram_in("sw2", [2, 16, 128, NFC * 128])
    rw1d = p.dram_in("rw1", [16, NFC, 128, D])
    rw2d = p.dram_in("rw2", [16, 16, 128, NFC * 128])
    rb2 = p.dram_in("rb2", [16, D])
    centT = p.dram_in("centT", [D, 16])
    BP = p.dram_in("BP", [128, P6_BP_W])
    yT = p.dram_out("yT", [D, T])
    if debug:
        p.dbg_ge = p.dram_out("dbg_ge", [16, T], BF16)
        p.dbg_acc = p.dram_out("dbg_acc", [D, T])
    C = load_consts(p, P6_CONSTS)
    bias_sb = p.sb("biasp", [128, P6_BP_W], F32)
    t_bp = p.s.add("sync", lambda sy: sy.dma_start(out=bias_sb[:], in_=BP),
                   inc=16)
    for e in ENGINES:
        p.s.add(e, lambda eng: eng.nop(), deps=(t_bp,), inc=1)
    emit_l1_moe(p, x3, [sw1d[i] for i in range(2)], [sw2d[i] for i in range(2)],
                [rw1d[i] for i in range(16)], [rw2d[i] for i in range(16)],
                rb2, centT, bias_sb, yT, C)
    return p.finalize()


_NC_CACHE = {}


def _get_nc(name, builder):
    if name not in _NC_CACHE:
        _NC_CACHE[name] = builder()
    return _NC_CACHE[name]


def kernel_unfused(**ins):
    consts = host_consts()
    ncores = 8
    cores = list(range(ncores))
    x = ins["L0_attn_x"]

    def ct(a):
        return np.ascontiguousarray(np.asarray(a, dtype=np.float32))

    xT = [np.ascontiguousarray(x[b].T) for b in range(ncores)]

    nc1 = _get_nc("p1", build_l0_proj)
    w1 = {"wqT": tile_wT(ins["L0_attn_wq"].T, 128),
          "wkT": tile_wT(ins["L0_attn_wk"].T, 128),
          "wvT": tile_wT(ins["L0_attn_wv"].T, 512)}
    cm1 = const_in_maps(consts, P1_CONSTS)
    r1 = run_bass_kernel_spmd(
        nc1, [dict(xT=xT[b], **w1, **cm1) for b in range(ncores)], cores)

    nc2 = _get_nc("p2", build_l0_attn)
    w2 = {"woT": tile_wT(ins["L0_attn_wo"].T, 128)}
    cm2 = const_in_maps(consts, P2_CONSTS)
    r2 = run_bass_kernel_spmd(
        nc2, [dict(Qd=r1.results[b]["Qd"], Kd=r1.results[b]["Kd"],
                   Vd=r1.results[b]["Vd"], CB=r1.results[b]["CB"],
                   xT=xT[b], **w2, **cm2) for b in range(ncores)], cores)

    nc3 = _get_nc("p3", build_l0_ffn)
    bp3 = pack_cols(ins["L0_ffn_b0a"], ins["L0_ffn_b0b"],
                    ins["L0_ffn_b1a"], ins["L0_ffn_b1b"])
    w3 = {"WgT": ct(ins["L0_ffn_Wg"].T),
          "W0aT": tile_wT(ins["L0_ffn_W0a"].T, 128),
          "W0bT": tile_wT(ins["L0_ffn_W0b"].T, 128, nk=32),
          "W1aT": tile_wT(ins["L0_ffn_W1a"].T, 128),
          "W1bT": tile_wT(ins["L0_ffn_W1b"].T, 128, nk=32), "BP": bp3}
    cm3 = const_in_maps(consts, P3_CONSTS)
    r3 = run_bass_kernel_spmd(
        nc3, [dict(x1=r2.results[b]["x1"], **w3, **cm3)
              for b in range(ncores)], cores)

    nc4 = _get_nc("p4", build_l1_proj)
    bp4 = pack_cols(ins["L1_attn_bq"], ins["L1_attn_bk"], ins["L1_attn_bo"])
    w4 = {"WqT": tile_wT(ins["L1_attn_Wq"].T, 128),
          "WkT": tile_wT(ins["L1_attn_Wk"].T, 128),
          "WvT": tile_wT(ins["L1_attn_Wv"].T, 512), "BP": bp4,
          "bv": ct(ins["L1_attn_bv"]).reshape(1, -1)}
    cm4 = const_in_maps(consts, P4_CONSTS)
    r4 = run_bass_kernel_spmd(
        nc4, [dict(x2=r3.results[b]["x2"], **w4, **cm4)
              for b in range(ncores)], cores)

    nc5 = _get_nc("p5", build_l1_attn)
    w5 = {"WoT": tile_wT(ins["L1_attn_Wo"].T, 128), "BP": bp4,
          "cmk": consts["cmaskK"]}
    cm5 = const_in_maps(consts, P5_CONSTS)
    r5 = run_bass_kernel_spmd(
        nc5, [dict(Qd=r4.results[b]["Qd"], Kd=r4.results[b]["Kd"],
                   Vd=r4.results[b]["Vd"], CB=r4.results[b]["CB"],
                   x2=r3.results[b]["x2"], **w5, **cm5)
              for b in range(ncores)], cores)

    nc6 = _get_nc("p6", build_l1_moe)
    bp6 = np.zeros((128, P6_BP_W), dtype=np.float32)
    sb1 = np.asarray(ins["L1_ffn_shared_b1"], dtype=np.float32)
    sb2 = np.asarray(ins["L1_ffn_shared_b2"], dtype=np.float32)
    rb1 = np.asarray(ins["L1_ffn_routed_b1"], dtype=np.float32)
    bp6[:, 0:22] = pack_cols(sb1[0], sb1[1])
    bp6[:, 22:198] = pack_cols(*[rb1[j] for j in range(16)])
    bp6[:, 198:230] = pack_cols(sb2[0], sb2[1])
    bp6[0:16, 230] = np.asarray(ins["L1_ffn_bias"], dtype=np.float32)
    w6 = {"sw1": np.stack([tile_wT(ins["L1_ffn_shared_w1"][n], 128)
                           for n in range(2)]),
          "sw2": np.stack([tile_wT(ins["L1_ffn_shared_w2"][n], 128, nk=NFC)
                           for n in range(2)]),
          "rw1": np.stack([tile_wT(ins["L1_ffn_routed_w1"][j], 128)
                           for j in range(16)]),
          "rw2": np.stack([tile_wT(ins["L1_ffn_routed_w2"][j], 128, nk=NFC)
                           for j in range(16)]),
          "rb2": ct(ins["L1_ffn_routed_b2"]),
          "centT": ct(ins["L1_ffn_centroids"].T), "BP": bp6}
    cm6 = const_in_maps(consts, P6_CONSTS)
    r6 = run_bass_kernel_spmd(
        nc6, [dict(x3=r5.results[b]["x3"], **w6, **cm6)
              for b in range(ncores)], cores)

    out = np.stack([r6.results[b]["yT"].T for b in range(ncores)], axis=0)
    return out.astype(np.float32)


FUSED_CONSTS = ("onesbf", "onesrowf", "onesrow1", "identF", "onehot")


def build_fused():
    p = P()
    xT = p.dram_in("xT", [D, T])
    wqT0 = p.dram_in("wqT0", [16, 128, D]); wkT0 = p.dram_in("wkT0", [16, 128, D])
    wvT0 = p.dram_in("wvT0", [4, 128, NKC * 512])
    woT0 = p.dram_in("woT0", [16, 128, D])
    WgT = p.dram_in("WgT", [D, 2])
    W0aT = p.dram_in("W0aT", [32, 128, D])
    W0bT = p.dram_in("W0bT", [16, 128, 32 * 128])
    W1aT = p.dram_in("W1aT", [32, 128, D])
    W1bT = p.dram_in("W1bT", [16, 128, 32 * 128])
    BP0 = p.dram_in("BP0", [128, L0_BP_W])
    wqT1 = p.dram_in("wqT1", [16, 128, D]); wkT1 = p.dram_in("wkT1", [16, 128, D])
    wvT1 = p.dram_in("wvT1", [4, 128, NKC * 512])
    woT1 = p.dram_in("woT1", [16, 128, D])
    BP1 = p.dram_in("BP1", [128, L1_BP_W])
    bvd = p.dram_in("bv", [1, D])
    sw1d = p.dram_in("sw1", [2, NFC, 128, D])
    sw2d = p.dram_in("sw2", [2, 16, 128, NFC * 128])
    rw1d = p.dram_in("rw1", [16, NFC, 128, D])
    rw2d = p.dram_in("rw2", [16, 16, 128, NFC * 128])
    rb2 = p.dram_in("rb2", [16, D])
    centT = p.dram_in("centT", [D, 16])
    BP6 = p.dram_in("BP6", [128, P6_BP_W])
    ropeCd = p.dram_in("c_ropeC", [64, T])
    ropeSd = p.dram_in("c_ropeS", [64, T])
    cmk = p.dram_in("cmk", [4, 128, 512])
    yT = p.dram_out("yT", [D, T])

    Qd = p.dram_tmp("Qs", [D, T], BF16)
    Kd = p.dram_tmp("Ks", [D, T], BF16)
    Vd = p.dram_tmp("Vs", [NH, T, HD], BF16)
    x1 = p.dram_tmp("x1s", [D, T])
    x2 = p.dram_tmp("x2s", [D, T])
    x3 = p.dram_tmp("x3s", [D, T])

    C = load_consts(p, FUSED_CONSTS)
    cb0 = p.sb("cb0", [128, NH], F32)
    cb1 = p.sb("cb1", [128, NH], F32)
    bias0 = p.sb("bias0", [128, L0_BP_W], F32)
    bias1 = p.sb("bias1", [128, L1_BP_W], F32)
    bias6 = p.sb("bias6", [128, P6_BP_W], F32)
    bvrow = p.sb("bvrow", [1, D], BF16)
    tb = [p.s.add("sync", (lambda d_, s_: lambda sy:
                  sy.dma_start(out=s_[:], in_=d_))(d, sbh), inc=16)
          for d, sbh in ((BP0, bias0), (BP1, bias1), (BP6, bias6))]
    tb.append(p.s.add("gpsimd", lambda g: g.dma_start(out=bvrow[0:1, :],
                                                      in_=bvd), inc=16))
    for e in ENGINES:
        p.s.add(e, lambda eng: eng.nop(), deps=tb, inc=1)

    p.push_scope()
    rC = p.sb("rC", [64, T], F32)
    rS = p.sb("rS", [64, T], F32)
    t1 = p.s.add("sync", lambda sy: sy.dma_start(out=rC[:], in_=ropeCd), inc=16)
    t2 = p.s.add("sync", lambda sy: sy.dma_start(out=rS[:], in_=ropeSd), inc=16)
    for e in ("vector",):
        p.s.add(e, lambda eng: eng.nop(), deps=(t1, t2), inc=1)
    emit_proj_phase(p, xT, [("q", wqT0, Qd), ("k", wkT0, Kd)], [(wvT0, Vd)],
                    C, ropeC_sb=rC, ropeS_sb=rS, cbias_out=cb0,
                    scale_for_c=SCALE0)
    p.phase_barrier()
    p.pop_scope()

    p.push_scope()
    cbt = p.s.barrier_token("scalar")
    emit_attn_phase(p, Qd, Kd, Vd, woT0, xT, x1, causal=False, scale=SCALE0,
                    C=C, cbias=cb0, cb_tok=cbt)
    p.phase_barrier()
    p.pop_scope()

    p.push_scope()
    emit_l0_ffn(p, x1, WgT, W0aT, W0bT, W1aT, W1bT, bias0, x2, C)
    p.phase_barrier()
    p.pop_scope()

    p.push_scope()
    emit_proj_phase(p, x2, [("q", wqT1, Qd), ("k", wkT1, Kd)], [(wvT1, Vd)],
                    C, bias_sb=bias1,
                    qk_bias_cols={"q": L1_BC["q"], "k": L1_BC["k"]},
                    v_bias_row=bvrow, cbias_out=cb1, scale_for_c=SCALE1)
    p.phase_barrier()
    p.pop_scope()

    p.push_scope()
    cbt1 = p.s.barrier_token("scalar")
    emit_attn_phase(p, Qd, Kd, Vd, woT1, x2, x3, causal=True, scale=SCALE1,
                    C=C, cbias=cb1, cb_tok=cbt1, bo_col=L1_BC["bo"],
                    bias_sb=bias1, cmask_dram=[cmk[r] for r in range(4)])
    p.phase_barrier()
    p.pop_scope()

    p.push_scope()
    emit_l1_moe(p, x3, [sw1d[i] for i in range(2)], [sw2d[i] for i in range(2)],
                [rw1d[i] for i in range(16)], [rw2d[i] for i in range(16)],
                rb2, centT, bias6, yT, C)
    p.pop_scope()
    return p.finalize()


def kernel_fused(**ins):
    consts = host_consts()
    ncores = 8
    cores = list(range(ncores))
    x = ins["L0_attn_x"]

    def ct(a):
        return np.ascontiguousarray(np.asarray(a, dtype=np.float32))

    nc = _get_nc("fused", build_fused)
    bp3 = pack_cols(ins["L0_ffn_b0a"], ins["L0_ffn_b0b"],
                    ins["L0_ffn_b1a"], ins["L0_ffn_b1b"])
    bp4 = pack_cols(ins["L1_attn_bq"], ins["L1_attn_bk"], ins["L1_attn_bo"])
    bp6 = np.zeros((128, P6_BP_W), dtype=np.float32)
    sb1 = np.asarray(ins["L1_ffn_shared_b1"], dtype=np.float32)
    sb2 = np.asarray(ins["L1_ffn_shared_b2"], dtype=np.float32)
    rb1 = np.asarray(ins["L1_ffn_routed_b1"], dtype=np.float32)
    bp6[:, 0:22] = pack_cols(sb1[0], sb1[1])
    bp6[:, 22:198] = pack_cols(*[rb1[j] for j in range(16)])
    bp6[:, 198:230] = pack_cols(sb2[0], sb2[1])
    bp6[0:16, 230] = np.asarray(ins["L1_ffn_bias"], dtype=np.float32)
    shared = {
        "wqT0": tile_wT(ins["L0_attn_wq"].T, 128),
        "wkT0": tile_wT(ins["L0_attn_wk"].T, 128),
        "wvT0": tile_wT(ins["L0_attn_wv"].T, 512),
        "woT0": tile_wT(ins["L0_attn_wo"].T, 128),
        "WgT": ct(ins["L0_ffn_Wg"].T),
        "W0aT": tile_wT(ins["L0_ffn_W0a"].T, 128),
        "W0bT": tile_wT(ins["L0_ffn_W0b"].T, 128, nk=32),
        "W1aT": tile_wT(ins["L0_ffn_W1a"].T, 128),
        "W1bT": tile_wT(ins["L0_ffn_W1b"].T, 128, nk=32),
        "BP0": bp3,
        "wqT1": tile_wT(ins["L1_attn_Wq"].T, 128),
        "wkT1": tile_wT(ins["L1_attn_Wk"].T, 128),
        "wvT1": tile_wT(ins["L1_attn_Wv"].T, 512),
        "woT1": tile_wT(ins["L1_attn_Wo"].T, 128),
        "BP1": bp4, "bv": ct(ins["L1_attn_bv"]).reshape(1, -1),
        "sw1": np.stack([tile_wT(ins["L1_ffn_shared_w1"][n], 128)
                         for n in range(2)]),
        "sw2": np.stack([tile_wT(ins["L1_ffn_shared_w2"][n], 128, nk=NFC)
                         for n in range(2)]),
        "rw1": np.stack([tile_wT(ins["L1_ffn_routed_w1"][j], 128)
                         for j in range(16)]),
        "rw2": np.stack([tile_wT(ins["L1_ffn_routed_w2"][j], 128, nk=NFC)
                         for j in range(16)]),
        "rb2": ct(ins["L1_ffn_routed_b2"]),
        "centT": ct(ins["L1_ffn_centroids"].T), "BP6": bp6,
        "c_ropeC": consts["ropeC"], "c_ropeS": consts["ropeS"],
        "cmk": consts["cmaskK"],
    }
    shared.update(const_in_maps(consts, FUSED_CONSTS))
    in_maps = [dict(xT=np.ascontiguousarray(x[b].T), **shared)
               for b in range(ncores)]
    res = run_bass_kernel_spmd(nc, in_maps, cores)
    out = np.stack([res.results[b]["yT"].T for b in range(ncores)], axis=0)
    return out.astype(np.float32)


kernel = kernel_fused


# version 10
# speedup vs baseline: 1.3070x; 1.0488x over previous
import numpy as np
import ml_dtypes
import concourse.bass as bass
import concourse.mybir as mybir
from concourse.bass_utils import run_bass_kernel_spmd

F32 = mybir.dt.float32
BF16 = mybir.dt.bfloat16
U8 = mybir.dt.uint8
AF = mybir.ActivationFunctionType
ALU = mybir.AluOpType
AX = mybir.AxisListType
BF = ml_dtypes.bfloat16

D = 2048
T = 2048
NH = 16
HD = 128
NKC = D // 128
NTT = T // 512
NTC = T // 128
EPS = 1e-5
SCALE0 = float(1.0 / np.sqrt(128.0))
SCALE1 = float(1.0 / np.sqrt(128.0))
FCONSTS = [1e-5, 1e-30, 0.5, -0.5, SCALE0, 2.0, 1e-8, -1.0]

ENGINES = ("tensor", "vector", "scalar", "gpsimd", "sync")
ESEM = {"tensor": "pe", "vector": "dv", "scalar": "sc", "gpsimd": "ql",
        "sync": "qs"}
NDSEM = 6
DMA_SEMS = {"gpsimd": ["qld%d" % i for i in range(NDSEM)],
            "sync": ["qsd%d" % i for i in range(NDSEM)]}
ALL_SEMS = ["pe", "dv", "sc", "ql", "qs"] + DMA_SEMS["gpsimd"] +     DMA_SEMS["sync"]


def _flat(deps):
    out = []
    for d in deps:
        if d is None:
            continue
        if isinstance(d, list) or isinstance(d, tuple):
            if len(d) == 2 and isinstance(d[0], str):
                out.append(d)
            else:
                out.extend(_flat(d))
        else:
            raise TypeError(f"bad dep {d!r}")
    return out


class Sched:

    def __init__(self, nc):
        self.nc = nc
        self.ops = {e: [] for e in ENGINES}
        self.cnt = {s: 0 for s in ALL_SEMS}
        self._rot = {"gpsimd": 0, "sync": 0}
        self._last_dma = {}

    def add(self, eng, fn, deps=(), inc=1):
        deps = tuple(_flat(deps))
        if inc == 16 and eng in DMA_SEMS:
            i = self._rot[eng] % NDSEM
            self._rot[eng] += 1
            sem = DMA_SEMS[eng][i]
            prev = self._last_dma.get(sem)
            if prev is not None:
                deps = deps + (prev,)
        else:
            sem = ESEM[eng]
        if inc:
            self.cnt[sem] += inc
        tok = (sem, self.cnt[sem]) if inc else None
        if inc == 16 and eng in DMA_SEMS:
            self._last_dma[sem] = tok
        self.ops[eng].append((fn, deps, sem, inc))
        return tok

    def barrier_token(self, eng):
        if eng in ("gpsimd", "sync"):
            toks = [(ESEM[eng], self.cnt[ESEM[eng]])]
            toks += [(s, self.cnt[s]) for s in DMA_SEMS[eng]]
            return [t for t in toks if t[1] > 0]
        return (ESEM[eng], self.cnt[ESEM[eng]])

    def all_tokens(self):
        out = []
        for e in ENGINES:
            t = self.barrier_token(e)
            out.extend(t if isinstance(t, list) else [t])
        return out

    def emit(self, block, sems):
        for e in ENGINES:
            ops = self.ops[e]
            if not ops:
                continue

            def make(section_ops):
                def body(eng):
                    seen = {}
                    for fn, deps, sem, inc in section_ops:
                        for dsem, dcnt in deps:
                            if seen.get(dsem, 0) < dcnt:
                                eng.wait_ge(sems[dsem], dcnt)
                                seen[dsem] = dcnt
                        ins = fn(eng)
                        if inc:
                            assert ins is not None, f"op on {sem} returned None"
                            ins.then_inc(sems[sem], inc)
                return body

            getattr(block, e)(make(ops))


class P:
    def __init__(self, name="mk"):
        self.nc = bass.Bass("TRN2", target_bir_lowering=False)
        self.s = Sched(self.nc)
        self._ctx = []
        self.sems = {}
        self.psS = self._psum("psS", [128, 2048], F32)
        self.psG = self._psum("psG", [128, 1024], F32)
        self.psY = self._psum("psY", [128, 512], F32)
        self.psX = self._psum("psX", [128, 512], F32)
        self.ps6 = [self.psS[:, i * 512:(i + 1) * 512] for i in range(4)] + \
                   [self.psG[:, 0:512], self.psG[:, 512:1024]]
        self.fc_sb = self.sb("fconsts", [128, len(FCONSTS)], F32)
        for i, v in enumerate(FCONSTS):
            self.nc.const_aps.register(F32, float(v), self.fc_sb.ap()[:, i:i + 1])

    def _enter(self, cm):
        h = cm.__enter__()
        if getattr(self, "_scopes", None):
            self._scopes[-1].append(cm)
        else:
            self._ctx.append(cm)
        return h

    def push_scope(self):
        self._scopes = getattr(self, "_scopes", [])
        self._scopes.append([])

    def pop_scope(self):
        for cm in reversed(self._scopes.pop()):
            cm.__exit__(None, None, None)

    def phase_barrier(self):
        toks = self.s.all_tokens()
        out = []
        for e in ENGINES:
            out.append(self.s.add(e, lambda eng: eng.nop(), deps=toks, inc=1))
        return out

    def sem(self, name):
        return self._enter(self.nc.semaphore(name))

    def sb(self, name, shape, dtype):
        self._sbser = getattr(self, "_sbser", 0) + 1
        return self._enter(self.nc.sbuf_tensor(f"{name}_{self._sbser}",
                                               shape, dtype))

    def _psum(self, name, shape, dtype=F32):
        return self._enter(self.nc.psum_tensor(name, shape, dtype))

    def dram_in(self, name, shape, dtype=F32):
        return self.nc.dram_tensor(name, shape, dtype, kind="ExternalInput").ap()

    def dram_out(self, name, shape, dtype=F32):
        return self.nc.dram_tensor(name, shape, dtype, kind="ExternalOutput").ap()

    def dram_tmp(self, name, shape, dtype=F32):
        return self.nc.dram_tensor(name, shape, dtype, kind="Internal").ap()

    def finalize(self):
        for s in ALL_SEMS:
            self.sems[s] = self.sem(s)
        toks = self.s.all_tokens()
        self.s.add("sync", lambda e: e.nop(), deps=toks, inc=1)
        with self.nc.Block() as block:
            self.s.emit(block, self.sems)
        for cm in reversed(self._ctx):
            cm.__exit__(None, None, None)
        return self.nc


def host_consts():
    c = {}
    inv_freq = 1.0 / 10000.0 ** (np.arange(0, 128, 2, dtype=np.float64) / 128.0)
    ang = np.arange(T, dtype=np.float64)[:, None] * inv_freq[None, :]
    c["ropeC"] = np.cos(ang).astype(np.float32).T.copy()
    sinc = np.sin(ang).astype(np.float32).T.copy()
    sinc[0:32] = -sinc[0:32]
    c["ropeS"] = np.ascontiguousarray(sinc)
    c["identB"] = np.eye(128, dtype=np.float32).astype(BF)
    c["identF"] = np.eye(128, dtype=np.float32)
    oh = np.zeros((16, 16 * 128), dtype=np.float32)
    for j in range(16):
        oh[j, j * 128:(j + 1) * 128] = 1.0
    c["onehot"] = oh.astype(BF)
    mm = np.zeros((4, 128, 512), dtype=np.float32)
    i = np.arange(128)[:, None]
    j = np.arange(512)[None, :]
    for r in range(4):
        mm[r] = np.where(i + 128 * r <= j, 0.0, -1e30)
    c["cmaskK"] = mm
    c["onesbf"] = np.ones((128, 1), dtype=BF)
    c["onesrowf"] = np.ones((1, 128), dtype=np.float32)
    c["onesrow1"] = np.ones((1, 128), dtype=BF)
    fc = np.zeros((128, len(FCONSTS)), dtype=np.float32)
    for i, v in enumerate(FCONSTS):
        fc[:, i] = v
    c["fconsts"] = fc
    return c


def tile_wT(wT, owid, nk=NKC):
    wT = np.ascontiguousarray(np.asarray(wT, dtype=np.float32))
    kk, oo = wT.shape
    assert kk == nk * 128 and oo % owid == 0
    nob = oo // owid
    a = wT.reshape(nk, 128, nob, owid).transpose(2, 1, 0, 3)
    return np.ascontiguousarray(a.reshape(nob, 128, nk * owid))


def pack_cols(*vecs):
    cols = []
    for v in vecs:
        v = np.asarray(v, dtype=np.float32)
        assert v.ndim == 1 and v.size % 128 == 0
        cols.append(v.reshape(-1, 128).T)
    return np.ascontiguousarray(np.concatenate(cols, axis=1))


def chunk3(ap, p=128):
    return ap.rearrange("(c p) n -> c p n", p=p)


def emit_load_xbf(p, xres, xbf, cols=None, deps=()):
    t0, w = cols if cols else (0, T)
    xv = chunk3(xres)

    def fn(g):
        return g.dma_start(
            out=xbf[:, 0:NKC * w].rearrange("p (c t) -> p c t", c=NKC),
            in_=xv[:, :, t0:t0 + w].rearrange("c p t -> p c t"))
    return p.s.add("gpsimd", fn, deps=deps, inc=16)


def emit_wcol_load(p, wT, wbuf, ostart, owid, deps=(), nk=NKC):
    wv = chunk3(wT)

    def fn(g):
        return g.dma_start(
            out=wbuf[:, 0:nk * owid].rearrange("p (c m) -> p c m", c=nk),
            in_=wv[:, :, ostart:ostart + owid].rearrange("c p o -> p c o"))
    return p.s.add("gpsimd", fn, deps=deps, inc=16)


def emit_wtile_load(p, wt_block, wbuf, width, deps=(), queue="gpsimd"):
    def fn(g):
        return g.dma_start(out=wbuf[:, 0:width], in_=wt_block)
    return p.s.add(queue, fn, deps=deps, inc=16)


def emit_mm_group(p, bank, pairs, deps=()):
    pairs = list(pairs)

    def fn(t):
        n = len(pairs)
        ins = None
        for i, (l, r) in enumerate(pairs):
            ins = t.matmul(bank, l, r, start=(i == 0), stop=(i == n - 1))
        return ins
    return p.s.add("tensor", fn, deps=deps, inc=1)


def emit_proj_phase(p, xres, qk_list, v_list, C, ropeC_sb=None, ropeS_sb=None,
                    bias_sb=None, qk_bias_cols=None, v_bias_row=None,
                    cbias_out=None, scale_for_c=1.0, deps=(), tab_tok=()):
    s = p.s
    xbf = p.sb("pj_xbf", [128, NKC * T], BF16)
    wbuf = [p.sb(f"pj_wb{i}", [128, NKC * 128], BF16) for i in range(3)]
    vwbuf = [p.sb(f"pj_vwb{i}", [128, NKC * 512], BF16) for i in range(2)]
    rt1 = [p.sb(f"pj_rt1_{i}", [64, 512], F32) for i in range(2)]
    rt2 = [p.sb(f"pj_rt2_{i}", [64, 512], F32) for i in range(2)]
    ost = [p.sb(f"pj_ost{i}", [128, 512], BF16) for i in range(6)]
    sqb = [p.sb(f"pj_sq{i}", [128, 512], BF16) for i in range(2)]
    nrm = p.sb("pj_nrm", [1, 2 * NH * NTT], F32)
    mrow = p.sb("pj_mrow", [1, 2 * NH], F32)
    crow = p.sb("pj_crow", [1, NH], F32)

    xtok = emit_load_xbf(p, xres, xbf, deps=deps)
    rope = ropeC_sb is not None

    bank_free = {}
    wb_free = {}
    ost_free = {}
    rt_free = {}
    sq_free = {}
    pend_nrm = []
    psX = p.psX
    nrm_free = None

    banks = p.ps6
    gidx = 0

    for pi, (tag, wT, outd) in enumerate(qk_list):
        for h in range(NH):
            wtok = emit_wtile_load(p, wT[h], wbuf[(pi * NH + h) % 3],
                                   NKC * 128,
                                   deps=(wb_free.get((pi * NH + h) % 3),))
            for tt in range(NTT):
                bk = gidx % 6
                bank = banks[bk]
                mmtok = emit_mm_group(
                    p, bank,
                    [(wbuf[(pi * NH + h) % 3][:, k * 128:(k + 1) * 128],
                      xbf[:, k * T + tt * 512: k * T + (tt + 1) * 512])
                     for k in range(NKC)],
                    deps=(xtok, wtok, bank_free.get(bk)))
                oi = gidx % 6
                ob = ost[oi]
                cs = slice(tt * 512, (tt + 1) * 512)
                if rope:
                    par = gidx % 2
                    t_m = s.add("vector", (lambda b_, pr, c_: lambda v:
                                v.tensor_tensor(rt1[pr][:, :], b_[0:64, :],
                                                ropeC_sb[:, c_], op=ALU.mult)
                                )(bank, par, cs),
                                deps=(mmtok, rt_free.get(par),
                                      ost_free.get(oi), tab_tok))
                    t_s1 = s.add("vector", (lambda b_, pr, c_: lambda v:
                                 v.tensor_tensor(rt2[pr][0:32, :], b_[32:64, :],
                                                 ropeS_sb[0:32, c_], op=ALU.mult)
                                 )(bank, par, cs), deps=(mmtok,))
                    t_s2 = s.add("vector", (lambda b_, pr, c_: lambda v:
                                 v.tensor_tensor(rt2[pr][32:64, :], b_[0:32, :],
                                                 ropeS_sb[32:64, c_], op=ALU.mult)
                                 )(bank, par, cs), deps=(mmtok,))
                    t_cp = s.add("scalar", (lambda b_, o_: lambda sc:
                                 sc.activation(out=o_[64:128, :],
                                               in_=b_[64:128, :],
                                               func=AF.Identity, bias=0.0,
                                               scale=1.0))(bank, ob),
                                 deps=(mmtok,))
                    t_add = s.add("vector", (lambda pr, o_: lambda v:
                                  v.tensor_tensor(o_[0:64, :], rt1[pr][:, :],
                                                  rt2[pr][:, :], op=ALU.add)
                                  )(par, ob), deps=(t_m, t_s1, t_s2))
                    bank_free[bk] = t_cp
                    rt_free[par] = t_add
                    done = (t_add, t_cp)
                else:
                    bias = 0.0
                    if qk_bias_cols is not None:
                        col = qk_bias_cols[tag] + h
                        bias = bias_sb[:, col:col + 1]
                    t_ev = s.add("scalar", (lambda b_, o_, bi: lambda sc:
                                 sc.activation(out=o_[:], in_=b_[:],
                                               func=AF.Identity, bias=bi,
                                               scale=1.0))(bank, ob, bias),
                                 deps=(mmtok, ost_free.get(oi)))
                    bank_free[bk] = t_ev
                    done = (t_ev,)
                t_st = s.add("sync", (lambda o_, h_, c_, od: lambda sy:
                             sy.dma_start(out=od[h_ * 128:(h_ + 1) * 128, c_],
                                          in_=o_[:]))(ob, h, cs, outd),
                             deps=done, inc=16)
                spar = gidx % 2
                t_sq = s.add("scalar", (lambda o_, q_: lambda sc:
                             sc.activation(out=q_[:], in_=o_[:],
                                           func=AF.Square, bias=0.0,
                                           scale=1.0))
                             (ob, sqb[spar]), deps=(done, sq_free.get(spar)))
                cell = (pi * NH + h) * NTT + tt
                pend_nrm.append((sqb[spar], cell, t_sq))
                if len(pend_nrm) > 1:
                    q_, ce_, tq_ = pend_nrm.pop(0)
                    t_nm = s.add("tensor", (lambda q2: lambda t:
                                 t.matmul(psX[0:1, :], C["onesbf"][:, 0:1],
                                          q2[:], start=True, stop=True))(q_),
                                 deps=(tq_, nrm_free))
                    t_nr = s.add("vector", (lambda ce2: lambda v:
                                 v.reduce_max(nrm[0:1, ce2:ce2 + 1],
                                              psX[0:1, 0:512],
                                              axis=AX.X))(ce_), deps=(t_nm,))
                    nrm_free = t_nr
                    sq_free[(0,)] = t_nm
                sq_free[spar] = sq_free.get((0,))
                ost_free[oi] = (t_st, t_sq)
                gidx += 1
            wb_free[(pi * NH + h) % 3] = s.barrier_token("tensor")

    while pend_nrm:
        q_, ce_, tq_ = pend_nrm.pop(0)
        t_nm = s.add("tensor", (lambda q2: lambda t:
                     t.matmul(psX[0:1, :], C["onesbf"][:, 0:1],
                              q2[:], start=True, stop=True))(q_),
                     deps=(tq_, nrm_free))
        t_nr = s.add("vector", (lambda ce2: lambda v:
                     v.reduce_max(nrm[0:1, ce2:ce2 + 1], psX[0:1, 0:512],
                                  axis=AX.X))(ce_), deps=(t_nm,))
        nrm_free = t_nr
    t_mr = []
    for cell in range(2 * NH):
        t_mr.append(s.add("vector", (lambda ce: lambda v:
                    v.reduce_max(mrow[0:1, ce:ce + 1],
                                 nrm[0:1, ce * NTT:(ce + 1) * NTT],
                                 axis=AX.X))(cell), deps=(nrm_free,)))
    t_p = s.add("vector", lambda v: v.tensor_tensor(
        crow[0:1, :], mrow[0:1, 0:NH], mrow[0:1, NH:2 * NH], op=ALU.mult),
        deps=t_mr)
    t_ln = s.add("scalar", lambda sc: sc.activation(
        out=crow[0:1, :], in_=crow[0:1, :], func=AF.Ln, bias=1e-30, scale=1.0),
        deps=(t_p,))
    t_ex = s.add("scalar", lambda sc: sc.activation(
        out=crow[0:1, :], in_=crow[0:1, :], func=AF.Exp, bias=0.0, scale=0.5),
        deps=(t_ln,))
    t_sc = s.add("vector", lambda v: v.tensor_scalar(
        out=crow[0:1, :], in0=crow[0:1, :],
        scalar1=float(-1.04 * scale_for_c), scalar2=-0.5, op0=ALU.mult,
        op1=ALU.add), deps=(t_ex,))
    t_cb = []
    for h in range(NH):
        t_b = s.add("tensor", (lambda h_: lambda t:
                    t.matmul(psX[:, 0:1], C["onesrowf"][0:1, :],
                             crow[0:1, h_:h_ + 1], start=True, stop=True))(h),
                    deps=(t_sc, t_cb[-1] if t_cb else None))
        t_e = s.add("scalar", (lambda h_: lambda sc:
                    sc.activation(out=cbias_out[:, h_:h_ + 1],
                                  in_=psX[:, 0:1], func=AF.Identity,
                                  bias=0.0, scale=1.0))(h), deps=(t_b,))
        t_cb.append(t_e)
    cb_tok = t_cb

    v_toks = []
    for vi, (wvT, Vd) in enumerate(v_list):
        vst = [p.sb(f"pj_vst{vi}_{i}", [128, 512], BF16) for i in range(4)]
        vst_free = {}
        for ot in range(NTT):
            vwtok = emit_wtile_load(p, wvT[ot], vwbuf[ot % 2], NKC * 512,
                                    deps=(wb_free.get(("v", vi, ot % 2)),))
            for tc in range(NTC):
                gg = ot * NTC + tc
                bk = gg % 6
                bank = banks[bk]
                pairs = [(xbf[:, k * T + tc * 128: k * T + tc * 128 + 128],
                          vwbuf[ot % 2][:, k * 512:(k + 1) * 512])
                         for k in range(NKC)]
                if v_bias_row is not None:
                    pairs.append((C["onesrow1"][0:1, :],
                                  v_bias_row[0:1, ot * 512:(ot + 1) * 512]))
                mmtok = emit_mm_group(p, bank, pairs,
                                      deps=(xtok, vwtok, bank_free.get(bk)))
                ob = vst[gg % 4]
                t_ev = s.add("scalar", (lambda b_, o_: lambda sc:
                             sc.activation(out=o_[:], in_=b_[:],
                                           func=AF.Identity, bias=0.0,
                                           scale=1.0))(bank, ob),
                             deps=(mmtok, vst_free.get(gg % 4)))
                bank_free[bk] = t_ev
                last = None
                for hh in range(4):
                    h = ot * 4 + hh
                    last = s.add("sync", (lambda o_, h_, tc_, hh_, vd: lambda sy:
                                 sy.dma_start(
                                     out=vd[h_, tc_ * 128:(tc_ + 1) * 128, :],
                                     in_=o_[:, hh_ * 128:(hh_ + 1) * 128]))
                                 (ob, h, tc, hh, Vd),
                                 deps=(t_ev,), inc=16)
                vst_free[gg % 4] = last
                v_toks.append(last)
            wb_free[("v", vi, ot % 2)] = s.barrier_token("tensor")
    return cb_tok


class PostLN:

    def __init__(self, p, name, xres_out, C, stats_banks, own_buffer=True):
        self.p = p
        self.name = name
        self.xres_out = xres_out
        self.C = C
        self.bk1, self.bk2 = stats_banks
        self.xafull = p.sb(f"{name}_xa", [128, NKC * 512], F32) \
            if own_buffer else None
        self.xabf = [p.sb(f"{name}_xb{i}", [128, 512], BF16) for i in range(2)]
        self.xsq = [p.sb(f"{name}_xs{i}", [128, 512], BF16) for i in range(2)]
        self.rows = p.sb(f"{name}_rows", [1, 512 * 4], F32)
        self.rbsb = p.sb(f"{name}_rb", [128, 512], F32)
        self.mbsb = p.sb(f"{name}_mb", [128, 512], F32)
        self._free = {}
        self._reset()

    def _reset(self):
        self.chunk_toks = {}
        self.stat_toks = []
        self._nchunk = 0

    def xa_slice(self, o):
        return self.xafull[:, o * 512:(o + 1) * 512]

    def xa_free(self, o):
        return self._free.get(("xa", o))

    def add_chunk(self, o, xa_tok, stats_dep=None, xa_ap=None):
        p, s = self.p, self.p.s
        if xa_ap is None:
            xa_ap = self.xa_slice(o)
        self.cur_ap = getattr(self, "cur_ap", {})
        self.cur_ap[o] = xa_ap
        par = self._nchunk % 2
        xb, xq = self.xabf[par], self.xsq[par]
        t_c = s.add("vector", (lambda a_, b_: lambda v:
                    v.tensor_copy(b_[:], a_))(xa_ap, xb),
                    deps=(xa_tok, self._free.get(("c", par))))
        t_q = s.add("scalar", (lambda b_, q_: lambda sc:
                    sc.activation(out=q_[:], in_=b_[:], func=AF.Square,
                                  bias=0.0, scale=1.0))(xb, xq),
                    deps=(t_c, self._free.get(("q", par))))
        first = self._nchunk == 0
        last = self._nchunk == NKC - 1
        dep1 = (t_c, stats_dep if first else None)
        t_s1 = s.add("tensor", (lambda b_, f, l: lambda t:
                     t.matmul(self.bk1[0:1, :], self.C["onesbf"][:, 0:1], b_[:],
                              start=f, stop=l))(xb, first, last), deps=dep1)
        t_s2 = s.add("tensor", (lambda q_, f, l: lambda t:
                     t.matmul(self.bk2[0:1, :], self.C["onesbf"][:, 0:1], q_[:],
                              start=f, stop=l))(xq, first, last),
                     deps=(t_q, stats_dep if first else None))
        self._free[("c", par)] = t_s1
        self._free[("q", par)] = t_s2
        self.chunk_toks[o] = xa_tok
        self.stat_toks = [t_s1, t_s2]
        self._nchunk += 1

    def finish(self, qt):
        p, s = self.p, self.p.s
        assert self._nchunk == NKC
        R = self.rows
        mrow = R[0:1, 0:512]
        e2row = R[0:1, 512:1024]
        trow = R[0:1, 1024:1536]
        vrow = R[0:1, 1536:2048]
        rrow = e2row
        mrrow = trow
        d1, d2 = self.stat_toks
        t_m = s.add("vector", lambda v: v.tensor_scalar(
            out=mrow, in0=self.bk1[0:1, :], scalar1=1.0 / D, scalar2=None,
            op0=ALU.mult), deps=(d1,))
        t_e = s.add("vector", lambda v: v.tensor_scalar(
            out=e2row, in0=self.bk2[0:1, :], scalar1=1.0 / D, scalar2=None,
            op0=ALU.mult), deps=(d2,))
        t_sq = s.add("vector", lambda v: v.tensor_tensor(
            trow, mrow, mrow, op=ALU.mult), deps=(t_m,))
        t_v = s.add("vector", lambda v: v.tensor_tensor(
            vrow, e2row, trow, op=ALU.subtract), deps=(t_e, t_sq))
        t_ln = s.add("scalar", lambda sc: sc.activation(
            out=trow, in_=vrow, func=AF.Ln, bias=EPS, scale=1.0), deps=(t_v,))
        t_r = s.add("scalar", lambda sc: sc.activation(
            out=rrow, in_=trow, func=AF.Exp, bias=0.0, scale=-0.5), deps=(t_ln,))
        t_mr = s.add("vector", lambda v: v.tensor_tensor(
            mrrow, mrow, rrow, op=ALU.mult), deps=(t_r, t_m))
        t_rb = s.add("tensor", lambda t: t.matmul(
            self.bk1[:, :], self.C["onesrowf"][0:1, :], rrow,
            start=True, stop=True), deps=(t_r, d1, d2))
        t_mb = s.add("tensor", lambda t: t.matmul(
            self.bk2[:, :], self.C["onesrowf"][0:1, :], mrrow,
            start=True, stop=True), deps=(t_mr,))
        t_rb = s.add("scalar", lambda sc: sc.activation(
            out=self.rbsb[:], in_=self.bk1[:, :], func=AF.Identity,
            bias=0.0, scale=1.0), deps=(t_rb, self._free.get("rbsb")))
        t_mb = s.add("scalar", lambda sc: sc.activation(
            out=self.mbsb[:], in_=self.bk2[:, :], func=AF.Identity,
            bias=0.0, scale=1.0), deps=(t_mb, self._free.get("mbsb")))
        last = None
        for o in range(NKC):
            xa_ap = self.cur_ap[o]
            t_1 = s.add("vector", (lambda a_: lambda v:
                        v.tensor_tensor(a_, a_, self.rbsb[:], op=ALU.mult))
                        (xa_ap), deps=(t_rb, self.chunk_toks[o]))
            t_2 = s.add("vector", (lambda a_: lambda v:
                        v.tensor_tensor(a_, a_, self.mbsb[:],
                                        op=ALU.subtract))(xa_ap),
                        deps=(t_mb, t_1))
            t_st = s.add("sync", (lambda o_, a_, q_: lambda sy:
                         sy.dma_start(
                             out=self.xres_out[o_ * 128:(o_ + 1) * 128,
                                               q_ * 512:(q_ + 1) * 512],
                             in_=a_))(o, xa_ap, qt),
                         deps=(t_2,), inc=16)
            self._free[("xa", o)] = t_st
            last = t_st
        if getattr(self.p, "dbg_rows", None) is not None and qt == 0:
            s.add("sync", lambda sy: sy.dma_start(
                out=self.p.dbg_rows, in_=R[0:1, :]), deps=(t_mr, t_m, t_e, t_v),
                inc=16)
        vb = self.p.s.barrier_token("vector")
        self._free["rbsb"] = vb
        self._free["mbsb"] = vb
        stats_free = (t_rb, t_mb)
        norm_done = last
        self._reset()
        return norm_done, stats_free


def emit_attn_phase(p, Qd, Kd, Vd, woT, xprev_dram, xres_out, causal,
                    scale, C, cbias, cb_tok, bo_col=None, bias_sb=None,
                    cmask_dram=None, deps=()):
    s = p.s
    p._atser = getattr(p, "_atser", 0) + 1
    WoC = p.dram_tmp(f"at_woc{p._atser}", [16, 128, NKC * 128], BF16)
    psS = p.psS
    psO_ = [p.psG[:, 0:512], p.psG[:, 512:1024]]
    psA_ = [p.psG[:, 0:512], p.psG[:, 512:1024]]
    psX = p.psX
    psY = p.psY

    qtile = [p.sb(f"at_q{i}", [128, 512], BF16) for i in range(3)]
    ktile = [p.sb(f"at_k{i}", [128, T], BF16) for i in range(3)]
    vtile = [p.sb(f"at_v{i}", [128, T], BF16) for i in range(3)]
    Pt = [p.sb(f"at_p{i}", [128, 1024], BF16) for i in range(2)]
    Ofsb = p.sb("at_of", [128, NH * 512], BF16)
    rvrow = [p.sb(f"at_rv{i}", [1, 512], F32) for i in range(2)]
    rvb = [p.sb(f"at_rb{i}", [128, 512], F32) for i in range(2)]
    wbuf = [p.sb(f"at_wb{i}", [128, NKC * 128], BF16) for i in range(2)]
    xprev = [p.sb(f"at_xp{i}", [128, 512], F32) for i in range(2)]
    cmasks = [p.sb(f"at_cm{r}", [128, 512], F32) for r in range(4)] \
        if causal else None

    pln = PostLN(p, "at_ln", xres_out, C, (psY, psX))

    free = {}
    t_done = None
    stats_free = None
    pend_norm = []

    def flush_norm():
        t_rv, t_pv_l, bi_, h_, psO = pend_norm.pop(0)
        t_rb = s.add("tensor", (lambda r_: lambda t: t.matmul(
            psY[:, :], C["onesrowf"][0:1, :], r_[0:1, :],
            start=True, stop=True))(rvrow[bi_ % 2]),
            deps=(t_rv, free.get("rvbank")))
        free[("rvrow", bi_ % 2)] = t_rb
        t_re = s.add("vector", (lambda b_: lambda v:
                     v.tensor_copy(b_[:], psY[:, :]))
                     (rvb[bi_ % 2]), deps=(t_rb, free.get(("rvb", bi_ % 2))))
        free["rvbank"] = t_re
        free["rvrow"] = t_rb
        t_oe = s.add("vector", (lambda h2, b_, po_: lambda v:
                     v.tensor_tensor(Ofsb[:, h2 * 512:(h2 + 1) * 512],
                                     po_, b_[:], op=ALU.mult))
                     (h_, rvb[bi_ % 2], psO),
                     deps=(t_pv_l, t_re, free.get("ofr")))
        free[("rvb", bi_ % 2)] = t_oe
        free[("O", bi_ % 2)] = t_oe
        free["O"] = t_oe

    if causal:
        cm_toks = []
        for r in range(4):
            cm_toks.append(s.add("sync", (lambda r_: lambda sy:
                           sy.dma_start(out=cmasks[r_][:],
                                        in_=cmask_dram[r_]))(r), inc=16))
        cmask_tok = tuple(cm_toks)
    else:
        cmask_tok = ()

    for qt in range(NTT):
        n_kc = (qt + 1) * 4 if causal else NKC
        KV = n_kc * 128
        for h in range(NH):
            bi = qt * NH + h
            t_q = s.add("sync", (lambda h_, q_, b_: lambda sy:
                        sy.dma_start(out=b_[:],
                                     in_=Qd[h_ * 128:(h_ + 1) * 128,
                                            q_ * 512:(q_ + 1) * 512]))
                        (h, qt, qtile[bi % 3]),
                        deps=(free.get(("q", bi % 3)), deps), inc=16)
            t_k = s.add("sync", (lambda h_, b_, w_: lambda sy:
                        sy.dma_start(out=b_[:, 0:w_],
                                     in_=Kd[h_ * 128:(h_ + 1) * 128, 0:w_]))
                        (h, ktile[bi % 3], KV),
                        deps=(free.get(("k", bi % 3)), deps), inc=16)
            t_v = s.add("sync", (lambda h_, b_, w_, nk_: lambda sy:
                        sy.dma_start(
                            out=b_[:, 0:w_].rearrange("p (c d) -> p c d", d=128),
                            in_=Vd[h_, 0:w_, :].rearrange("(c p) d -> p c d",
                                                          p=128)))
                        (h, vtile[bi % 3], KV, n_kc),
                        deps=(free.get(("v", bi % 3)), deps), inc=16)
            qb, kb, vb = qtile[bi % 3], ktile[bi % 3], vtile[bi % 3]
            psO = psO_[bi % 2]
            e_toks = {}

            t_pv_last = None
            t_sm_last = None

            def emit_s(kc):
                bk = kc % 4
                bank = psS[:, bk * 512:(bk + 1) * 512]
                t_s = s.add("tensor", (lambda b_, kb_, qb_, kc_: lambda t:
                            t.matmul(b_, kb_[:, kc_ * 128:(kc_ + 1) * 128],
                                     qb_[:], start=True, stop=True))
                            (bank, kb, qb, kc),
                            deps=(t_q, t_k, free.get(("S", bk))))
                if causal and (kc >= n_kc - 4):
                    r = kc - (n_kc - 4)
                    t_s = s.add("vector", (lambda b_, r_: lambda v:
                                v.tensor_tensor(b_, b_, cmasks[r_][:],
                                                op=ALU.add))(bank, r),
                                deps=(t_s, cmask_tok))
                e_toks[("s", kc)] = t_s
                if kc % 2 == 1:
                    pr = (kc // 2) % 2
                    pb = psS[:, (bk - 1) * 512:(bk + 1) * 512]
                    pt = Pt[pr]
                    t_e = s.add("scalar", (lambda b_, p_, h_: lambda sc:
                                sc.activation(out=p_[0:128, 0:1024], in_=b_,
                                              func=AF.Exp,
                                              bias=cbias[:, h_:h_ + 1],
                                              scale=float(scale)))(pb, pt, h),
                                deps=(e_toks[("s", kc - 1)], t_s, cb_tok,
                                      free.get(("P", pr))))
                    free[("S", bk)] = t_e
                    free[("S", bk - 1)] = t_e
                    e_toks[kc] = t_e
                    e_toks[kc - 1] = t_e

            for kc in range(min(4, n_kc)):
                emit_s(kc)
            for kc in range(n_kc):
                if kc + 4 < n_kc:
                    emit_s(kc + 4)
                pt = Pt[(kc // 2) % 2][:, (kc % 2) * 512:(kc % 2 + 1) * 512]
                t_e = e_toks[kc]
                t_pv = s.add("tensor", (lambda kc_, f, l, vb_, p_, po_:
                             lambda t:
                             t.matmul(po_, vb_[:, kc_ * 128:(kc_ + 1) * 128],
                                      p_, start=f, stop=l))
                             (kc, kc == 0, kc == n_kc - 1, vb, pt, psO),
                             deps=(t_e, t_v,
                                   free.get(("O", bi % 2)) if kc == 0
                                   else None))
                t_sm = s.add("tensor", (lambda f, l, p_: lambda t:
                             t.matmul(psX[0:1, :], C["onesbf"][:, 0:1],
                                      p_, start=f, stop=l))
                             (kc == 0, kc == n_kc - 1, pt),
                             deps=(t_e, free.get("N") if kc == 0 else None))
                if kc % 2 == 1:
                    free[("P", (kc // 2) % 2)] = \
                        (t_pv, t_sm, free.get(("Ph", (kc // 2) % 2)))
                else:
                    free[("Ph", (kc // 2) % 2)] = (t_pv, t_sm)
                t_pv_last, t_sm_last = t_pv, t_sm
            t_rv = s.add("vector", (lambda r_: lambda v: v.reciprocal(
                r_[0:1, :], psX[0:1, :]))(rvrow[bi % 2]),
                deps=(t_sm_last, free.get(("rvrow", bi % 2))))
            free["N"] = t_rv
            pend_norm.append((t_rv, t_pv_last, bi, h, psO))
            if len(pend_norm) > 1:
                flush_norm()
            free[("q", bi % 3)] = s.barrier_token("tensor")
            free[("k", bi % 3)] = s.barrier_token("tensor")
            free[("v", bi % 3)] = t_pv_last
        flush_norm()
        free["Olast"] = free["O"]

        for o in range(NKC):
            if qt == 0:
                t_w = emit_wtile_load(p, woT[o], wbuf[o % 2], NKC * 128,
                                      deps=(free.get(("wo", o % 2)),))
                t_w = s.add("sync", (lambda o_, b_: lambda sy:
                            sy.dma_start(out=WoC[o_], in_=b_[:]))
                            (o, wbuf[o % 2]), deps=(t_w,), inc=16)
            else:
                t_w = emit_wtile_load(p, WoC[o], wbuf[o % 2], NKC * 128,
                                      deps=(free.get(("wo", o % 2)),),
                                      queue="sync")
            t_xp = s.add("sync", (lambda o_, q_, b_: lambda sy:
                         sy.dma_start(out=b_[:],
                                      in_=xprev_dram[o_ * 128:(o_ + 1) * 128,
                                                     q_ * 512:(q_ + 1) * 512]))
                         (o, qt, xprev[o % 2]),
                         deps=(free.get(("xp", o % 2)),), inc=16)
            psA = psA_[o % 2]
            t_a = emit_mm_group(
                p, psA,
                [(wbuf[o % 2][:, k * 128:(k + 1) * 128],
                  Ofsb[:, k * 512:(k + 1) * 512]) for k in range(NKC)],
                deps=(t_w, free["O"], free.get(("A", o % 2)),
                      free.get("Olast")))
            free[("wo", o % 2)] = t_a
            free["ofr"] = t_a
            xa = pln.xa_slice(o)
            if bo_col is not None:
                t_xa = s.add("vector", (lambda o_, xa_, xp_, pa_: lambda v:
                             v.scalar_tensor_tensor(
                                 out=xa_, in0=pa_,
                                 scalar=bias_sb[:, bo_col + o_:bo_col + o_ + 1],
                                 in1=xp_[:], op0=ALU.add, op1=ALU.add))
                             (o, xa, xprev[o % 2], psA),
                             deps=(t_a, t_xp, pln.xa_free(o)))
            else:
                t_xa = s.add("vector", (lambda xa_, xp_, pa_: lambda v:
                             v.tensor_tensor(xa_, pa_, xp_[:], op=ALU.add))
                             (xa, xprev[o % 2], psA),
                             deps=(t_a, t_xp, pln.xa_free(o)))
            free[("A", o % 2)] = t_xa
            free[("xp", o % 2)] = t_xa
            sdep = (stats_free, free.get("rvbank"), free.get("N"))
            pln.add_chunk(o, t_xa, stats_dep=sdep)
        free["O"] = free.get(("A", 0))
        t_done, stats_free = pln.finish(qt)
        free["rvbank"] = (free.get("rvbank"), stats_free[0])
        free["N"] = (free.get("N"), stats_free[1])
    return t_done


def load_consts(p, want=("onesbf", "onesrowf")):
    shapes = {
        "onesbf": ([128, 1], BF16),
        "onesrowf": ([1, 128], F32),
        "onesrow1": ([1, 128], BF16),
        "ropeC": ([64, T], F32),
        "ropeS": ([64, T], F32),
        "identB": ([128, 128], BF16),
        "identF": ([128, 128], F32),
        "onehot": ([16, 16 * 128], BF16),
    }
    C = {}
    toks = []
    fcd = p.dram_in("c_fconsts", [128, len(FCONSTS)], F32)
    toks.append(p.s.add("sync", lambda sy: sy.dma_start(out=p.fc_sb[:],
                                                        in_=fcd), inc=16))
    for name in want:
        shape, dt = shapes[name]
        d = p.dram_in("c_" + name, shape, dt)
        sbh = p.sb("cs_" + name, shape, dt)
        toks.append(p.s.add("sync", (lambda d_, s_: lambda sy:
                    sy.dma_start(out=s_[:], in_=d_))(d, sbh), inc=16))
        C[name] = sbh
    for e in ENGINES:
        p.s.add(e, lambda eng: eng.nop(), deps=toks, inc=1)
    return C


def const_in_maps(consts, want):
    m = {("c_" + n): consts[n] for n in want}
    m["c_fconsts"] = consts["fconsts"]
    return m


P1_CONSTS = ("onesbf", "onesrowf", "ropeC", "ropeS")


def build_l0_proj():
    p = P()
    xT = p.dram_in("xT", [D, T])
    wqT = p.dram_in("wqT", [16, 128, D])
    wkT = p.dram_in("wkT", [16, 128, D])
    wvT = p.dram_in("wvT", [4, 128, NKC * 512])
    Qd = p.dram_out("Qd", [D, T], BF16)
    Kd = p.dram_out("Kd", [D, T], BF16)
    Vd = p.dram_out("Vd", [NH, T, HD], BF16)
    CB = p.dram_out("CB", [128, NH])
    C = load_consts(p, P1_CONSTS)
    cbias = p.sb("cbias", [128, NH], F32)
    cb_tok = emit_proj_phase(
        p, xT, [("q", wqT, Qd), ("k", wkT, Kd)], [(wvT, Vd)], C,
        ropeC_sb=C["ropeC"], ropeS_sb=C["ropeS"], cbias_out=cbias,
        scale_for_c=SCALE0)
    p.s.add("sync", lambda sy: sy.dma_start(out=CB, in_=cbias[:]),
            deps=cb_tok, inc=16)
    return p.finalize()


P2_CONSTS = ("onesbf", "onesrowf")


def build_l0_attn():
    p = P()
    Qd = p.dram_in("Qd", [D, T], BF16)
    Kd = p.dram_in("Kd", [D, T], BF16)
    Vd = p.dram_in("Vd", [NH, T, HD], BF16)
    woT = p.dram_in("woT", [16, 128, D])
    xT = p.dram_in("xT", [D, T])
    CB = p.dram_in("CB", [128, NH])
    x1 = p.dram_out("x1", [D, T])
    C = load_consts(p, P2_CONSTS)
    cbias = p.sb("cbias", [128, NH], F32)
    cb_tok = p.s.add("sync", lambda sy: sy.dma_start(out=cbias[:], in_=CB),
                     inc=16)
    emit_attn_phase(p, Qd, Kd, Vd, woT, xT, x1, causal=False, scale=SCALE0,
                    C=C, cbias=cbias, cb_tok=cb_tok)
    return p.finalize()


L0_BC = {"b0a": 0, "b0b": 32, "b1a": 48, "b1b": 80}
L0_BP_W = 96


def emit_l0_ffn(p, x1, WgT, W0aT, W0bT, W1aT, W1bT, bias_sb, x2, C, deps=()):
    WaC = [p.dram_tmp(f"f_wac{i}", [32, 128, NKC * 128], BF16) for i in range(2)]
    WbC = [p.dram_tmp(f"f_wbc{i}", [16, 128, 32 * 128], BF16) for i in range(2)]
    """L0 FFN: hard-gated 2-expert MLP (gelu / relu) + residual + LN -> x2.
    Processed per 512-token tile; weights re-streamed per tile."""
    s = p.s
    psH = [p.psS[:, i * 512:(i + 1) * 512] for i in range(4)]
    psO_ = [p.psG[:, 0:512], p.psG[:, 512:1024]]
    psX = p.psX

    xbf = p.sb("f_xbf", [128, NKC * 512], BF16)
    xf32 = [p.sb(f"f_xf{i}", [128, 512], F32) for i in range(2)]
    xprev = [p.sb(f"f_xp{i}", [128, 512], F32) for i in range(2)]
    harena = p.sb("f_h", [128, 32 * 512], BF16)
    acc = p.sb("f_acc", [128, NKC * 512], F32)
    wab = [p.sb(f"f_wa{i}", [128, NKC * 128], BF16) for i in range(2)]
    wbb = [p.sb(f"f_wb{i}", [128, 32 * 128], BF16) for i in range(2)]
    wg = p.sb("f_wg", [128, 32], F32)
    wgd = p.sb("f_wgd", [128, 16], F32)
    mrow = p.sb("f_mrow", [1, 512], F32)
    mb = [p.sb(f"f_mb{i}", [128, 512], F32) for i in range(2)]
    m1b = [p.sb(f"f_m1b{i}", [128, 512], F32) for i in range(2)]
    htmp = [p.sb(f"f_ht{i}", [128, 512], F32) for i in range(2)]
    bd = p.sb("f_bd", [128, 16], F32)

    psY = p.psY
    pln = PostLN(p, "f_ln", x2, C, (psX, psY), own_buffer=False)

    t_wg = s.add("sync", lambda sy: sy.dma_start(
        out=wg[:].rearrange("p (c e) -> p c e", e=2),
        in_=chunk3(WgT).rearrange("c p e -> p c e")), deps=deps, inc=16)
    wgv = wg[:].rearrange("p (c e) -> p c e", e=2)
    t_wgd = s.add("vector", lambda v: v.tensor_tensor(
        wgd[:], wgv[:, :, 0], wgv[:, :, 1], op=ALU.subtract), deps=(t_wg,))
    t_bd = s.add("vector", lambda v: v.tensor_tensor(
        bd[:], bias_sb[:, L0_BC["b0b"]:L0_BC["b0b"] + 16],
        bias_sb[:, L0_BC["b1b"]:L0_BC["b1b"] + 16], op=ALU.subtract),
        deps=deps)

    free = {}
    stats_free = None
    t_done = None
    gate_toks = {}

    def emit_gate(tt):
        par = tt % 2
        mm_d = None
        for kc in range(NKC):
            t_x = s.add("sync", (lambda kc_, tt_, b_: lambda sy:
                        sy.dma_start(out=b_[:],
                                     in_=x1[kc_ * 128:(kc_ + 1) * 128,
                                            tt_ * 512:(tt_ + 1) * 512]))
                        (kc, tt, xf32[kc % 2]),
                        deps=(free.get(("xf", kc % 2)), deps), inc=16)
            mm_d = s.add("tensor", (lambda kc_, b_, f, l: lambda t:
                         t.matmul(psX[0:1, :], wgd[:, kc_:kc_ + 1], b_[:],
                                  start=f, stop=l))
                         (kc, xf32[kc % 2], kc == 0, kc == NKC - 1),
                         deps=(t_x, t_wgd,
                               free.get("psXg") if kc == 0 else None))
            free[("xf", kc % 2)] = mm_d
        t_m = s.add("vector", lambda v: v.tensor_scalar(
            out=mrow[0:1, :], in0=psX[0:1, :], scalar1=0.0, scalar2=None,
            op0=ALU.is_ge), deps=(mm_d, free.get("mrow")))
        t_b1 = s.add("tensor", lambda t: t.matmul(
            psX[:, :], C["onesrowf"][0:1, :], mrow[0:1, :],
            start=True, stop=True), deps=(t_m,))
        t_e1 = s.add("scalar", (lambda p_: lambda sc:
                     sc.activation(out=p_[:], in_=psX[:, :], func=AF.Identity,
                                   bias=0.0, scale=1.0))(mb[par]),
                     deps=(t_b1, free.get(("mb", par))))
        t_e2 = s.add("scalar", (lambda p_: lambda sc:
                     sc.activation(out=p_[:], in_=psX[:, :], func=AF.Identity,
                                   bias=1.0, scale=-1.0))(m1b[par]),
                     deps=(t_b1, free.get(("m1b", par))))
        free["psXg"] = (t_e1, t_e2)
        free["mrow"] = t_b1
        gate_toks[tt] = (t_e1, t_e2)

    emit_gate(0)
    for tt in range(NTT):
        par = tt % 2
        for o in range(NKC):
            t_xp = s.add("sync", (lambda o_, tt_, b_: lambda sy:
                         sy.dma_start(out=b_[:],
                                      in_=x1[o_ * 128:(o_ + 1) * 128,
                                             tt_ * 512:(tt_ + 1) * 512]))
                         (o, tt, xprev[o % 2]),
                         deps=(free.get(("xp", o % 2)),), inc=16)
            t_t1 = s.add("vector", (lambda o_, b_, p_: lambda v:
                         v.tensor_scalar(out=b_[:], in0=p_[:],
                                         scalar1=bd[:, o_:o_ + 1],
                                         scalar2=None, op0=ALU.mult))
                         (o, htmp[o % 2], mb[par]),
                         deps=(gate_toks[tt][0], t_bd,
                               free.get(("htmp", o % 2))))
            acc_sl = acc[:, o * 512:(o + 1) * 512]
            t_init = s.add("vector", (lambda o_, a_, xp_, ti_: lambda v:
                           v.scalar_tensor_tensor(
                               out=a_, in0=xp_[:],
                               scalar=bias_sb[:, L0_BC["b1b"] + o_:
                                              L0_BC["b1b"] + o_ + 1],
                               in1=ti_[:], op0=ALU.add, op1=ALU.add))
                           (o, acc_sl, xprev[o % 2], htmp[o % 2]),
                           deps=(t_xp, t_t1, free.get(("acci", o))))
            free[("xp", o % 2)] = t_init
            free[("htmp", o % 2)] = t_init
            free[("acc", o)] = t_init
        xtok = emit_load_xbf(p, x1, xbf, cols=(tt * 512, 512),
                             deps=(deps, free.get("xbf")))
        for ex in range(2):
            WaT = W0aT if ex == 0 else W1aT
            WbT = W0bT if ex == 0 else W1bT
            act = AF.Gelu if ex == 0 else AF.Relu
            bcol = L0_BC["b0a"] if ex == 0 else L0_BC["b1a"]
            mbx = mb[tt % 2] if ex == 0 else m1b[tt % 2]
            for f in range(32):
                t_w = emit_wtile_load(p, WaT[f], wab[f % 2], NKC * 128,
                                      deps=(free.get(("wa", f % 2)),))
                bank = psH[f % 4]
                mmtok = emit_mm_group(
                    p, bank,
                    [(wab[f % 2][:, k * 128:(k + 1) * 128],
                      xbf[:, k * 512:(k + 1) * 512]) for k in range(NKC)],
                    deps=(xtok, t_w, free.get(("hb", f % 4))))
                t_hv = s.add("scalar", (lambda b_, h_, f_, a_, bc_: lambda sc:
                             sc.activation(out=h_[:], in_=b_, func=a_,
                                           bias=bias_sb[:, bc_ + f_:
                                                        bc_ + f_ + 1],
                                           scale=1.0))
                             (bank, htmp[f % 2], f, act, bcol),
                             deps=(mmtok, free.get(("htmp", f % 2))))
                free[("hb", f % 4)] = t_hv
                hsl = harena[:, f * 512:(f + 1) * 512]
                t_hm = s.add("vector", (lambda h_, m_, o_: lambda v:
                             v.tensor_tensor(o_, h_[:], m_[:], op=ALU.mult))
                             (htmp[f % 2], mbx, hsl),
                             deps=(t_hv, gate_toks[tt][ex],
                                   free.get(("hs", f))))
                free[("htmp", f % 2)] = t_hm
                free[("h", f)] = t_hm
                free[("wa", f % 2)] = s.barrier_token("tensor")
            if ex == 1:
                free["xbf"] = s.barrier_token("tensor")
            if ex == 0 and tt + 1 < NTT:
                emit_gate(tt + 1)
            for o in range(NKC):
                if tt == 0:
                    t_w = emit_wtile_load(p, WbT[o], wbb[o % 2], 32 * 128,
                                          deps=(free.get(("wb", o % 2)),))
                    t_w = s.add("sync", (lambda o_, b_, ex_: lambda sy:
                                sy.dma_start(out=WbC[ex_][o_],
                                             in_=b_[:]))(o, wbb[o % 2], ex),
                                deps=(t_w,), inc=16)
                else:
                    t_w = emit_wtile_load(p, WbC[ex][o], wbb[o % 2], 32 * 128,
                                          deps=(free.get(("wb", o % 2)),),
                                          queue="sync")
                bank = psO_[o % 2]
                mmtok = emit_mm_group(
                    p, bank,
                    [(wbb[o % 2][:, k * 128:(k + 1) * 128],
                      harena[:, k * 512:(k + 1) * 512]) for k in range(32)],
                    deps=[t_w, free.get(("ob", o % 2))] +
                         [free.get(("h", k)) for k in range(32)])
                acc_sl = acc[:, o * 512:(o + 1) * 512]
                t_add = s.add("vector", (lambda a_, b_: lambda v:
                              v.tensor_tensor(a_, a_, b_, op=ALU.add))
                              (acc_sl, bank),
                              deps=(mmtok, free.get(("acc", o))))
                free[("ob", o % 2)] = t_add
                free[("acc", o)] = t_add
                free[("wb", o % 2)] = s.barrier_token("tensor")
            if getattr(p, "dbg_h0", None) is not None:
                dst = p.dbg_h0 if ex == 0 else p.dbg_h
                for f in range(32):
                    s.add("sync", (lambda f_, a_, d_, t2: lambda sy:
                          sy.dma_start(out=d_[f_ * 128:(f_ + 1) * 128,
                                              t2 * 512:(t2 + 1) * 512],
                                       in_=a_))(f,
                          harena[:, f * 512:(f + 1) * 512], dst, tt),
                          deps=(free[("h", f)],), inc=16)
            bt = s.barrier_token("tensor")
            for f in range(32):
                free[("hs", f)] = bt
        if getattr(p, "dbg_acc", None) is not None:
            for o in range(NKC):
                s.add("sync", (lambda o_, t2, a_: lambda sy:
                      sy.dma_start(out=p.dbg_acc[o_ * 128:(o_ + 1) * 128,
                                                 t2 * 512:(t2 + 1) * 512],
                                   in_=a_))(o, tt,
                      acc[:, o * 512:(o + 1) * 512]),
                      deps=(free[("acc", o)],), inc=16)
        for o in range(NKC):
            acc_sl = acc[:, o * 512:(o + 1) * 512]
            sdep = (stats_free, free.get("psXg"))
            pln.add_chunk(o, free[("acc", o)], stats_dep=sdep, xa_ap=acc_sl)
        t_done, stats_free = pln.finish(tt)
        free["psXg"] = stats_free[0]
        free["psY"] = stats_free[1]
        for o in range(NKC):
            free[("acci", o)] = pln.xa_free(o)
            free.pop(("acc", o), None)
        free[("mb", tt % 2)] = s.barrier_token("vector")
        free[("m1b", tt % 2)] = s.barrier_token("vector")
    return t_done


P3_CONSTS = ("onesbf", "onesrowf")


def build_l0_ffn(debug=False):
    p = P()
    x1 = p.dram_in("x1", [D, T])
    WgT = p.dram_in("WgT", [D, 2])
    W0aT = p.dram_in("W0aT", [32, 128, D])
    W0bT = p.dram_in("W0bT", [16, 128, 32 * 128])
    W1aT = p.dram_in("W1aT", [32, 128, D])
    W1bT = p.dram_in("W1bT", [16, 128, 32 * 128])
    BP = p.dram_in("BP", [128, L0_BP_W])
    x2 = p.dram_out("x2", [D, T])
    if debug:
        p.dbg_acc = p.dram_out("dbg_acc", [D, T])
        p.dbg_h = p.dram_out("dbg_h", [2 * D, T], BF16)
        p.dbg_rows = p.dram_out("dbg_rows", [1, 2048])
        p.dbg_h0 = p.dram_out("dbg_h0", [2 * D, T], BF16)
    C = load_consts(p, P3_CONSTS)
    bias_sb = p.sb("biasp", [128, L0_BP_W], F32)
    t_bp = p.s.add("sync", lambda sy: sy.dma_start(out=bias_sb[:], in_=BP),
                   inc=16)
    emit_l0_ffn(p, x1, WgT, W0aT, W0bT, W1aT, W1bT, bias_sb, x2, C,
                deps=(t_bp,))
    return p.finalize()


L1_BC = {"q": 0, "k": 16, "bo": 32}
L1_BP_W = 48
P4_CONSTS = ("onesbf", "onesrowf", "onesrow1")


def build_l1_proj():
    p = P()
    x2 = p.dram_in("x2", [D, T])
    WqT = p.dram_in("WqT", [16, 128, D])
    WkT = p.dram_in("WkT", [16, 128, D])
    WvT = p.dram_in("WvT", [4, 128, NKC * 512])
    BP = p.dram_in("BP", [128, L1_BP_W])
    bvd = p.dram_in("bv", [1, D])
    Qd = p.dram_out("Qd", [D, T], BF16)
    Kd = p.dram_out("Kd", [D, T], BF16)
    Vd = p.dram_out("Vd", [NH, T, HD], BF16)
    CB = p.dram_out("CB", [128, NH])
    C = load_consts(p, P4_CONSTS)
    bias_sb = p.sb("biasp", [128, L1_BP_W], F32)
    bvrow = p.sb("bvrow", [1, D], BF16)
    t_bp = p.s.add("sync", lambda sy: sy.dma_start(out=bias_sb[:], in_=BP),
                   inc=16)
    t_bv = p.s.add("gpsimd", lambda g: g.dma_start(out=bvrow[0:1, :], in_=bvd),
                   inc=16)
    cbias = p.sb("cbias", [128, NH], F32)
    for e in ENGINES:
        p.s.add(e, lambda eng: eng.nop(), deps=(t_bp, t_bv), inc=1)
    emit_proj_phase(
        p, x2, [("q", WqT, Qd), ("k", WkT, Kd)], [(WvT, Vd)], C,
        bias_sb=bias_sb, qk_bias_cols={"q": L1_BC["q"], "k": L1_BC["k"]},
        v_bias_row=bvrow, cbias_out=cbias, scale_for_c=SCALE1)
    p.s.add("sync", lambda sy: sy.dma_start(out=CB, in_=cbias[:]),
            deps=(p.s.barrier_token("scalar"),), inc=16)
    return p.finalize()


P5_CONSTS = ("onesbf", "onesrowf")


def build_l1_attn():
    p = P()
    Qd = p.dram_in("Qd", [D, T], BF16)
    Kd = p.dram_in("Kd", [D, T], BF16)
    Vd = p.dram_in("Vd", [NH, T, HD], BF16)
    WoT = p.dram_in("WoT", [16, 128, D])
    x2 = p.dram_in("x2", [D, T])
    CB = p.dram_in("CB", [128, NH])
    BP = p.dram_in("BP", [128, L1_BP_W])
    cmk = p.dram_in("cmk", [4, 128, 512])
    x3 = p.dram_out("x3", [D, T])
    C = load_consts(p, P5_CONSTS)
    bias_sb = p.sb("biasp", [128, L1_BP_W], F32)
    t_bp = p.s.add("sync", lambda sy: sy.dma_start(out=bias_sb[:], in_=BP),
                   inc=16)
    cbias = p.sb("cbias", [128, NH], F32)
    cb_tok = p.s.add("sync", lambda sy: sy.dma_start(out=cbias[:], in_=CB),
                     inc=16)
    for e in ENGINES:
        p.s.add(e, lambda eng: eng.nop(), deps=(t_bp,), inc=1)
    emit_attn_phase(p, Qd, Kd, Vd, WoT, x2, x3, causal=True, scale=SCALE1,
                    C=C, cbias=cbias, cb_tok=cb_tok, bo_col=L1_BC["bo"],
                    bias_sb=bias_sb, cmask_dram=[cmk[r] for r in range(4)])
    return p.finalize()


P6_BC = {"sb1": 0, "rb1": 22, "sb2": 198, "rbias": 230}
P6_BP_W = 231
P6_CONSTS = ("onesbf", "onesrowf", "identF", "onehot")
DFF = 1408
NFC = 11


def emit_l1_moe(p, x3, sw1, sw2, rw1, rw2, rb2, centT, bias_sb, yT, C,
                deps=()):
    s = p.s
    psH = [p.psS[:, i * 512:(i + 1) * 512] for i in range(3)]
    psT = p.psS[:, 1536:2048]
    psO_ = [p.psG[:, 0:512], p.psG[:, 512:1024]]
    psX = p.psX

    xbf = p.sb("m_xbf", [128, NKC * 1024], BF16)
    xf32 = [p.sb(f"m_xf{i}", [128, 512], F32) for i in range(2)]
    xprev = xf32
    hpair = p.sb("m_h", [128, 2 * NFC * 1024], BF16)
    acc = p.sb("m_acc", [128, NKC * 1024], F32)
    w1b = [p.sb(f"m_w1{i}", [128, NKC * 128], BF16) for i in range(2)]
    w2b = [p.sb(f"m_w2{i}", [128, NFC * 128], BF16) for i in range(4)]
    centsb = p.sb("m_cent", [128, NKC * 16], F32)
    scor = p.sb("m_scor", [16, 512], F32)
    stok = p.sb("m_stok", [128, 16], F32)
    cmpb = p.sb("m_cmp", [128, 256], BF16)
    rnk = p.sb("m_rnk", [128, 64], BF16)
    gtok = p.sb("m_gtok", [128, 16], F32)
    grow = p.sb("m_grow", [128, 2], F32)
    ge16b = [p.sb(f"m_geb{i}", [16, 1024], BF16) for i in range(2)]
    rb2b = p.sb("m_rb2", [16, 2048], BF16)
    b2s = p.sb("m_b2s", [128, 16], F32)
    gb = [[p.sb(f"m_gb{i}_{t}", [128, 512], BF16) for t in range(2)]
          for i in range(2)]
    htmp = [p.sb(f"m_ht{i}", [128, 512], BF16) for i in range(2)]

    psY = p.psY
    pln = PostLN(p, "m_ln", yT, C, (psX, psY), own_buffer=False)

    t_ct = s.add("sync", lambda sy: sy.dma_start(
        out=centsb[:].rearrange("p (c e) -> p c e", e=16),
        in_=chunk3(centT).rearrange("c p e -> p c e")), deps=deps, inc=16)
    t_r2 = s.add("gpsimd", lambda g: g.dma_start(out=rb2b[:], in_=rb2),
                 deps=deps, inc=16)
    t_b2 = s.add("vector", lambda v: v.tensor_tensor(
        b2s[:], bias_sb[:, P6_BC["sb2"]:P6_BC["sb2"] + 16],
        bias_sb[:, P6_BC["sb2"] + 16:P6_BC["sb2"] + 32], op=ALU.add),
        deps=deps)

    free = {}
    stats_free = None
    t_done = None

    def emit_router(half):
        geb = ge16b[half]
        ge_toks = []
        for tth in range(2):
            tt = half * 2 + tth
            mm_d = None
            for kc in range(NKC):
                t_x = s.add("sync", (lambda kc_, tt_, b_: lambda sy:
                            sy.dma_start(out=b_[:],
                                         in_=x3[kc_ * 128:(kc_ + 1) * 128,
                                                tt_ * 512:(tt_ + 1) * 512]))
                            (kc, tt, xf32[kc % 2]),
                            deps=(free.get(("xf", kc % 2)), deps), inc=16)
                mm_d = s.add("tensor", (lambda kc_, b_, f, l: lambda t:
                             t.matmul(psX[0:16, :],
                                      centsb[:, kc_ * 16:(kc_ + 1) * 16],
                                      b_[:], start=f, stop=l))
                             (kc, xf32[kc % 2], kc == 0, kc == NKC - 1),
                             deps=(t_x, t_ct,
                                   free.get("psXr") if kc == 0 else None))
                free[("xf", kc % 2)] = mm_d
            t_sg = s.add("scalar", lambda sc: sc.activation(
                out=scor[:], in_=psX[0:16, :], func=AF.Sigmoid, bias=0.0,
                scale=1.0), deps=(mm_d, free.get("scor")))
            free["psXr"] = t_sg
            t_sb = s.add("vector", lambda v: v.tensor_scalar(
                out=scor[:], in0=scor[:],
                scalar1=bias_sb[0:16, P6_BC["rbias"]:P6_BC["rbias"] + 1],
                scalar2=None, op0=ALU.add), deps=(t_sg, deps))
            for tc in range(4):
                t_tr = s.add("tensor", (lambda tc_: lambda t:
                             t.transpose(psT[:, 0:16],
                                         scor[0:16, tc_ * 128:(tc_ + 1) * 128],
                                         C["identF"][0:16, 0:16]))(tc),
                             deps=(t_sb, free.get("psT")))
                t_st = s.add("vector", lambda v: v.tensor_copy(
                    stok[:], psT[:, 0:16]), deps=(t_tr, free.get("stok")))
                free["psT"] = t_st
                cps = []
                for e in range(16):
                    cps.append(s.add("vector", (lambda e_: lambda v:
                               v.tensor_scalar(
                                   out=cmpb[:, e_ * 16:(e_ + 1) * 16],
                                   in0=stok[:], scalar1=stok[:, e_:e_ + 1],
                                   scalar2=None, op0=ALU.is_lt))(e),
                               deps=(t_st, free.get("cmpb"))))
                t_a1 = s.add("vector", lambda v: v.tensor_tensor(
                    rnk[:, 0:64], cmpb[:, 0:64], cmpb[:, 64:128], op=ALU.add),
                    deps=cps[0:8])
                t_a2 = s.add("vector", lambda v: v.tensor_tensor(
                    rnk[:, 0:64], rnk[:, 0:64], cmpb[:, 128:192], op=ALU.add),
                    deps=(t_a1,) + tuple(cps[8:12]))
                t_a3 = s.add("vector", lambda v: v.tensor_tensor(
                    rnk[:, 0:64], rnk[:, 0:64], cmpb[:, 192:256], op=ALU.add),
                    deps=(t_a2,) + tuple(cps[12:16]))
                free["cmpb"] = t_a3
                t_a4 = s.add("vector", lambda v: v.tensor_tensor(
                    rnk[:, 0:32], rnk[:, 0:32], rnk[:, 32:64], op=ALU.add),
                    deps=(t_a3,))
                t_a5 = s.add("vector", lambda v: v.tensor_tensor(
                    rnk[:, 0:16], rnk[:, 0:16], rnk[:, 16:32], op=ALU.add),
                    deps=(t_a4,))
                t_k = s.add("vector", lambda v: v.tensor_scalar(
                    out=rnk[:, 16:32], in0=rnk[:, 0:16], scalar1=4.0,
                    scalar2=None, op0=ALU.is_lt), deps=(t_a5,))
                t_g0 = s.add("vector", lambda v: v.tensor_tensor(
                    gtok[:], stok[:], rnk[:, 16:32], op=ALU.mult),
                    deps=(t_k, free.get("gtok")))
                t_gs = s.add("vector", lambda v: v.reduce_sum(
                    grow[:, 0:1], gtok[:], axis=AX.X), deps=(t_g0,))
                t_gc = s.add("vector", lambda v: v.tensor_scalar(
                    out=grow[:, 0:1], in0=grow[:, 0:1], scalar1=1e-8,
                    scalar2=None, op0=ALU.max), deps=(t_gs,))
                t_gr = s.add("vector", lambda v: v.reciprocal(
                    grow[:, 1:2], grow[:, 0:1]), deps=(t_gc,))
                t_gn = s.add("vector", lambda v: v.tensor_scalar(
                    out=gtok[:], in0=gtok[:], scalar1=grow[:, 1:2],
                    scalar2=None, op0=ALU.mult), deps=(t_gr,))
                t_tb = s.add("tensor", (lambda: lambda t:
                             t.transpose(psT[0:16, 16:144], gtok[:],
                                         C["identF"][:, :]))(),
                             deps=(t_gn, free.get("psT2")))
                col = tth * 512 + tc * 128
                t_ev = s.add("scalar", (lambda c_, g_: lambda sc:
                             sc.activation(out=g_[0:16, c_:c_ + 128],
                                           in_=psT[0:16, 16:144],
                                           func=AF.Identity, bias=0.0,
                                           scale=1.0))(col, geb),
                             deps=(t_tb, free.get(("ge16", half))))
                free["psT2"] = t_ev
                free["stok"] = cps
                free["gtok"] = t_tb
                ge_toks.append(t_ev)
            free["scor"] = s.barrier_token("tensor")
        return tuple(ge_toks)

    ge_done = [emit_router(0), emit_router(1)]
    for half in range(2):
        th0 = half * 1024
        t_geb = ge_done[half]
        xtok = emit_load_xbf(p, x3, xbf, cols=(th0, 1024),
                             deps=(deps, free.get("xbf")))
        for tth in range(2):
            tt = half * 2 + tth
            for o in range(NKC):
                t_xp = s.add("sync", (lambda o_, tt_, b_: lambda sy:
                             sy.dma_start(out=b_[:],
                                          in_=x3[o_ * 128:(o_ + 1) * 128,
                                                 tt_ * 512:(tt_ + 1) * 512]))
                             (o, tt, xprev[o % 2]),
                             deps=(free.get(("xf", o % 2)),), inc=16)
                acc_sl = acc[:, o * 1024 + tth * 512:
                             o * 1024 + tth * 512 + 512]
                t_init = s.add("vector", (lambda o_, a_, xp_: lambda v:
                               v.tensor_scalar(
                                   out=a_, in0=xp_[:], scalar1=2.0,
                                   scalar2=b2s[:, o_:o_ + 1],
                                   op0=ALU.mult, op1=ALU.add))
                               (o, acc_sl, xprev[o % 2]),
                               deps=(t_xp, t_b2, free.get(("acci", o, tth))))
                free[("xf", o % 2)] = t_init
                free[("acc", o, tth)] = t_init

        for pi in range(9):
            members = []
            if pi == 0:
                members = [("s", 0), ("s", 1)]
            else:
                members = [("r", (pi - 1) * 2), ("r", (pi - 1) * 2 + 1)]
            gb_toks = {}
            for mi, (kind, j) in enumerate(members):
                if kind == "r":
                    for tth in range(2):
                        t_gb = s.add("tensor", (lambda j_, t2, g_: lambda t:
                                     t.matmul(psT[:, 0:512],
                                              C["onehot"][0:16,
                                                          j_ * 128:(j_ + 1) * 128],
                                              g_[0:16,
                                                 t2 * 512:(t2 + 1) * 512],
                                              start=True, stop=True))
                                     (j, tth, ge16b[half]),
                                     deps=(t_geb, free.get("psT2")))
                        t_ge = s.add("scalar", (lambda b_: lambda sc:
                                     sc.activation(out=b_[:], in_=psT[:, 0:512],
                                                   func=AF.Identity, bias=0.0,
                                                   scale=1.0))
                                     (gb[mi][tth]), deps=(t_gb,
                                                     free.get(("gb", mi, tth))))
                        free["psT2"] = t_ge
                        gb_toks[(mi, tth)] = t_ge
            for mi, (kind, j) in enumerate(members):
                if kind == "s":
                    w1ap = sw1[j]
                    bcol = P6_BC["sb1"] + j * NFC
                else:
                    w1ap = rw1[j]
                    bcol = P6_BC["rb1"] + j * NFC
                for f in range(NFC):
                    t_w = emit_wtile_load(p, w1ap[f], w1b[f % 2], NKC * 128,
                                          deps=(free.get(("w1", f % 2)),))
                    for tth in range(2):
                        g = f * 2 + tth
                        bank = psH[g % 3]
                        mmtok = emit_mm_group(
                            p, bank,
                            [(w1b[f % 2][:, k * 128:(k + 1) * 128],
                              xbf[:, k * 1024 + tth * 512:
                                  k * 1024 + tth * 512 + 512])
                             for k in range(NKC)],
                            deps=(xtok, t_w, free.get(("hb", g % 3))))
                        hsl = hpair[:, (mi * NFC + f) * 1024 + tth * 512:
                                    (mi * NFC + f) * 1024 + tth * 512 + 512]
                        if kind == "s":
                            t_hv = s.add("scalar",
                                         (lambda b_, o_, bc_, f_: lambda sc:
                                          sc.activation(
                                              out=o_, in_=b_, func=AF.Relu,
                                              bias=bias_sb[:, bc_ + f_:
                                                           bc_ + f_ + 1],
                                              scale=1.0))
                                         (bank, hsl, bcol, f),
                                         deps=(mmtok,
                                               free.get(("hs", mi, f, tth))))
                            free[("hb", g % 3)] = t_hv
                            t_hm = t_hv
                        else:
                            t_hv = s.add("scalar",
                                         (lambda b_, h_, bc_, f_: lambda sc:
                                          sc.activation(
                                              out=h_[:], in_=b_, func=AF.Relu,
                                              bias=bias_sb[:, bc_ + f_:
                                                           bc_ + f_ + 1],
                                              scale=1.0))
                                         (bank, htmp[g % 2], bcol, f),
                                         deps=(mmtok,
                                               free.get(("htmp", g % 2))))
                            free[("hb", g % 3)] = t_hv
                            t_hm = s.add("vector",
                                         (lambda h_, m_, o_: lambda v:
                                          v.tensor_tensor(o_, h_[:], m_[:],
                                                          op=ALU.mult))
                                         (htmp[g % 2], gb[mi][tth], hsl),
                                         deps=(t_hv, gb_toks[(mi, tth)],
                                               free.get(("hs", mi, f, tth))))
                            free[("htmp", g % 2)] = t_hm
                        free[("h", mi, f, tth)] = t_hm
                    free[("w1", f % 2)] = s.barrier_token("tensor")
            for o in range(NKC):
                wtoks = []
                for mi, (kind, j) in enumerate(members):
                    w2ap = sw2[j] if kind == "s" else rw2[j]
                    wtoks.append(emit_wtile_load(
                        p, w2ap[o], w2b[2 * (o % 2) + mi], NFC * 128,
                        deps=(free.get(("w2", 2 * (o % 2) + mi)),)))
                for tth in range(2):
                    tt = half * 2 + tth
                    bank = psO_[(o * 2 + tth) % 2]
                    pairs = []
                    for mi in range(2):
                        pairs += [(w2b[2 * (o % 2) + mi][:, k * 128:(k + 1) * 128],
                                   hpair[:, (mi * NFC + k) * 1024 + tth * 512:
                                         (mi * NFC + k) * 1024 + tth * 512 + 512])
                                  for k in range(NFC)]
                    if pi == 0:
                        pairs.append((rb2b[0:16, o * 128:(o + 1) * 128],
                                      ge16b[half][0:16,
                                            tth * 512:(tth + 1) * 512]))
                    hdeps = [free.get(("h", mi, k, tth))
                             for mi in range(2) for k in range(NFC)]
                    mmtok = emit_mm_group(
                        p, bank, pairs,
                        deps=[wtoks[0], wtoks[1], t_r2, t_geb,
                              free.get(("ob", (o * 2 + tth) % 2))] + hdeps)
                    acc_sl = acc[:, o * 1024 + tth * 512:
                                 o * 1024 + tth * 512 + 512]
                    t_add = s.add("vector", (lambda a_, b_: lambda v:
                                  v.tensor_tensor(a_, a_, b_, op=ALU.add))
                                  (acc_sl, bank),
                                  deps=(mmtok, free.get(("acc", o, tth))))
                    free[("ob", (o * 2 + tth) % 2)] = t_add
                    free[("acc", o, tth)] = t_add
                for mi in range(2):
                    free[("w2", 2 * (o % 2) + mi)] = s.barrier_token("tensor")
            bt = s.barrier_token("tensor")
            for mi in range(2):
                for f in range(NFC):
                    for tth in range(2):
                        free[("hs", mi, f, tth)] = bt
                for tth in range(2):
                    free[("gb", mi, tth)] = bt
            if pi == 8:
                free["xbf"] = s.barrier_token("tensor")

        for tth in range(2):
            tt = half * 2 + tth
            for o in range(NKC):
                acc_sl = acc[:, o * 1024 + tth * 512:
                             o * 1024 + tth * 512 + 512]
                sdep = (stats_free, free.get("psXr"))
                pln.add_chunk(o, free[("acc", o, tth)], stats_dep=sdep,
                              xa_ap=acc_sl)
            t_done, stats_free = pln.finish(tt)
            free["psXr"] = stats_free[0]
        for o in range(NKC):
            for tth in range(2):
                free[("acci", o, tth)] = pln.xa_free(o)
                free.pop(("acc", o, tth), None)
        if getattr(p, "dbg_ge", None) is not None:
            s.add("sync", (lambda h_, g_: lambda sy: sy.dma_start(
                  out=p.dbg_ge[:, h_ * 1024:(h_ + 1) * 1024], in_=g_[:]))
                  (half, ge16b[half]), deps=t_geb, inc=16)
        if getattr(p, "dbg_acc", None) is not None:
            for o in range(NKC):
                for tth in range(2):
                    t2 = half * 2 + tth
                    s.add("sync", (lambda o_, t2_, a_: lambda sy:
                          sy.dma_start(out=p.dbg_acc[o_ * 128:(o_ + 1) * 128,
                                                     t2_ * 512:(t2_ + 1) * 512],
                                       in_=a_))(o, t2,
                          acc[:, o * 1024 + tth * 512:o * 1024 + tth * 512 + 512]),
                          deps=(free.get(("acci", o, tth)),
                                s.barrier_token("vector")), inc=16)
        free["xbf"] = s.barrier_token("tensor")
        free[("ge16", half)] = s.barrier_token("tensor")
    return t_done


def build_l1_moe(debug=False):
    p = P()
    x3 = p.dram_in("x3", [D, T])
    sw1d = p.dram_in("sw1", [2, NFC, 128, D])
    sw2d = p.dram_in("sw2", [2, 16, 128, NFC * 128])
    rw1d = p.dram_in("rw1", [16, NFC, 128, D])
    rw2d = p.dram_in("rw2", [16, 16, 128, NFC * 128])
    rb2 = p.dram_in("rb2", [16, D])
    centT = p.dram_in("centT", [D, 16])
    BP = p.dram_in("BP", [128, P6_BP_W])
    yT = p.dram_out("yT", [D, T])
    if debug:
        p.dbg_ge = p.dram_out("dbg_ge", [16, T], BF16)
        p.dbg_acc = p.dram_out("dbg_acc", [D, T])
    C = load_consts(p, P6_CONSTS)
    bias_sb = p.sb("biasp", [128, P6_BP_W], F32)
    t_bp = p.s.add("sync", lambda sy: sy.dma_start(out=bias_sb[:], in_=BP),
                   inc=16)
    for e in ENGINES:
        p.s.add(e, lambda eng: eng.nop(), deps=(t_bp,), inc=1)
    emit_l1_moe(p, x3, [sw1d[i] for i in range(2)], [sw2d[i] for i in range(2)],
                [rw1d[i] for i in range(16)], [rw2d[i] for i in range(16)],
                rb2, centT, bias_sb, yT, C)
    return p.finalize()


_NC_CACHE = {}


def _get_nc(name, builder):
    if name not in _NC_CACHE:
        _NC_CACHE[name] = builder()
    return _NC_CACHE[name]


def kernel_unfused(**ins):
    consts = host_consts()
    ncores = 8
    cores = list(range(ncores))
    x = ins["L0_attn_x"]

    def ct(a):
        return np.ascontiguousarray(np.asarray(a, dtype=np.float32))

    xT = [np.ascontiguousarray(x[b].T) for b in range(ncores)]

    nc1 = _get_nc("p1", build_l0_proj)
    w1 = {"wqT": tile_wT(ins["L0_attn_wq"].T, 128),
          "wkT": tile_wT(ins["L0_attn_wk"].T, 128),
          "wvT": tile_wT(ins["L0_attn_wv"].T, 512)}
    cm1 = const_in_maps(consts, P1_CONSTS)
    r1 = run_bass_kernel_spmd(
        nc1, [dict(xT=xT[b], **w1, **cm1) for b in range(ncores)], cores)

    nc2 = _get_nc("p2", build_l0_attn)
    w2 = {"woT": tile_wT(ins["L0_attn_wo"].T, 128)}
    cm2 = const_in_maps(consts, P2_CONSTS)
    r2 = run_bass_kernel_spmd(
        nc2, [dict(Qd=r1.results[b]["Qd"], Kd=r1.results[b]["Kd"],
                   Vd=r1.results[b]["Vd"], CB=r1.results[b]["CB"],
                   xT=xT[b], **w2, **cm2) for b in range(ncores)], cores)

    nc3 = _get_nc("p3", build_l0_ffn)
    bp3 = pack_cols(ins["L0_ffn_b0a"], ins["L0_ffn_b0b"],
                    ins["L0_ffn_b1a"], ins["L0_ffn_b1b"])
    w3 = {"WgT": ct(ins["L0_ffn_Wg"].T),
          "W0aT": tile_wT(ins["L0_ffn_W0a"].T, 128),
          "W0bT": tile_wT(ins["L0_ffn_W0b"].T, 128, nk=32),
          "W1aT": tile_wT(ins["L0_ffn_W1a"].T, 128),
          "W1bT": tile_wT(ins["L0_ffn_W1b"].T, 128, nk=32), "BP": bp3}
    cm3 = const_in_maps(consts, P3_CONSTS)
    r3 = run_bass_kernel_spmd(
        nc3, [dict(x1=r2.results[b]["x1"], **w3, **cm3)
              for b in range(ncores)], cores)

    nc4 = _get_nc("p4", build_l1_proj)
    bp4 = pack_cols(ins["L1_attn_bq"], ins["L1_attn_bk"], ins["L1_attn_bo"])
    w4 = {"WqT": tile_wT(ins["L1_attn_Wq"].T, 128),
          "WkT": tile_wT(ins["L1_attn_Wk"].T, 128),
          "WvT": tile_wT(ins["L1_attn_Wv"].T, 512), "BP": bp4,
          "bv": ct(ins["L1_attn_bv"]).reshape(1, -1)}
    cm4 = const_in_maps(consts, P4_CONSTS)
    r4 = run_bass_kernel_spmd(
        nc4, [dict(x2=r3.results[b]["x2"], **w4, **cm4)
              for b in range(ncores)], cores)

    nc5 = _get_nc("p5", build_l1_attn)
    w5 = {"WoT": tile_wT(ins["L1_attn_Wo"].T, 128), "BP": bp4,
          "cmk": consts["cmaskK"]}
    cm5 = const_in_maps(consts, P5_CONSTS)
    r5 = run_bass_kernel_spmd(
        nc5, [dict(Qd=r4.results[b]["Qd"], Kd=r4.results[b]["Kd"],
                   Vd=r4.results[b]["Vd"], CB=r4.results[b]["CB"],
                   x2=r3.results[b]["x2"], **w5, **cm5)
              for b in range(ncores)], cores)

    nc6 = _get_nc("p6", build_l1_moe)
    bp6 = np.zeros((128, P6_BP_W), dtype=np.float32)
    sb1 = np.asarray(ins["L1_ffn_shared_b1"], dtype=np.float32)
    sb2 = np.asarray(ins["L1_ffn_shared_b2"], dtype=np.float32)
    rb1 = np.asarray(ins["L1_ffn_routed_b1"], dtype=np.float32)
    bp6[:, 0:22] = pack_cols(sb1[0], sb1[1])
    bp6[:, 22:198] = pack_cols(*[rb1[j] for j in range(16)])
    bp6[:, 198:230] = pack_cols(sb2[0], sb2[1])
    bp6[0:16, 230] = np.asarray(ins["L1_ffn_bias"], dtype=np.float32)
    w6 = {"sw1": np.stack([tile_wT(ins["L1_ffn_shared_w1"][n], 128)
                           for n in range(2)]),
          "sw2": np.stack([tile_wT(ins["L1_ffn_shared_w2"][n], 128, nk=NFC)
                           for n in range(2)]),
          "rw1": np.stack([tile_wT(ins["L1_ffn_routed_w1"][j], 128)
                           for j in range(16)]),
          "rw2": np.stack([tile_wT(ins["L1_ffn_routed_w2"][j], 128, nk=NFC)
                           for j in range(16)]),
          "rb2": ct(ins["L1_ffn_routed_b2"]),
          "centT": ct(ins["L1_ffn_centroids"].T), "BP": bp6}
    cm6 = const_in_maps(consts, P6_CONSTS)
    r6 = run_bass_kernel_spmd(
        nc6, [dict(x3=r5.results[b]["x3"], **w6, **cm6)
              for b in range(ncores)], cores)

    out = np.stack([r6.results[b]["yT"].T for b in range(ncores)], axis=0)
    return out.astype(np.float32)


FUSED_CONSTS = ("onesbf", "onesrowf", "onesrow1", "identF", "onehot")


def build_fused():
    p = P()
    xT = p.dram_in("xT", [D, T])
    wqT0 = p.dram_in("wqT0", [16, 128, D]); wkT0 = p.dram_in("wkT0", [16, 128, D])
    wvT0 = p.dram_in("wvT0", [4, 128, NKC * 512])
    woT0 = p.dram_in("woT0", [16, 128, D])
    WgT = p.dram_in("WgT", [D, 2])
    W0aT = p.dram_in("W0aT", [32, 128, D])
    W0bT = p.dram_in("W0bT", [16, 128, 32 * 128])
    W1aT = p.dram_in("W1aT", [32, 128, D])
    W1bT = p.dram_in("W1bT", [16, 128, 32 * 128])
    BP0 = p.dram_in("BP0", [128, L0_BP_W])
    wqT1 = p.dram_in("wqT1", [16, 128, D]); wkT1 = p.dram_in("wkT1", [16, 128, D])
    wvT1 = p.dram_in("wvT1", [4, 128, NKC * 512])
    woT1 = p.dram_in("woT1", [16, 128, D])
    BP1 = p.dram_in("BP1", [128, L1_BP_W])
    bvd = p.dram_in("bv", [1, D])
    sw1d = p.dram_in("sw1", [2, NFC, 128, D])
    sw2d = p.dram_in("sw2", [2, 16, 128, NFC * 128])
    rw1d = p.dram_in("rw1", [16, NFC, 128, D])
    rw2d = p.dram_in("rw2", [16, 16, 128, NFC * 128])
    rb2 = p.dram_in("rb2", [16, D])
    centT = p.dram_in("centT", [D, 16])
    BP6 = p.dram_in("BP6", [128, P6_BP_W])
    ropeCd = p.dram_in("c_ropeC", [64, T])
    ropeSd = p.dram_in("c_ropeS", [64, T])
    cmk = p.dram_in("cmk", [4, 128, 512])
    yT = p.dram_out("yT", [D, T])

    Qd = p.dram_tmp("Qs", [D, T], BF16)
    Kd = p.dram_tmp("Ks", [D, T], BF16)
    Vd = p.dram_tmp("Vs", [NH, T, HD], BF16)
    x1 = p.dram_tmp("x1s", [D, T])
    x2 = p.dram_tmp("x2s", [D, T])
    x3 = p.dram_tmp("x3s", [D, T])

    C = load_consts(p, FUSED_CONSTS)
    cb0 = p.sb("cb0", [128, NH], F32)
    cb1 = p.sb("cb1", [128, NH], F32)
    bias0 = p.sb("bias0", [128, L0_BP_W], F32)
    bias1 = p.sb("bias1", [128, L1_BP_W], F32)
    bias6 = p.sb("bias6", [128, P6_BP_W], F32)
    bvrow = p.sb("bvrow", [1, D], BF16)
    tb = [p.s.add("sync", (lambda d_, s_: lambda sy:
                  sy.dma_start(out=s_[:], in_=d_))(d, sbh), inc=16)
          for d, sbh in ((BP0, bias0), (BP1, bias1), (BP6, bias6))]
    tb.append(p.s.add("gpsimd", lambda g: g.dma_start(out=bvrow[0:1, :],
                                                      in_=bvd), inc=16))
    for e in ENGINES:
        p.s.add(e, lambda eng: eng.nop(), deps=tb, inc=1)

    p.push_scope()
    rC = p.sb("rC", [64, T], F32)
    rS = p.sb("rS", [64, T], F32)
    t1 = p.s.add("sync", lambda sy: sy.dma_start(out=rC[:], in_=ropeCd), inc=16)
    t2 = p.s.add("sync", lambda sy: sy.dma_start(out=rS[:], in_=ropeSd), inc=16)
    for e in ("vector",):
        p.s.add(e, lambda eng: eng.nop(), deps=(t1, t2), inc=1)
    emit_proj_phase(p, xT, [("q", wqT0, Qd), ("k", wkT0, Kd)], [(wvT0, Vd)],
                    C, ropeC_sb=rC, ropeS_sb=rS, cbias_out=cb0,
                    scale_for_c=SCALE0)
    p.phase_barrier()
    p.pop_scope()

    p.push_scope()
    cbt = p.s.barrier_token("scalar")
    emit_attn_phase(p, Qd, Kd, Vd, woT0, xT, x1, causal=False, scale=SCALE0,
                    C=C, cbias=cb0, cb_tok=cbt)
    p.phase_barrier()
    p.pop_scope()

    p.push_scope()
    emit_l0_ffn(p, x1, WgT, W0aT, W0bT, W1aT, W1bT, bias0, x2, C)
    p.phase_barrier()
    p.pop_scope()

    p.push_scope()
    emit_proj_phase(p, x2, [("q", wqT1, Qd), ("k", wkT1, Kd)], [(wvT1, Vd)],
                    C, bias_sb=bias1,
                    qk_bias_cols={"q": L1_BC["q"], "k": L1_BC["k"]},
                    v_bias_row=bvrow, cbias_out=cb1, scale_for_c=SCALE1)
    p.phase_barrier()
    p.pop_scope()

    p.push_scope()
    cbt1 = p.s.barrier_token("scalar")
    emit_attn_phase(p, Qd, Kd, Vd, woT1, x2, x3, causal=True, scale=SCALE1,
                    C=C, cbias=cb1, cb_tok=cbt1, bo_col=L1_BC["bo"],
                    bias_sb=bias1, cmask_dram=[cmk[r] for r in range(4)])
    p.phase_barrier()
    p.pop_scope()

    p.push_scope()
    emit_l1_moe(p, x3, [sw1d[i] for i in range(2)], [sw2d[i] for i in range(2)],
                [rw1d[i] for i in range(16)], [rw2d[i] for i in range(16)],
                rb2, centT, bias6, yT, C)
    p.pop_scope()
    return p.finalize()


def kernel_fused(**ins):
    consts = host_consts()
    ncores = 8
    cores = list(range(ncores))
    x = ins["L0_attn_x"]

    def ct(a):
        return np.ascontiguousarray(np.asarray(a, dtype=np.float32))

    nc = _get_nc("fused", build_fused)
    bp3 = pack_cols(ins["L0_ffn_b0a"], ins["L0_ffn_b0b"],
                    ins["L0_ffn_b1a"], ins["L0_ffn_b1b"])
    bp4 = pack_cols(ins["L1_attn_bq"], ins["L1_attn_bk"], ins["L1_attn_bo"])
    bp6 = np.zeros((128, P6_BP_W), dtype=np.float32)
    sb1 = np.asarray(ins["L1_ffn_shared_b1"], dtype=np.float32)
    sb2 = np.asarray(ins["L1_ffn_shared_b2"], dtype=np.float32)
    rb1 = np.asarray(ins["L1_ffn_routed_b1"], dtype=np.float32)
    bp6[:, 0:22] = pack_cols(sb1[0], sb1[1])
    bp6[:, 22:198] = pack_cols(*[rb1[j] for j in range(16)])
    bp6[:, 198:230] = pack_cols(sb2[0], sb2[1])
    bp6[0:16, 230] = np.asarray(ins["L1_ffn_bias"], dtype=np.float32)
    shared = {
        "wqT0": tile_wT(ins["L0_attn_wq"].T, 128),
        "wkT0": tile_wT(ins["L0_attn_wk"].T, 128),
        "wvT0": tile_wT(ins["L0_attn_wv"].T, 512),
        "woT0": tile_wT(ins["L0_attn_wo"].T, 128),
        "WgT": ct(ins["L0_ffn_Wg"].T),
        "W0aT": tile_wT(ins["L0_ffn_W0a"].T, 128),
        "W0bT": tile_wT(ins["L0_ffn_W0b"].T, 128, nk=32),
        "W1aT": tile_wT(ins["L0_ffn_W1a"].T, 128),
        "W1bT": tile_wT(ins["L0_ffn_W1b"].T, 128, nk=32),
        "BP0": bp3,
        "wqT1": tile_wT(ins["L1_attn_Wq"].T, 128),
        "wkT1": tile_wT(ins["L1_attn_Wk"].T, 128),
        "wvT1": tile_wT(ins["L1_attn_Wv"].T, 512),
        "woT1": tile_wT(ins["L1_attn_Wo"].T, 128),
        "BP1": bp4, "bv": ct(ins["L1_attn_bv"]).reshape(1, -1),
        "sw1": np.stack([tile_wT(ins["L1_ffn_shared_w1"][n], 128)
                         for n in range(2)]),
        "sw2": np.stack([tile_wT(ins["L1_ffn_shared_w2"][n], 128, nk=NFC)
                         for n in range(2)]),
        "rw1": np.stack([tile_wT(ins["L1_ffn_routed_w1"][j], 128)
                         for j in range(16)]),
        "rw2": np.stack([tile_wT(ins["L1_ffn_routed_w2"][j], 128, nk=NFC)
                         for j in range(16)]),
        "rb2": ct(ins["L1_ffn_routed_b2"]),
        "centT": ct(ins["L1_ffn_centroids"].T), "BP6": bp6,
        "c_ropeC": consts["ropeC"], "c_ropeS": consts["ropeS"],
        "cmk": consts["cmaskK"],
    }
    shared.update(const_in_maps(consts, FUSED_CONSTS))
    in_maps = [dict(xT=np.ascontiguousarray(x[b].T), **shared)
               for b in range(ncores)]
    res = run_bass_kernel_spmd(nc, in_maps, cores)
    out = np.stack([res.results[b]["yT"].T for b in range(ncores)], axis=0)
    return out.astype(np.float32)


kernel = kernel_fused
